# version 16
# baseline (speedup 1.0000x reference)
import numpy as np
import concourse.bass as bass
import concourse.mybir as mybir
from concourse.alu_op_type import AluOpType as OP
from concourse.bass_utils import run_bass_kernel_spmd

F32 = mybir.dt.float32
BF16 = mybir.dt.bfloat16
F16 = mybir.dt.float16
F8 = mybir.dt.float8e4
AF = mybir.ActivationFunctionType
AX = mybir.AxisListType
f32 = np.float32

QUANT = "cast"
DEBUG = False
_dbg_names = []

S_LEN = 2048
D = 1024
HID = 4096
NSEQ = 2
ROWS = NSEQ * S_LEN
INV_SQRT_HS = f32(1.0 / float(np.sqrt(np.float64(512.0))))

LN2_ = np.float64(0.6931471805599453)
INVLN2 = f32(1.0 / LN2_)
LN2HI = f32(LN2_)
LN2LO = f32(LN2_ - np.float64(f32(LN2_)))
B3HI = f32(3.0 * LN2_)
B3LO = f32(3.0 * LN2_ - np.float64(f32(3.0 * LN2_)))
NB3LO = f32(-np.float64(f32(3.0 * LN2_ - np.float64(f32(3.0 * LN2_)))))
NB10 = f32(-(10.0 * LN2_))

ENGS = ("sync", "vector", "scalar", "gpsimd", "tensor")


class Prog:
    def __init__(self, nc):
        self.nc = nc
        self.stream = {e: [] for e in ENGS}
        self.sems = {}
        self.cnt = {}
        for name in ("T", "V", "C", "G", "DS", "DG", "DB", "DW0", "DW1",
                     "DX0", "DX1", "DX2", "DX3", "DT0", "DT1"):
            self.sems[name] = nc.alloc_semaphore("sem_" + name)
            self.cnt[name] = 0

    def w(self, eng, sem, tok):
        if tok is None or tok <= 0:
            return
        s = self.sems[sem]
        self.stream[eng].append(lambda e, s=s, t=tok: e.wait_ge(s, t))

    def emit(self, eng, fn, inc=None):
        if inc is None:
            self.stream[eng].append(fn)
            return None
        s = self.sems[inc]
        step = 16 if inc.startswith("D") else 1
        self.cnt[inc] += step
        tok = self.cnt[inc]
        self.stream[eng].append(lambda e, fn=fn, s=s, st=step: fn(e).then_inc(s, st))
        return tok


class Arena:
    def __init__(self, t, nels_bf16):
        self.t = t
        self.nb = nels_bf16 * 2
        self.off = 0
        self.marks = []
        self.peak = 0

    def alloc(self, els, dtype):
        unit = {F32: 4, BF16: 2, F16: 2}[dtype]
        off = (self.off + 3) & ~3
        sz = els * unit
        assert off + sz <= self.nb, f"arena overflow: {off}+{sz} > {self.nb}"
        self.off = off + sz
        self.peak = max(self.peak, self.off)
        v = self.t[:, off // 2:(off + sz) // 2]
        if dtype != BF16:
            v = v.bitcast(dtype)
        return v

    def mark(self):
        self.marks.append(self.off)

    def release(self):
        self.off = self.marks.pop()


def rope_table_T():
    pos = np.arange(S_LEN, dtype=np.float64)
    inv = 10000.0 ** (np.arange(0, 512, 2, dtype=np.float64) / 512.0)
    radians = pos[:, None] / inv[None, :]
    sincos = np.stack((np.sin(radians), np.cos(radians)), axis=-1).reshape(S_LEN, -1)
    t = np.repeat(sincos, 2, axis=-1) * (512.0 ** -0.25)
    return np.ascontiguousarray(t.T.astype(np.float32))


def build_core_program():
    nc = bass.Bass("TRN2", target_bir_lowering=False)
    xh = nc.dram_tensor("x", [ROWS, D], F32, kind="ExternalInput").ap()
    wqh = nc.dram_tensor("wq", [D, D], F32, kind="ExternalInput").ap()
    wkh = nc.dram_tensor("wk", [D, D], F32, kind="ExternalInput").ap()
    wvh = nc.dram_tensor("wv", [D, D], F32, kind="ExternalInput").ap()
    bqh = nc.dram_tensor("bq", [1, D], F32, kind="ExternalInput").ap()
    bkh = nc.dram_tensor("bk", [1, D], F32, kind="ExternalInput").ap()
    bvh = nc.dram_tensor("bv", [1, D], F32, kind="ExternalInput").ap()
    w1h = nc.dram_tensor("w1", [HID, D], F32, kind="ExternalInput").ap()
    b1h = nc.dram_tensor("b1", [1, HID], F32, kind="ExternalInput").ap()
    w2h = nc.dram_tensor("w2", [D, HID], F32, kind="ExternalInput").ap()
    b2h = nc.dram_tensor("b2", [1, D], F32, kind="ExternalInput").ap()
    tth = nc.dram_tensor("tt", [D, S_LEN], F32, kind="ExternalInput").ap()
    outh = nc.dram_tensor("out", [ROWS, D], F32, kind="ExternalOutput").ap()
    x1h = nc.dram_tensor("x1s", [ROWS, D], F32, kind="Internal").ap()
    st_wq = nc.dram_tensor("st_wq", [D, D], BF16, kind="Internal").ap()
    st_wk = nc.dram_tensor("st_wk", [D, D], BF16, kind="Internal").ap()
    st_wv = nc.dram_tensor("st_wv", [D, D], BF16, kind="Internal").ap()
    st_w1 = nc.dram_tensor("st_w1", [D, HID], BF16, kind="Internal").ap()
    st_w2 = nc.dram_tensor("st_w2", [HID, D], BF16, kind="Internal").ap()
    st_bp = nc.dram_tensor("st_bp", [2, D], F32, kind="Internal").ap()
    dbg = {}
    def dbg_out(name, shape, dtype=F32):
        global _dbg_names
        if name not in _dbg_names:
            _dbg_names.append(name)
        dbg[name] = nc.dram_tensor("dbg_" + name, shape, dtype, kind="ExternalOutput").ap()
        return dbg[name]
    def dbg_tap(name, shape, src_ap, sem, tok, dtype=F32):
        if not DEBUG:
            return
        d = dbg_out(name, shape, dtype)
        p.w("gpsimd", sem, tok)
        tk = p.emit("gpsimd", lambda e, d=d, src_ap=src_ap: e.dma_start(d, src_ap), inc="DG")
        for eng in ("sync", "vector", "scalar", "tensor"):
            p.w(eng, "DG", tk)

    ident = nc.alloc_sbuf_tensor("ident", [128, 128], F32)
    bqr = nc.alloc_sbuf_tensor("bqr", [128, 8], F32)
    bkr = nc.alloc_sbuf_tensor("bkr", [128, 8], F32)
    b1L = nc.alloc_sbuf_tensor("b1L", [128, 32], F32)
    b2qb = nc.alloc_sbuf_tensor("b2qb", [128, D], F32)
    bvb = nc.alloc_sbuf_tensor("bvb", [128, D], F32)
    stats = nc.alloc_sbuf_tensor("stats", [128, 32], F32)
    bscr = nc.alloc_sbuf_tensor("bscr", [128, 2 * D], F32)

    NELS = 94208
    AB = nc.alloc_sbuf_tensor("AB", [128, NELS], BF16)

    psA0 = nc.alloc_psum_tensor("psA0", [128, 512], F32)
    psA1 = nc.alloc_psum_tensor("psA1", [128, 512], F32)
    psB = nc.alloc_psum_tensor("psB", [128, 2048], F32)
    psPV = nc.alloc_psum_tensor("psPV", [128, 1024], F32)

    p = Prog(nc)
    A = Arena(AB, NELS)
    st = {}

    def emit_ladder(sc, src_ap, out_ap, W, npart, man, e_lo, e_hi, e_levels,
                    r_levels, maxval, signed, dep):
        ax, sg, l, v, dm, e, u, v2, r, pp, t1 = (
            sc["ax"][:npart, :W], sc["sg"][:npart, :W], sc["l"][:npart, :W],
            sc["v"][:npart, :W], sc["dm"][:npart, :W], sc["e"][:npart, :W],
            sc["u"][:npart, :W], sc["v2"][:npart, :W], sc["r"][:npart, :W],
            sc["pp"][:npart, :W], sc["t1"][:npart, :W])
        a1, a = sc["a1"][:npart, :W], sc["a"][:npart, :W]
        pv = sc["pv"][:npart, :W]
        if dep is not None:
            p.w("vector", dep[0], dep[1])
        if signed:
            p.emit("vector", lambda ee: ee.scalar_tensor_tensor(
                ax, src_ap, float(-1.0), src_ap, OP.mult, OP.max), inc="V")
            p.emit("vector", lambda ee: ee.tensor_single_scalar(
                sg, src_ap, float(0.0), OP.is_lt), inc="V")
            p.emit("vector", lambda ee: ee.tensor_scalar(
                sg, sg, float(-2.0), float(1.0), OP.mult, OP.add), inc="V")
        else:
            p.emit("vector", lambda ee: ee.tensor_copy(ax, src_ap), inc="V")
        tv = p.emit("vector", lambda ee: ee.tensor_single_scalar(
            l, ax, float(1e-7), OP.max), inc="V")
        p.w("scalar", "V", tv)
        tc = p.emit("scalar", lambda ee: ee.activation(l, l, AF.Ln), inc="C")
        p.w("vector", "C", tc)
        p.emit("vector", lambda ee: ee.tensor_scalar(
            e, l, float(INVLN2), float(e_lo), OP.mult, OP.max), inc="V")
        p.emit("vector", lambda ee: ee.tensor_single_scalar(
            e, e, float(e_hi), OP.min), inc="V")
        p.emit("vector", lambda ee: ee.tensor_single_scalar(
            t1, e, float(-e_lo), OP.add), inc="V")
        first_e = True
        for stp in e_levels:
            sv_ = t1 if first_e else v
            p.emit("vector", lambda ee, s=stp, sv_=sv_: ee.tensor_scalar(
                dm, sv_, float(s), float(-s), OP.is_ge, OP.mult), inc="V")
            p.emit("vector", lambda ee, sv_=sv_: ee.tensor_add(v, sv_, dm), inc="V")
            first_e = False
        p.emit("vector", lambda ee: ee.tensor_sub(e, t1, v), inc="V")
        p.emit("vector", lambda ee: ee.tensor_single_scalar(
            e, e, float(e_lo), OP.add), inc="V")
        if man == 3:
            p.emit("vector", lambda ee: ee.tensor_scalar(
                a1, e, float(-LN2HI), float(B3HI), OP.mult, OP.add), inc="V")
            tv = p.emit("vector", lambda ee: ee.scalar_tensor_tensor(
                a, e, float(-LN2LO), a1, OP.mult, OP.add), inc="V")
            p.w("scalar", "V", tv)
            p.emit("scalar", lambda ee: ee.activation(
                pv, a, AF.Exp, bias=B3LO), inc="C")
            tc = p.emit("scalar", lambda ee: ee.activation(
                pp, a, AF.Exp, bias=NB3LO, scale=f32(-1.0)), inc="C")
            p.w("vector", "C", tc)
            p.emit("vector", lambda ee: ee.tensor_mul(u, ax, pv), inc="V")
        else:
            p.emit("vector", lambda ee: ee.tensor_single_scalar(
                a1, e, float(-LN2HI), OP.mult), inc="V")
            tv = p.emit("vector", lambda ee: ee.scalar_tensor_tensor(
                a, e, float(-LN2LO), a1, OP.mult, OP.add), inc="V")
            p.w("scalar", "V", tv)
            p.emit("scalar", lambda ee: ee.activation(pv, a, AF.Exp), inc="C")
            tc = p.emit("scalar", lambda ee: ee.activation(
                pp, a, AF.Exp, bias=NB10, scale=f32(-1.0)), inc="C")
            p.w("vector", "C", tc)
            p.emit("vector", lambda ee: ee.tensor_single_scalar(
                dm, ax, float(1024.0), OP.mult), inc="V")
            p.emit("vector", lambda ee: ee.tensor_mul(u, dm, pv), inc="V")
        first = True
        for stp in r_levels:
            sv = u if first else v2
            p.emit("vector", lambda ee, s=stp, sv=sv: ee.tensor_scalar(
                dm, sv, float(s - 0.5), float(-s), OP.is_ge, OP.mult), inc="V")
            p.emit("vector", lambda ee, sv=sv: ee.tensor_add(v2, sv, dm), inc="V")
            first = False
        p.emit("vector", lambda ee: ee.tensor_sub(r, u, v2), inc="V")
        p.emit("vector", lambda ee: ee.tensor_mul(r, r, pp), inc="V")
        p.emit("vector", lambda ee: ee.tensor_scalar(
            t1, r, float(-maxval), float(0.0), OP.add, OP.max), inc="V")
        p.emit("vector", lambda ee: ee.tensor_scalar(
            t1, t1, float(60000.0), float(60000.0), OP.mult, OP.mult), inc="V")
        if signed:
            p.emit("vector", lambda ee: ee.tensor_add(r, r, t1), inc="V")
            return p.emit("vector", lambda ee: ee.tensor_mul(out_ap, r, sg), inc="V")
        return p.emit("vector", lambda ee: ee.tensor_add(out_ap, r, t1), inc="V")

    tr = {"ping": 0, "lastT": 0}

    def emit_transpose_f32(src_view, nblk, dst_fn, extra_t_waits=(), stage=None):
        pair = stage if stage is not None else (psA0, psA1)
        last_v = None
        for g0 in range(0, nblk, 4):
            gn = min(4, nblk - g0)
            ps = pair[tr["ping"]]
            tr["ping"] = 1 - tr["ping"]
            p.w("tensor", "V", p.cnt["V"])
            p.w("tensor", "C", p.cnt["C"])
            for s_, tk in extra_t_waits:
                p.w("tensor", s_, tk)
            p.w("tensor", "G", st["ident"])
            tt = None
            for b in range(gn):
                blk = g0 + b
                tt = p.emit("tensor", lambda e, blk=blk, b=b, ps=ps: e.transpose(
                    ps[:, b * 128:(b + 1) * 128],
                    src_view[:, blk * 128:(blk + 1) * 128], ident[:]),
                    inc="T" if b == gn - 1 else None)
            tr["lastT"] = tt
            p.w("vector", "T", tt)
            for b in range(gn):
                dst = dst_fn(g0 + b)
                last_v = p.emit("vector", lambda e, dst=dst, b=b, ps=ps: e.tensor_copy(
                    dst, ps[:, b * 128:(b + 1) * 128]), inc="V")
        return last_v

    p.emit("gpsimd", lambda e: e.memset(ident[:], 0.0))
    st["ident"] = p.emit("gpsimd", lambda e: e.affine_select(
        out=ident[:], in_=ident[:], compare_op=OP.not_equal, fill=1.0,
        base=0, pattern=[[-1, 128]], channel_multiplier=1), inc="G")

    A.mark()
    wchunk = A.alloc(2 * D, F32)
    wt_bf = A.alloc(8 * D, BF16)
    wr_bf = A.alloc(8 * D, BF16)
    prev_stash = 0

    def prep_attn_weight(wsrc, dst_stash, fold):
        nonlocal prev_stash
        p.w("vector", "DB", prev_stash)
        last_v = None
        tr_tok = [0, 0]
        for c in range(8):
            off = (c % 2) * D
            wv_ = wchunk[:, off:off + D]
            p.w("sync", "T", tr_tok[c % 2])
            tok_ld = p.emit("sync", lambda e, c=c, wv_=wv_: e.dma_start(
                wv_, wsrc[c * 128:(c + 1) * 128, :]), inc="DT%d" % (c % 2))
            last_v = emit_transpose_f32(
                wv_, 8, lambda b, c=c: wt_bf[:, b * D + c * 128: b * D + c * 128 + 128],
                extra_t_waits=(("DT%d" % (c % 2), tok_ld), ("DB", prev_stash)))
            tr_tok[c % 2] = tr["lastT"]
        if fold:
            p.w("vector", "V", last_v)
            for c in range(8):
                wtc = wt_bf[:, c * D:(c + 1) * D].rearrange("p (n two) -> p n two", two=2)
                wrc = wr_bf[:, c * D:(c + 1) * D].rearrange("p (n two) -> p n two", two=2)
                p.emit("vector", lambda e, wtc=wtc, wrc=wrc: e.tensor_sub(
                    wrc[:, :, 0:1], wtc[:, :, 0:1], wtc[:, :, 1:2]), inc="V")
                last_v = p.emit("vector", lambda e, wtc=wtc, wrc=wrc: e.tensor_add(
                    wrc[:, :, 1:2], wtc[:, :, 1:2], wtc[:, :, 0:1]), inc="V")
            src = wr_bf
        else:
            src = wt_bf
        p.w("sync", "V", last_v)
        tok = None
        for c in range(8):
            tok = p.emit("sync", lambda e, c=c, src=src: e.dma_start(
                dst_stash[c * 128:(c + 1) * 128, :], src[:, c * D:(c + 1) * D]), inc="DB")
        prev_stash = tok
        return tok

    prep_attn_weight(wqh, st_wq, True)
    prep_attn_weight(wkh, st_wk, True)
    st["attn_w_stash"] = prep_attn_weight(wvh, st_wv, False)
    A.release()

    A.mark()
    w1c = A.alloc(2 * D, F32)
    w1t_full = A.alloc(8 * HID, BF16)
    tr_tok1 = [0, 0]
    for c in range(32):
        off = (c % 2) * D
        wv_ = w1c[:, off:off + D]
        p.w("sync", "T", tr_tok1[c % 2])
        tok_ld = p.emit("sync", lambda e, c=c, wv_=wv_: e.dma_start(
            wv_, w1h[c * 128:(c + 1) * 128, :]), inc="DT%d" % (c % 2))
        tv = emit_transpose_f32(
            wv_, 8, lambda b, c=c: w1t_full[:, b * HID + c * 128: b * HID + c * 128 + 128],
            extra_t_waits=(("DT%d" % (c % 2), tok_ld),))
        tr_tok1[c % 2] = tr["lastT"]
    p.w("sync", "V", p.cnt["V"])
    p.w("sync", "DB", prev_stash)
    for r_ in range(8):
        prev_stash = p.emit("sync", lambda e, r_=r_: e.dma_start(
            st_w1[r_ * 128:(r_ + 1) * 128, :], w1t_full[:, r_ * HID:(r_ + 1) * HID]), inc="DB")
    A.release()

    q_tok2 = [("V", 0), ("V", 0)]
    A.mark()
    w2c = A.alloc(2 * HID, F32)
    w2q_f32 = A.alloc(HID, F32)
    w2t_full = A.alloc(32 * D, BF16)
    if QUANT == "ladder":
        scq = {k: A.alloc(HID // 2, F16) for k in
               ("ax", "sg", "l", "v", "dm", "e", "u", "v2", "r", "t1")}
        scq["a1"] = A.alloc(HID // 2, F32)
        scq["a"] = A.alloc(HID // 2, F32)
        scq["pp"] = A.alloc(HID // 2, F32)
        scq["pv"] = A.alloc(HID // 2, F32)
    else:
        w2q_f8 = A.alloc(HID // 2, F16).bitcast(F8)
    for c in range(8):
        off = (c % 2) * HID
        wv_ = w2c[:, off:off + HID]
        p.w("sync", q_tok2[c % 2][0], q_tok2[c % 2][1])
        tok_ld = p.emit("sync", lambda e, c=c, wv_=wv_: e.dma_start(
            wv_, w2h[c * 128:(c + 1) * 128, :]), inc="DT%d" % (c % 2))
        p.w("vector", "T", tr["lastT"])
        if QUANT == "ladder":
            for half in range(2):
                hw_ = HID // 2
                tq = emit_ladder(scq, wv_[:, half * hw_:(half + 1) * hw_],
                                 w2q_f32[:, half * hw_:(half + 1) * hw_], hw_,
                                 128, 3, -6.0, 3.99,
                                 (8, 4, 2, 1), (16, 8, 4, 2, 1), 240.0, True,
                                 ("DT%d" % (c % 2), tok_ld))
            dep = ("V", tq)
        else:
            p.w("scalar", "DT%d" % (c % 2), tok_ld)
            p.w("scalar", "T", tr["lastT"])
            p.emit("scalar", lambda e, wv_=wv_: e.copy(w2q_f8[:, :HID], wv_), inc="C")
            tq = p.emit("scalar", lambda e: e.copy(w2q_f32[:, :], w2q_f8[:, :HID]), inc="C")
            dep = ("C", tq)
        q_tok2[c % 2] = dep
        tv = emit_transpose_f32(
            w2q_f32[:, :], 32, lambda b, c=c: w2t_full[:, b * D + c * 128: b * D + c * 128 + 128],
            extra_t_waits=(dep,))
    p.w("sync", "V", p.cnt["V"])
    p.w("sync", "DB", prev_stash)
    for r_ in range(32):
        prev_stash = p.emit("sync", lambda e, r_=r_: e.dma_start(
            st_w2[r_ * 128:(r_ + 1) * 128, :], w2t_full[:, r_ * D:(r_ + 1) * D]), inc="DB")
    A.release()
    st["w_stash_done"] = prev_stash

    tok_bv = p.emit("sync", lambda e: e.dma_start(bscr[1:2, 0:D], bvh), inc="DS")

    def rope_fold_row(dep):
        p.w("vector", "DS", dep)
        sr = bscr[0:1, 0:D].rearrange("p (n two) -> p n two", two=2)
        dr = bscr[0:1, D:2 * D].rearrange("p (n two) -> p n two", two=2)
        p.emit("vector", lambda e: e.tensor_sub(
            dr[:, :, 0:1], sr[:, :, 0:1], sr[:, :, 1:2]), inc="V")
        return p.emit("vector", lambda e: e.tensor_add(
            dr[:, :, 1:2], sr[:, :, 1:2], sr[:, :, 0:1]), inc="V")

    tok_bq = p.emit("sync", lambda e: e.dma_start(bscr[0:1, 0:D], bqh), inc="DS")
    tqf = rope_fold_row(tok_bq)
    p.w("sync", "V", tqf)
    tbp0 = p.emit("sync", lambda e: e.dma_start(st_bp[0:1, :], bscr[0:1, D:2 * D]), inc="DS")
    p.w("sync", "DS", tbp0)
    tok_bk = p.emit("sync", lambda e: e.dma_start(bscr[0:1, 0:D], bkh), inc="DS")
    tqf = rope_fold_row(tok_bk)
    p.w("sync", "V", tqf)
    tbp1 = p.emit("sync", lambda e: e.dma_start(st_bp[1:2, :], bscr[0:1, D:2 * D]), inc="DS")
    p.w("sync", "DS", tbp1)
    p.emit("sync", lambda e: e.dma_start(
        bqr[:], st_bp[0:1, :].rearrange("one (c p) -> (one p) c", p=128),
        allow_slow_non_contiguous=True), inc="DS")
    st["bqr"] = p.emit("sync", lambda e: e.dma_start(
        bkr[:], st_bp[1:2, :].rearrange("one (c p) -> (one p) c", p=128),
        allow_slow_non_contiguous=True), inc="DS")
    st["b1L"] = p.emit("sync", lambda e: e.dma_start(
        b1L[:], b1h[0:1, :].rearrange("one (c p) -> (one p) c", p=128),
        allow_slow_non_contiguous=True), inc="DS")
    p.w("sync", "DS", tbp1)
    tok_b2 = p.emit("sync", lambda e: e.dma_start(bscr[0:1, 0:D], b2h), inc="DS")
    if QUANT == "ladder":
        A.mark()
        scb = {k: A.alloc(D, F16) for k in
               ("ax", "sg", "l", "v", "dm", "e", "u", "v2", "r", "t1")}
        scb["a1"] = A.alloc(D, F32)
        scb["a"] = A.alloc(D, F32)
        scb["pp"] = A.alloc(D, F32)
        scb["pv"] = A.alloc(D, F32)
        tok_b2q = emit_ladder(scb, bscr[0:1, 0:D], bscr[0:1, D:2 * D], D, 1, 10,
                              -14.0, 3.99, (16, 8, 4, 2, 1),
                              (2048, 1024, 512, 256, 128, 64, 32, 16, 8, 4, 2, 1),
                              65504.0, True, ("DS", tok_b2))
        A.release()
        sem_b2 = ("V", tok_b2q)
    else:
        p.w("scalar", "DS", tok_b2)
        b2f16 = b2qb[0:1, 0:512].bitcast(F16)
        p.emit("scalar", lambda e: e.copy(b2f16, bscr[0:1, 0:D]), inc="C")
        tok_b2q = p.emit("scalar", lambda e: e.copy(bscr[0:1, D:2 * D], b2f16), inc="C")
        sem_b2 = ("C", tok_b2q)

    def bcast_rows(dst, src_row, dep_sem, dep_tok):
        p.w("gpsimd", dep_sem, dep_tok)
        tok = p.emit("gpsimd", lambda e: e.dma_start(dst[0:1, :], src_row), inc="DG")
        n = 1
        while n < 128:
            p.w("gpsimd", "DG", tok)
            m = min(n, 128 - n)
            tok = p.emit("gpsimd", lambda e, n=n, m=m: e.dma_start(
                dst[n:n + m, :], dst[0:m, :]), inc="DG")
            n += m
        return tok

    st["b2qb"] = bcast_rows(b2qb, bscr[0:1, D:2 * D], sem_b2[0], sem_b2[1])
    st["bvb"] = bcast_rows(bvb, bscr[1:2, 0:D], "DS", tok_b2)

    def phase_barrier():
        for nm in ("T", "V", "C", "DG", "DB", "DW0", "DW1",
                   "DX0", "DX1", "DX2", "DX3", "DT0", "DT1"):
            p.w("sync", nm, p.cnt[nm])

    guard = {}
    last_out_store = 0
    x1_store = 0

    for seq in range(NSEQ):
        base = seq * S_LEN
        A.mark()
        wq_s = A.alloc(8 * D, BF16)
        wk_s = A.alloc(8 * D, BF16)
        q_fm = A.alloc(8 * S_LEN, BF16)
        k_fm = A.alloc(8 * S_LEN, BF16)
        v_tm = A.alloc(16 * D, BF16)
        xt = A.alloc(2 * D, F32)
        lnt = A.alloc(2 * D, F32)
        tpc = A.alloc(2 * 512, F32)
        etmp = A.alloc(2 * 512, F32)
        A.mark()
        ln1 = A.alloc(8 * S_LEN, BF16)

        tok_wq_ld = [None] * 8
        tok_wk_ld = [None] * 8
        phase_barrier()
        p.w("sync", "DB", st["w_stash_done"])
        for c in range(8):
            tok_wq_ld[c] = p.emit("sync", lambda e, c=c: e.dma_start(
                wq_s[:, c * D:(c + 1) * D], st_wq[c * 128:(c + 1) * 128, :]), inc="DB")
        for c in range(8):
            tok_wk_ld[c] = p.emit("sync", lambda e, c=c: e.dma_start(
                wk_s[:, c * D:(c + 1) * D], st_wk[c * 128:(c + 1) * 128, :]), inc="DB")

        if seq == 0:
            dbg_tap("wq_s", [128, 8 * D], wq_s[:, :], "DB", tok_wq_ld[7], BF16)
        xt_busy = [0, 0]
        ln_ready = None
        ln_evac = []
        tp_busy = [0, 0]
        etmp_busy = [0, 0]
        qk_last_v = 0
        trstage = (psB[:, 1024:1536], psB[:, 1536:2048])

        def a2_strip(sc_):
            nonlocal qk_last_v
            for c in range(8):
                gi = sc_ * 8 + c
                tpp = gi % 2
                tview = tpc[:, tpp * 512:(tpp + 1) * 512]
                p.w("sync", "V", tp_busy[tpp])
                tok_t = p.emit("sync", lambda e, c=c, sc_=sc_, tview=tview: e.dma_start(
                    tview, tth[c * 128:(c + 1) * 128, sc_ * 512:(sc_ + 1) * 512]), inc="DT%d" % tpp)
                for ping, (wv_, bias_t, dst) in enumerate(
                        ((wq_s, bqr, q_fm), (wk_s, bkr, k_fm))):
                    dep2 = gi % 2
                    if ping == 0:
                        ps_sl = (psA0 if dep2 == 0 else psA1)[:, 0:512]
                    else:
                        ps_sl = psB[:, dep2 * 512:(dep2 + 1) * 512]
                    ev = etmp[:, ping * 512:(ping + 1) * 512]
                    p.w("tensor", "V", ln_evac[sc_ * 4 + 3])
                    p.w("tensor", "DB", tok_wk_ld[7])
                    p.w("tensor", "C", guard.get(("qk", ping, dep2), 0))
                    tmm = None
                    for kk in range(8):
                        tmm = p.emit("tensor", lambda e, wv_=wv_, c=c, sc_=sc_, kk=kk, ps_sl=ps_sl: e.matmul(
                            ps_sl,
                            wv_[:, kk * D + c * 128: kk * D + c * 128 + 128],
                            ln1[:, kk * S_LEN + sc_ * 512: kk * S_LEN + (sc_ + 1) * 512],
                            start=(kk == 0), stop=(kk == 7)),
                            inc="T" if kk == 7 else None)
                    p.w("scalar", "T", tmm)
                    p.w("scalar", "V", etmp_busy[ping])
                    p.w("scalar", "DS", st["bqr"])
                    tev = p.emit("scalar", lambda e, ps_sl=ps_sl, ev=ev, bias_t=bias_t, c=c: e.activation(
                        ev, ps_sl, AF.Identity, bias=bias_t[:, c:c + 1]), inc="C")
                    guard[("qk", ping, dep2)] = tev
                    p.w("vector", "C", tev)
                    p.w("vector", "DT%d" % tpp, tok_t)
                    qk_last_v = p.emit("vector", lambda e, dst=dst, c=c, sc_=sc_, ev=ev, tview=tview: e.tensor_mul(
                        dst[:, c * S_LEN + sc_ * 512: c * S_LEN + (sc_ + 1) * 512],
                        ev, tview), inc="V")
                    etmp_busy[ping] = qk_last_v
                tp_busy[tpp] = qk_last_v

        for i in range(16):
            pi = i % 2
            xv = xt[:, pi * D:pi * D + D]
            p.w("sync", "V", xt_busy[pi])
            tok_x = p.emit("sync", lambda e, i=i, xv=xv, base=base: e.dma_start(
                xv, xh[base + i * 128: base + (i + 1) * 128, :]), inc="DX%d" % pi)
            p.w("vector", "DX%d" % pi, tok_x)
            p.w("vector", "T", guard.get(("lnt", i % 2), 0))
            sm = (i % 4) * 8
            lv_ = lnt[:, (i % 2) * D:(i % 2) * D + D]
            p.emit("vector", lambda e, xv=xv, lv_=lv_: e.tensor_mul(
                lv_, xv, xv), inc="V")
            p.emit("vector", lambda e, xv=xv, sm=sm: e.reduce_sum(
                stats[:, sm:sm + 1], xv, AX.X), inc="V")
            tvv = p.emit("vector", lambda e, sm=sm, lv_=lv_: e.reduce_sum(
                stats[:, sm + 1:sm + 2], lv_, AX.X), inc="V")
            p.w("scalar", "V", tvv)
            p.emit("scalar", lambda e, sm=sm: e.activation(
                stats[:, sm + 6:sm + 7], stats[:, sm + 1:sm + 2], AF.Identity,
                scale=f32(1.0 / D), bias=f32(1e-5)), inc="C")
            tcs = p.emit("scalar", lambda e, sm=sm: e.mul(
                stats[:, sm + 2:sm + 3], stats[:, sm:sm + 1], f32(1.0 / D)), inc="C")
            p.w("vector", "C", tcs)
            tvq = p.emit("vector", lambda e, sm=sm: e.tensor_mul(
                stats[:, sm + 3:sm + 4], stats[:, sm + 2:sm + 3], stats[:, sm + 2:sm + 3]), inc="V")
            p.w("scalar", "V", tvq)
            tcs = p.emit("scalar", lambda e, sm=sm: e.activation(
                stats[:, sm + 5:sm + 6], stats[:, sm + 3:sm + 4], AF.Sqrt,
                scale=f32(-1.0), bias=stats[:, sm + 6:sm + 7]), inc="C")
            p.w("vector", "C", tcs)
            p.emit("vector", lambda e, sm=sm: e.reciprocal(
                stats[:, sm + 4:sm + 5], stats[:, sm + 5:sm + 6]), inc="V")
            p.emit("vector", lambda e, xv=xv, sm=sm, lv_=lv_: e.tensor_single_scalar(
                lv_, xv, stats[:, sm + 2:sm + 3], OP.subtract), inc="V")
            tvn = p.emit("vector", lambda e, sm=sm, lv_=lv_: e.tensor_single_scalar(
                lv_, lv_, stats[:, sm + 4:sm + 5], OP.mult), inc="V")
            xt_busy[pi] = tvn
            if seq == 0 and i == 0:
                dbg_tap("xv0", [128, D], xv, "V", tvn)
                dbg_tap("lnt0", [128, D], lv_, "V", tvn)
                dbg_tap("stats0", [128, 32], stats[:], "V", tvn)
            ln_ready = emit_transpose_f32(
                lv_, 8,
                lambda b, i=i: ln1[:, b * S_LEN + i * 128: b * S_LEN + i * 128 + 128],
                stage=trstage)
            guard[("lnt", i % 2)] = tr["lastT"]
            ln_evac.append(ln_ready)
            if i % 4 == 3:
                a2_strip(i // 4)

        if seq == 0:
            dbg_tap("ln1c0", [128, S_LEN], ln1[:, 0:S_LEN], "V", ln_ready, BF16)
        if seq == 0:
            dbg_tap("qc0", [128, S_LEN], q_fm[:, 0:S_LEN], "V", qk_last_v, BF16)
            dbg_tap("kc0", [128, S_LEN], k_fm[:, 0:S_LEN], "V", qk_last_v, BF16)
        wv_s = wq_s
        tok_wv_ld = [None] * 8
        p.w("sync", "V", qk_last_v)
        for c in range(8):
            tok_wv_ld[c] = p.emit("sync", lambda e, c=c: e.dma_start(
                wv_s[:, c * D:(c + 1) * D], st_wv[c * 128:(c + 1) * 128, :]), inc="DB")
        if seq == 0:
            dbg_tap("wv_s", [128, 8 * D], wv_s[:, :], "DB", tok_wv_ld[7], BF16)
        v_last = 0
        for i in range(16):
            for n in range(2):
                ping = (i * 2 + n) % 2
                ps_sl = psB[:, ping * 512:(ping + 1) * 512]
                if i == 0:
                    p.w("tensor", "C", p.cnt["C"])
                p.w("tensor", "V", guard.get(("v", ping), 0))
                p.w("tensor", "DB", tok_wv_ld[7])
                tmm = None
                for kk in range(8):
                    tmm = p.emit("tensor", lambda e, i=i, n=n, kk=kk, ps_sl=ps_sl: e.matmul(
                        ps_sl,
                        ln1[:, kk * S_LEN + i * 128: kk * S_LEN + i * 128 + 128],
                        wv_s[:, kk * D + n * 512: kk * D + (n + 1) * 512],
                        start=(kk == 0), stop=(kk == 7)),
                        inc="T" if kk == 7 else None)
                p.w("vector", "T", tmm)
                v_last = p.emit("vector", lambda e, i=i, n=n, ps_sl=ps_sl: e.tensor_copy(
                    v_tm[:, i * D + n * 512: i * D + (n + 1) * 512], ps_sl), inc="V")
                guard[("v", ping)] = v_last

        if seq == 0:
            dbg_tap("vt0", [128, D], v_tm[:, 0:D], "V", v_last, BF16)
        A.release()
        probs = A.alloc(S_LEN, F32)
        probsT = A.alloc(S_LEN, BF16)
        x1t = A.alloc(2 * D, F32)
        xbv = A.alloc(D, F32)

        x1t_busy = [0, 0]
        prev_evac = 0
        prev_exp = 0
        for i in range(16):
            sm = 16 + (i % 2) * 4
            p.w("tensor", "V", max(v_last, prev_evac))
            p.w("tensor", "C", prev_exp)
            tmm = None
            for kk in range(8):
                for tc_ in range(4):
                    tmm = p.emit("tensor", lambda e, i=i, kk=kk, tc_=tc_: e.matmul(
                        psB[:, tc_ * 512:(tc_ + 1) * 512],
                        q_fm[:, kk * S_LEN + i * 128: kk * S_LEN + i * 128 + 128],
                        k_fm[:, kk * S_LEN + tc_ * 512: kk * S_LEN + (tc_ + 1) * 512],
                        start=(kk == 0), stop=(kk == 7)),
                        inc="T" if kk == 7 and tc_ == 3 else None)
            p.w("vector", "T", tmm)
            tnm2 = p.emit("vector", lambda e, sm=sm: e.reduce_max(
                stats[:, sm:sm + 1], psB[:, 0:2048], AX.X, negate=True), inc="V")
            p.w("scalar", "V", tnm2)
            prev_exp = p.emit("scalar", lambda e, sm=sm: e.activation(
                probs[:, :], psB[:, 0:2048], AF.Exp, bias=stats[:, sm:sm + 1],
                scale=f32(1.0)), inc="C")
            p.w("vector", "C", prev_exp)
            trs = p.emit("vector", lambda e, sm=sm: e.reduce_sum(
                stats[:, sm + 2:sm + 3], probs[:, :], AX.X), inc="V")
            for g in range(4):
                p.w("tensor", "C", prev_exp)
                p.w("tensor", "V", prev_evac)
                p.w("tensor", "G", st["ident"])
                tt = None
                for b in range(4):
                    blk = g * 4 + b
                    tt = p.emit("tensor", lambda e, blk=blk, b=b: e.transpose(
                        psB[:, b * 128:(b + 1) * 128],
                        probs[:, blk * 128:(blk + 1) * 128], ident[:]),
                        inc="T" if b == 3 else None)
                tr["lastT"] = tt
                p.w("vector", "T", tt)
                if g == 2:
                    trz = p.emit("vector", lambda e, sm=sm: e.reciprocal(
                        stats[:, sm + 3:sm + 4], stats[:, sm + 2:sm + 3]), inc="V")
                prev_evac = p.emit("vector", lambda e, g=g: e.tensor_copy(
                    probsT[:, g * 512:(g + 1) * 512], psB[:, 0:512]), inc="V")
            if seq == 0 and i == 0:
                dbg_tap("probs0", [128, S_LEN], probs[:, :], "V", prev_evac)
            p.w("tensor", "V", max(prev_evac, guard.get("pv", 0)))
            tpv = None
            for tk in range(16):
                for n in range(2):
                    tpv = p.emit("tensor", lambda e, tk=tk, n=n: e.matmul(
                        psPV[:, n * 512:(n + 1) * 512],
                        probsT[:, tk * 128:(tk + 1) * 128],
                        v_tm[:, tk * D + n * 512: tk * D + (n + 1) * 512],
                        start=(tk == 0), stop=(tk == 15)),
                        inc="T" if tk == 15 and n == 1 else None)
            pi = i % 2
            xv = xt[:, pi * D:pi * D + D]
            p.w("sync", "V", xt_busy[pi])
            tok_x = p.emit("sync", lambda e, i=i, xv=xv, base=base: e.dma_start(
                xv, xh[base + i * 128: base + (i + 1) * 128, :]), inc="DX%d" % pi)
            p.w("vector", "DX%d" % pi, tok_x)
            p.w("vector", "DG", st["bvb"])
            tb = p.emit("vector", lambda e, xv=xv: e.tensor_add(xbv[:, :], xv, bvb[:]), inc="V")
            xt_busy[pi] = tb
            p.w("vector", "DG", x1t_busy[pi])
            p.w("vector", "T", tpv)
            p.w("vector", "V", trz)
            x1v = x1t[:, pi * D:pi * D + D]
            tx1 = p.emit("vector", lambda e, x1v=x1v, sm=sm: e.scalar_tensor_tensor(
                x1v, psPV[:, 0:1024], stats[:, sm + 3:sm + 4], xbv[:, :],
                OP.mult, OP.add), inc="V")
            guard["pv"] = tx1
            p.w("gpsimd", "V", tx1)
            x1_store = p.emit("gpsimd", lambda e, i=i, x1v=x1v, base=base: e.dma_start(
                x1h[base + i * 128: base + (i + 1) * 128, :], x1v), inc="DG")
            x1t_busy[pi] = x1_store
            if seq == 0 and i == 0:
                dbg_tap("x1t0", [128, D], x1v, "V", guard["pv"])

        A.release()

        A.mark()
        w1t_s = A.alloc(8 * HID, BF16)
        hq = A.alloc(32 * 512, BF16)
        ln2s = [A.alloc(8 * 512, BF16), A.alloc(8 * 512, BF16)]
        w2t_win = A.alloc(4 * D, BF16)
        xt2 = A.alloc(2 * D, F32)
        xt3 = A.alloc(2 * D, F32)
        lnt2 = A.alloc(4 * D, F32)
        y0 = A.alloc(2 * D, F32)
        outt = A.alloc(2 * D, F32)
        if QUANT == "ladder":
            scl = {k: A.alloc(D, F16) for k in
                   ("ax", "sg", "l", "v", "dm", "e", "u", "v2", "r", "t1")}
            scl["a1"] = A.alloc(D, F32)
            scl["a"] = A.alloc(D, F32)
            scl["pp"] = A.alloc(D, F32)
            scl["pv"] = A.alloc(D, F32)
            hsrc = A.alloc(2 * 512, F32)
            yq = A.alloc(D, F32)
        else:
            h_f8 = A.alloc(512 // 2, F16).bitcast(F8)
            y16 = A.alloc(D, F16)

        tok_w1_ld = [None] * 8
        phase_barrier()
        p.w("sync", "DB", st["w_stash_done"])
        for c in range(8):
            tok_w1_ld[c] = p.emit("sync", lambda e, c=c: e.dma_start(
                w1t_s[:, c * HID:(c + 1) * HID], st_w1[c * 128:(c + 1) * 128, :]), inc="DB")

        xt2_busy = [0, 0]
        xt3_busy = [0, 0]
        out_busy = [0, 0]
        ln2_ready_ch = {}

        def b1_chain(ch):
            crow = base + ch * 512
            for ii in range(4):
                pi = ii % 2
                xv = xt2[:, pi * D:pi * D + D]
                p.w("sync", "V", xt2_busy[pi])
                p.w("sync", "DG", x1_store)
                tok_x = p.emit("sync", lambda e, ii=ii, xv=xv, crow=crow: e.dma_start(
                    xv, x1h[crow + ii * 128: crow + (ii + 1) * 128, :]), inc="DX%d" % (2 + pi))
                p.w("vector", "DX%d" % (2 + pi), tok_x)
                p.w("vector", "T", guard.get(("lnt2", ii), 0))
                sm = ii * 8
                lv_ = lnt2[:, ii * D:ii * D + D]
                p.emit("vector", lambda e, xv=xv, lv_=lv_: e.tensor_mul(
                    lv_, xv, xv), inc="V")
                p.emit("vector", lambda e, xv=xv, sm=sm: e.reduce_sum(
                    stats[:, sm:sm + 1], xv, AX.X), inc="V")
                tvv = p.emit("vector", lambda e, sm=sm, lv_=lv_: e.reduce_sum(
                    stats[:, sm + 1:sm + 2], lv_, AX.X), inc="V")
                p.w("scalar", "V", tvv)
                p.emit("scalar", lambda e, sm=sm: e.activation(
                    stats[:, sm + 6:sm + 7], stats[:, sm + 1:sm + 2], AF.Identity,
                    scale=f32(1.0 / D), bias=f32(1e-5)), inc="C")
                tcs = p.emit("scalar", lambda e, sm=sm: e.mul(
                    stats[:, sm + 2:sm + 3], stats[:, sm:sm + 1], f32(1.0 / D)), inc="C")
                p.w("vector", "C", tcs)
                tvq = p.emit("vector", lambda e, sm=sm: e.tensor_mul(
                    stats[:, sm + 3:sm + 4], stats[:, sm + 2:sm + 3], stats[:, sm + 2:sm + 3]), inc="V")
                p.w("scalar", "V", tvq)
                tcs = p.emit("scalar", lambda e, sm=sm: e.activation(
                    stats[:, sm + 5:sm + 6], stats[:, sm + 3:sm + 4], AF.Sqrt,
                    scale=f32(-1.0), bias=stats[:, sm + 6:sm + 7]), inc="C")
                p.w("vector", "C", tcs)
                p.emit("vector", lambda e, sm=sm: e.reciprocal(
                    stats[:, sm + 4:sm + 5], stats[:, sm + 5:sm + 6]), inc="V")
                p.emit("vector", lambda e, xv=xv, sm=sm, lv_=lv_: e.tensor_single_scalar(
                    lv_, xv, stats[:, sm + 2:sm + 3], OP.subtract), inc="V")
                tvn = p.emit("vector", lambda e, sm=sm, lv_=lv_: e.tensor_single_scalar(
                    lv_, lv_, stats[:, sm + 4:sm + 5], OP.mult), inc="V")
                xt2_busy[pi] = tvn

        def b1_tr(ch):
            lnb = ln2s[ch % 2]
            last = None
            for ii in range(4):
                lv_ = lnt2[:, ii * D:ii * D + D]
                last = emit_transpose_f32(
                    lv_, 8,
                    lambda b, ii=ii, lnb=lnb: lnb[:, b * 512 + ii * 128: b * 512 + ii * 128 + 128])
                guard[("lnt2", ii)] = tr["lastT"]
            ln2_ready_ch[ch] = last

        b1_chain(0)
        b1_tr(0)
        for ch in range(4):
            crow = base + ch * 512
            lnb = ln2s[ch % 2]
            ln2_ready = ln2_ready_ch[ch]
            if seq == 0 and ch == 0:
                dbg_tap("ln2c0", [128, 512], lnb[:, 0:512], "V", ln2_ready, BF16)
            hq_ready_v = 0
            last_relu_c = 0
            hq_busy = [0, 0]
            for jt in range(32):
                ping = jt % 2
                ps = psA0 if ping == 0 else psA1
                p.w("tensor", "C", guard.get(("f1", ping), 0))
                p.w("tensor", "V", ln2_ready)
                p.w("tensor", "DB", tok_w1_ld[7])
                tmm = None
                for kk in range(8):
                    tmm = p.emit("tensor", lambda e, jt=jt, kk=kk, ps=ps, lnb=lnb: e.matmul(
                        ps[:, 0:512],
                        w1t_s[:, kk * HID + jt * 128: kk * HID + jt * 128 + 128],
                        lnb[:, kk * 512:(kk + 1) * 512],
                        start=(kk == 0), stop=(kk == 7)),
                        inc="T" if kk == 7 else None)
                p.w("scalar", "T", tmm)
                p.w("scalar", "DS", st["b1L"])
                if QUANT == "ladder":
                    hv = hsrc[:, ping * 512:(ping + 1) * 512]
                    p.w("scalar", "V", hq_busy[ping])
                    trl = p.emit("scalar", lambda e, ps=ps, hv=hv, jt=jt: e.activation(
                        hv, ps[:, 0:512], AF.Relu, bias=b1L[:, jt:jt + 1]), inc="C")
                    guard[("f1", ping)] = trl
                    last_relu_c = trl
                    hq_ready_v = emit_ladder(
                        scl, hv, hq[:, jt * 512:(jt + 1) * 512], 512, 128, 3,
                        -6.0, 7.99, (8, 4, 2, 1), (16, 8, 4, 2, 1), 240.0,
                        False, ("C", trl))
                    hq_busy[ping] = hq_ready_v
                else:
                    p.emit("scalar", lambda e, ps=ps, jt=jt: e.activation(
                        h_f8[:, 0:512], ps[:, 0:512], AF.Relu, bias=b1L[:, jt:jt + 1]), inc="C")
                    trl = p.emit("scalar", lambda e, jt=jt: e.copy(
                        hq[:, jt * 512:(jt + 1) * 512], h_f8[:, 0:512]), inc="C")
                    guard[("f1", ping)] = trl
                    last_relu_c = trl

            if seq == 0 and ch == 0:
                if QUANT == "ladder":
                    dbg_tap("hq0", [128, 512], hq[:, 0:512], "V", hq_ready_v, BF16)
                else:
                    dbg_tap("hq0", [128, 512], hq[:, 0:512], "C", last_relu_c, BF16)
            yps = [psB[:, 0:1024], psB[:, 1024:2048], psPV[:, 0:1024]]
            if seq == 0 and ch == 0:
                dbg_tap("hqall", [128, 32 * 512], hq[:, :], "C", last_relu_c, BF16)
                dbg_tap("b2qbt", [128, D], b2qb[:], "DG", st["b2qb"])
            w2_busy = [0, guard.get("b3_prev_t", 0)]
            w2_tok = [0, 0]
            tff = None
            for jkp in range(17):
                if jkp < 16:
                    wp = jkp % 2
                    p.w("sync", "T", w2_busy[wp] if jkp > 0 else guard.get("b3_prev_t", 0))
                    for tl in range(2):
                        w2_tok[wp] = p.emit("sync", lambda e, jkp=jkp, wp=wp, tl=tl: e.dma_start(
                            w2t_win[:, (wp * 2 + tl) * D:(wp * 2 + tl + 1) * D],
                            st_w2[jkp * 256 + tl * 128: jkp * 256 + (tl + 1) * 128, :]),
                            inc="DW%d" % wp)
                if jkp < 1:
                    continue
                for jk in range((jkp - 1) * 2, jkp * 2):
                    p.w("tensor", "DW%d" % ((jk // 2) % 2), w2_tok[(jk // 2) % 2])
                    wwin = w2t_win[:, (jk % 4) * D:((jk % 4) + 1) * D]
                    if jk == 0:
                        p.w("tensor", "V", max(hq_ready_v, guard.get("ffn2", 0)))
                        p.w("tensor", "C", last_relu_c)
                    for ii in range(4):
                        for n in range(2):
                            if ii < 3:
                                dst = yps[ii][:, n * 512:(n + 1) * 512]
                            else:
                                dst = (psA0 if n == 0 else psA1)[:, 0:512]
                            tff = p.emit("tensor", lambda e, jk=jk, dst=dst, ii=ii, n=n, wwin=wwin: e.matmul(
                                dst,
                                hq[:, jk * 512 + ii * 128: jk * 512 + ii * 128 + 128],
                                wwin[:, n * 512:(n + 1) * 512],
                                start=(jk == 0), stop=(jk == 31)),
                                inc="T" if ii == 3 and n == 1 else None)
                    w2_busy[(jk // 2) % 2] = tff
            guard["b3_prev_t"] = tff
            guard[("f1", 0)] = 0
            guard[("f1", 1)] = 0
            if ch < 3:
                b1_chain(ch + 1)
            yv3 = y0[:, D:2 * D]
            p.w("vector", "T", tff)
            p.w("vector", "DG", st["b2qb"])
            p.w("vector", "V", guard.get(("y0", 1), 0))
            p.emit("vector", lambda e, yv3=yv3: e.tensor_add(
                yv3[:, 0:512], psA0[:, 0:512], b2qb[:, 0:512]), inc="V")
            ty0_3 = p.emit("vector", lambda e, yv3=yv3: e.tensor_add(
                yv3[:, 512:1024], psA1[:, 0:512], b2qb[:, 512:1024]), inc="V")
            if ch < 3:
                b1_tr(ch + 1)
            y_ep = 0
            for ii in (3, 0, 1, 2):
                pi = ii % 2
                xv = xt3[:, pi * D:pi * D + D]
                p.w("sync", "V", xt3_busy[pi])
                tok_x = p.emit("sync", lambda e, ii=ii, xv=xv, crow=crow: e.dma_start(
                    xv, x1h[crow + ii * 128: crow + (ii + 1) * 128, :]), inc="DX%d" % pi)
                yv = y0[:, pi * D:pi * D + D]
                if ii == 3:
                    ty0 = ty0_3
                else:
                    p.w("vector", "T", tff)
                    p.w("vector", "DG", st["b2qb"])
                    p.w("vector", "V", guard.get(("y0", pi), 0))
                    ty0 = p.emit("vector", lambda e, ii=ii, yv=yv: e.tensor_add(
                        yv, yps[ii], b2qb[:]), inc="V")
                ov = outt[:, pi * D:pi * D + D]
                if seq == 0 and ch == 0 and ii == 0:
                    dbg_tap("y00", [128, D], yv, "V", ty0)
                p.w("vector", "DG", out_busy[pi])
                if QUANT == "ladder":
                    emit_ladder(scl, yv, yq[:, 0:D], D, 128, 10, -14.0, 3.99,
                                (16, 8, 4, 2, 1),
                                (2048, 1024, 512, 256, 128, 64, 32, 16, 8, 4, 2, 1),
                                65504.0, True, ("V", ty0))
                    p.w("vector", "DX%d" % pi, tok_x)
                    tout = p.emit("vector", lambda e, ov=ov, xv=xv: e.tensor_add(
                        ov, yq[:, 0:D], xv), inc="V")
                else:
                    p.w("scalar", "V", ty0)
                    tqc = p.emit("scalar", lambda e, yv=yv: e.copy(y16[:, 0:D], yv), inc="C")
                    p.w("vector", "C", tqc)
                    p.w("vector", "DX%d" % pi, tok_x)
                    tout = p.emit("vector", lambda e, ov=ov, xv=xv: e.tensor_add(
                        ov, y16[:, 0:D], xv), inc="V")
                guard[("y0", pi)] = tout
                xt3_busy[pi] = tout
                y_ep = tout
                p.w("gpsimd", "V", tout)
                last_out_store = p.emit("gpsimd", lambda e, ii=ii, ov=ov, crow=crow: e.dma_start(
                    outh[crow + ii * 128: crow + (ii + 1) * 128, :], ov), inc="DG")
                out_busy[pi] = last_out_store
            guard["ffn2"] = y_ep

        A.release()

    p.w("gpsimd", "DG", last_out_store)

    with nc.Block() as block:
        @block.sync
        def _(e):
            for fn in p.stream["sync"]:
                fn(e)

        @block.vector
        def _(e):
            for fn in p.stream["vector"]:
                fn(e)

        @block.scalar
        def _(e):
            for fn in p.stream["scalar"]:
                fn(e)

        @block.gpsimd
        def _(e):
            for fn in p.stream["gpsimd"]:
                fn(e)

        @block.tensor
        def _(e):
            for fn in p.stream["tensor"]:
                fn(e)

    return nc


def kernel(L0_attn_hidden_states, L0_attn_Wq, L0_attn_bq, L0_attn_Wk, L0_attn_bk,
           L0_attn_Wv, L0_attn_bv, L0_ffn_w1, L0_ffn_b1, L0_ffn_w2, L0_ffn_b2):
    x = np.ascontiguousarray(np.asarray(L0_attn_hidden_states, dtype=np.float32))
    tt = rope_table_T()
    common = {
        "wq": np.ascontiguousarray(np.asarray(L0_attn_Wq, np.float32)),
        "wk": np.ascontiguousarray(np.asarray(L0_attn_Wk, np.float32)),
        "wv": np.ascontiguousarray(np.asarray(L0_attn_Wv, np.float32)),
        "bq": np.asarray(L0_attn_bq, np.float32).reshape(1, D),
        "bk": np.asarray(L0_attn_bk, np.float32).reshape(1, D),
        "bv": np.asarray(L0_attn_bv, np.float32).reshape(1, D),
        "w1": np.ascontiguousarray(np.asarray(L0_ffn_w1, np.float32)),
        "b1": np.asarray(L0_ffn_b1, np.float32).reshape(1, HID),
        "w2": np.ascontiguousarray(np.asarray(L0_ffn_w2, np.float32)),
        "b2": np.asarray(L0_ffn_b2, np.float32).reshape(1, D),
        "tt": tt,
    }
    in_maps = []
    for c in range(8):
        m = dict(common)
        m["x"] = np.ascontiguousarray(
            x[c * NSEQ:(c + 1) * NSEQ].reshape(NSEQ * S_LEN, D))
        in_maps.append(m)
    nc = build_core_program()
    res = run_bass_kernel_spmd(nc, in_maps, core_ids=list(range(8)))
    if DEBUG:
        global _dbg
        _dbg = {k: np.asarray(res.results[0]["dbg_" + k]) for k in _dbg_names}
        global _dbg2
        _dbg2 = {k: np.asarray(res.results[1]["dbg_" + k]) for k in _dbg_names}
    out = np.concatenate(
        [np.asarray(r["out"]).reshape(NSEQ, S_LEN, D) for r in res.results], axis=0)
    return np.ascontiguousarray(out.astype(np.float32))


# version 17
# speedup vs baseline: 1.0667x; 1.0667x over previous
import numpy as np
import concourse.bass as bass
import concourse.mybir as mybir
from concourse.alu_op_type import AluOpType as OP
from concourse.bass_utils import run_bass_kernel_spmd

F32 = mybir.dt.float32
BF16 = mybir.dt.bfloat16
F16 = mybir.dt.float16
F8 = mybir.dt.float8e4
AF = mybir.ActivationFunctionType
AX = mybir.AxisListType
f32 = np.float32

QUANT = "cast"
DEBUG = False
_dbg_names = []

S_LEN = 2048
D = 1024
HID = 4096
NSEQ = 2
ROWS = NSEQ * S_LEN
INV_SQRT_HS = f32(1.0 / float(np.sqrt(np.float64(512.0))))

LN2_ = np.float64(0.6931471805599453)
INVLN2 = f32(1.0 / LN2_)
LN2HI = f32(LN2_)
LN2LO = f32(LN2_ - np.float64(f32(LN2_)))
B3HI = f32(3.0 * LN2_)
B3LO = f32(3.0 * LN2_ - np.float64(f32(3.0 * LN2_)))
NB3LO = f32(-np.float64(f32(3.0 * LN2_ - np.float64(f32(3.0 * LN2_)))))
NB10 = f32(-(10.0 * LN2_))

ENGS = ("sync", "vector", "scalar", "gpsimd", "tensor")


class Prog:
    def __init__(self, nc):
        self.nc = nc
        self.stream = {e: [] for e in ENGS}
        self.sems = {}
        self.cnt = {}
        self.last_wait = {}
        for name in ("T", "V", "C", "G", "DS", "DG", "DB", "DW0", "DW1",
                     "DX0", "DX1", "DX2", "DX3", "DT0", "DT1"):
            self.sems[name] = nc.alloc_semaphore("sem_" + name)
            self.cnt[name] = 0

    def w(self, eng, sem, tok):
        if tok is None or tok <= 0:
            return
        if self.last_wait.get((eng, sem), 0) >= tok:
            return
        self.last_wait[(eng, sem)] = tok
        s = self.sems[sem]
        self.stream[eng].append(lambda e, s=s, t=tok: e.wait_ge(s, t))

    def emit(self, eng, fn, inc=None):
        if inc is None:
            self.stream[eng].append(fn)
            return None
        s = self.sems[inc]
        step = 16 if inc.startswith("D") else 1
        self.cnt[inc] += step
        tok = self.cnt[inc]
        self.stream[eng].append(lambda e, fn=fn, s=s, st=step: fn(e).then_inc(s, st))
        return tok


class Arena:
    def __init__(self, t, nels_bf16):
        self.t = t
        self.nb = nels_bf16 * 2
        self.off = 0
        self.marks = []
        self.peak = 0

    def alloc(self, els, dtype):
        unit = {F32: 4, BF16: 2, F16: 2}[dtype]
        off = (self.off + 3) & ~3
        sz = els * unit
        assert off + sz <= self.nb, f"arena overflow: {off}+{sz} > {self.nb}"
        self.off = off + sz
        self.peak = max(self.peak, self.off)
        v = self.t[:, off // 2:(off + sz) // 2]
        if dtype != BF16:
            v = v.bitcast(dtype)
        return v

    def mark(self):
        self.marks.append(self.off)

    def release(self):
        self.off = self.marks.pop()


def rope_table_T():
    pos = np.arange(S_LEN, dtype=np.float64)
    inv = 10000.0 ** (np.arange(0, 512, 2, dtype=np.float64) / 512.0)
    radians = pos[:, None] / inv[None, :]
    sincos = np.stack((np.sin(radians), np.cos(radians)), axis=-1).reshape(S_LEN, -1)
    t = np.repeat(sincos, 2, axis=-1) * (512.0 ** -0.25)
    return np.ascontiguousarray(t.T.astype(np.float32))


def build_core_program():
    nc = bass.Bass("TRN2", target_bir_lowering=False)
    xh = nc.dram_tensor("x", [ROWS, D], F32, kind="ExternalInput").ap()
    wqh = nc.dram_tensor("wq", [D, D], F32, kind="ExternalInput").ap()
    wkh = nc.dram_tensor("wk", [D, D], F32, kind="ExternalInput").ap()
    wvh = nc.dram_tensor("wv", [D, D], F32, kind="ExternalInput").ap()
    bqh = nc.dram_tensor("bq", [1, D], F32, kind="ExternalInput").ap()
    bkh = nc.dram_tensor("bk", [1, D], F32, kind="ExternalInput").ap()
    bvh = nc.dram_tensor("bv", [1, D], F32, kind="ExternalInput").ap()
    w1h = nc.dram_tensor("w1", [HID, D], F32, kind="ExternalInput").ap()
    b1h = nc.dram_tensor("b1", [1, HID], F32, kind="ExternalInput").ap()
    w2h = nc.dram_tensor("w2", [D, HID], F32, kind="ExternalInput").ap()
    b2h = nc.dram_tensor("b2", [1, D], F32, kind="ExternalInput").ap()
    tth = nc.dram_tensor("tt", [D, S_LEN], F32, kind="ExternalInput").ap()
    outh = nc.dram_tensor("out", [ROWS, D], F32, kind="ExternalOutput").ap()
    x1h = nc.dram_tensor("x1s", [ROWS, D], F32, kind="Internal").ap()
    st_wq = nc.dram_tensor("st_wq", [D, D], BF16, kind="Internal").ap()
    st_wk = nc.dram_tensor("st_wk", [D, D], BF16, kind="Internal").ap()
    st_wv = nc.dram_tensor("st_wv", [D, D], BF16, kind="Internal").ap()
    st_w1 = nc.dram_tensor("st_w1", [D, HID], BF16, kind="Internal").ap()
    st_w2 = nc.dram_tensor("st_w2", [HID, D], BF16, kind="Internal").ap()
    st_bp = nc.dram_tensor("st_bp", [2, D], F32, kind="Internal").ap()
    dbg = {}
    def dbg_out(name, shape, dtype=F32):
        global _dbg_names
        if name not in _dbg_names:
            _dbg_names.append(name)
        dbg[name] = nc.dram_tensor("dbg_" + name, shape, dtype, kind="ExternalOutput").ap()
        return dbg[name]
    def dbg_tap(name, shape, src_ap, sem, tok, dtype=F32):
        if not DEBUG:
            return
        d = dbg_out(name, shape, dtype)
        p.w("gpsimd", sem, tok)
        tk = p.emit("gpsimd", lambda e, d=d, src_ap=src_ap: e.dma_start(d, src_ap), inc="DG")
        for eng in ("sync", "vector", "scalar", "tensor"):
            p.w(eng, "DG", tk)

    ident = nc.alloc_sbuf_tensor("ident", [128, 128], F32)
    bqr = nc.alloc_sbuf_tensor("bqr", [128, 8], F32)
    bkr = nc.alloc_sbuf_tensor("bkr", [128, 8], F32)
    b1L = nc.alloc_sbuf_tensor("b1L", [128, 32], F32)
    b2qb = nc.alloc_sbuf_tensor("b2qb", [128, D], F32)
    bvb = nc.alloc_sbuf_tensor("bvb", [128, D], F32)
    stats = nc.alloc_sbuf_tensor("stats", [128, 32], F32)
    bscr = nc.alloc_sbuf_tensor("bscr", [128, 2 * D], F32)

    NELS = 94208
    AB = nc.alloc_sbuf_tensor("AB", [128, NELS], BF16)

    psA0 = nc.alloc_psum_tensor("psA0", [128, 512], F32)
    psA1 = nc.alloc_psum_tensor("psA1", [128, 512], F32)
    psB = nc.alloc_psum_tensor("psB", [128, 2048], F32)
    psPV = nc.alloc_psum_tensor("psPV", [128, 1024], F32)

    p = Prog(nc)
    A = Arena(AB, NELS)
    st = {}

    def emit_ladder(sc, src_ap, out_ap, W, npart, man, e_lo, e_hi, e_levels,
                    r_levels, maxval, signed, dep):
        ax, sg, l, v, dm, e, u, v2, r, pp, t1 = (
            sc["ax"][:npart, :W], sc["sg"][:npart, :W], sc["l"][:npart, :W],
            sc["v"][:npart, :W], sc["dm"][:npart, :W], sc["e"][:npart, :W],
            sc["u"][:npart, :W], sc["v2"][:npart, :W], sc["r"][:npart, :W],
            sc["pp"][:npart, :W], sc["t1"][:npart, :W])
        a1, a = sc["a1"][:npart, :W], sc["a"][:npart, :W]
        pv = sc["pv"][:npart, :W]
        if dep is not None:
            p.w("vector", dep[0], dep[1])
        if signed:
            p.emit("vector", lambda ee: ee.scalar_tensor_tensor(
                ax, src_ap, float(-1.0), src_ap, OP.mult, OP.max), inc="V")
            p.emit("vector", lambda ee: ee.tensor_single_scalar(
                sg, src_ap, float(0.0), OP.is_lt), inc="V")
            p.emit("vector", lambda ee: ee.tensor_scalar(
                sg, sg, float(-2.0), float(1.0), OP.mult, OP.add), inc="V")
        else:
            p.emit("vector", lambda ee: ee.tensor_copy(ax, src_ap), inc="V")
        tv = p.emit("vector", lambda ee: ee.tensor_single_scalar(
            l, ax, float(1e-7), OP.max), inc="V")
        p.w("scalar", "V", tv)
        tc = p.emit("scalar", lambda ee: ee.activation(l, l, AF.Ln), inc="C")
        p.w("vector", "C", tc)
        p.emit("vector", lambda ee: ee.tensor_scalar(
            e, l, float(INVLN2), float(e_lo), OP.mult, OP.max), inc="V")
        p.emit("vector", lambda ee: ee.tensor_single_scalar(
            e, e, float(e_hi), OP.min), inc="V")
        p.emit("vector", lambda ee: ee.tensor_single_scalar(
            t1, e, float(-e_lo), OP.add), inc="V")
        first_e = True
        for stp in e_levels:
            sv_ = t1 if first_e else v
            p.emit("vector", lambda ee, s=stp, sv_=sv_: ee.tensor_scalar(
                dm, sv_, float(s), float(-s), OP.is_ge, OP.mult), inc="V")
            p.emit("vector", lambda ee, sv_=sv_: ee.tensor_add(v, sv_, dm), inc="V")
            first_e = False
        p.emit("vector", lambda ee: ee.tensor_sub(e, t1, v), inc="V")
        p.emit("vector", lambda ee: ee.tensor_single_scalar(
            e, e, float(e_lo), OP.add), inc="V")
        if man == 3:
            p.emit("vector", lambda ee: ee.tensor_scalar(
                a1, e, float(-LN2HI), float(B3HI), OP.mult, OP.add), inc="V")
            tv = p.emit("vector", lambda ee: ee.scalar_tensor_tensor(
                a, e, float(-LN2LO), a1, OP.mult, OP.add), inc="V")
            p.w("scalar", "V", tv)
            p.emit("scalar", lambda ee: ee.activation(
                pv, a, AF.Exp, bias=B3LO), inc="C")
            tc = p.emit("scalar", lambda ee: ee.activation(
                pp, a, AF.Exp, bias=NB3LO, scale=f32(-1.0)), inc="C")
            p.w("vector", "C", tc)
            p.emit("vector", lambda ee: ee.tensor_mul(u, ax, pv), inc="V")
        else:
            p.emit("vector", lambda ee: ee.tensor_single_scalar(
                a1, e, float(-LN2HI), OP.mult), inc="V")
            tv = p.emit("vector", lambda ee: ee.scalar_tensor_tensor(
                a, e, float(-LN2LO), a1, OP.mult, OP.add), inc="V")
            p.w("scalar", "V", tv)
            p.emit("scalar", lambda ee: ee.activation(pv, a, AF.Exp), inc="C")
            tc = p.emit("scalar", lambda ee: ee.activation(
                pp, a, AF.Exp, bias=NB10, scale=f32(-1.0)), inc="C")
            p.w("vector", "C", tc)
            p.emit("vector", lambda ee: ee.tensor_single_scalar(
                dm, ax, float(1024.0), OP.mult), inc="V")
            p.emit("vector", lambda ee: ee.tensor_mul(u, dm, pv), inc="V")
        first = True
        for stp in r_levels:
            sv = u if first else v2
            p.emit("vector", lambda ee, s=stp, sv=sv: ee.tensor_scalar(
                dm, sv, float(s - 0.5), float(-s), OP.is_ge, OP.mult), inc="V")
            p.emit("vector", lambda ee, sv=sv: ee.tensor_add(v2, sv, dm), inc="V")
            first = False
        p.emit("vector", lambda ee: ee.tensor_sub(r, u, v2), inc="V")
        p.emit("vector", lambda ee: ee.tensor_mul(r, r, pp), inc="V")
        p.emit("vector", lambda ee: ee.tensor_scalar(
            t1, r, float(-maxval), float(0.0), OP.add, OP.max), inc="V")
        p.emit("vector", lambda ee: ee.tensor_scalar(
            t1, t1, float(60000.0), float(60000.0), OP.mult, OP.mult), inc="V")
        if signed:
            p.emit("vector", lambda ee: ee.tensor_add(r, r, t1), inc="V")
            return p.emit("vector", lambda ee: ee.tensor_mul(out_ap, r, sg), inc="V")
        return p.emit("vector", lambda ee: ee.tensor_add(out_ap, r, t1), inc="V")

    tr = {"ping": 0, "lastT": 0}

    def emit_transpose_f32(src_view, nblk, dst_fn, extra_t_waits=(), stage=None):
        pair = stage if stage is not None else (psA0, psA1)
        last_v = None
        for g0 in range(0, nblk, 4):
            gn = min(4, nblk - g0)
            ps = pair[tr["ping"]]
            tr["ping"] = 1 - tr["ping"]
            p.w("tensor", "V", p.cnt["V"])
            p.w("tensor", "C", p.cnt["C"])
            for s_, tk in extra_t_waits:
                p.w("tensor", s_, tk)
            p.w("tensor", "G", st["ident"])
            tt = None
            for b in range(gn):
                blk = g0 + b
                tt = p.emit("tensor", lambda e, blk=blk, b=b, ps=ps: e.transpose(
                    ps[:, b * 128:(b + 1) * 128],
                    src_view[:, blk * 128:(blk + 1) * 128], ident[:]),
                    inc="T" if b == gn - 1 else None)
            tr["lastT"] = tt
            p.w("vector", "T", tt)
            for b in range(gn):
                dst = dst_fn(g0 + b)
                last_v = p.emit("vector", lambda e, dst=dst, b=b, ps=ps: e.tensor_copy(
                    dst, ps[:, b * 128:(b + 1) * 128]), inc="V")
        return last_v

    p.emit("gpsimd", lambda e: e.memset(ident[:], 0.0))
    st["ident"] = p.emit("gpsimd", lambda e: e.affine_select(
        out=ident[:], in_=ident[:], compare_op=OP.not_equal, fill=1.0,
        base=0, pattern=[[-1, 128]], channel_multiplier=1), inc="G")

    A.mark()
    wchunk = A.alloc(2 * D, F32)
    wt_bf = A.alloc(8 * D, BF16)
    wr_bf = A.alloc(8 * D, BF16)
    prev_stash = 0

    def prep_attn_weight(wsrc, dst_stash, fold):
        nonlocal prev_stash
        p.w("vector", "DB", prev_stash)
        last_v = None
        tr_tok = [0, 0]
        for c in range(8):
            off = (c % 2) * D
            wv_ = wchunk[:, off:off + D]
            p.w("sync", "T", tr_tok[c % 2])
            tok_ld = p.emit("sync", lambda e, c=c, wv_=wv_: e.dma_start(
                wv_, wsrc[c * 128:(c + 1) * 128, :]), inc="DT%d" % (c % 2))
            last_v = emit_transpose_f32(
                wv_, 8, lambda b, c=c: wt_bf[:, b * D + c * 128: b * D + c * 128 + 128],
                extra_t_waits=(("DT%d" % (c % 2), tok_ld), ("DB", prev_stash)))
            tr_tok[c % 2] = tr["lastT"]
        if fold:
            p.w("vector", "V", last_v)
            for c in range(8):
                wtc = wt_bf[:, c * D:(c + 1) * D].rearrange("p (n two) -> p n two", two=2)
                wrc = wr_bf[:, c * D:(c + 1) * D].rearrange("p (n two) -> p n two", two=2)
                p.emit("vector", lambda e, wtc=wtc, wrc=wrc: e.tensor_sub(
                    wrc[:, :, 0:1], wtc[:, :, 0:1], wtc[:, :, 1:2]), inc="V")
                last_v = p.emit("vector", lambda e, wtc=wtc, wrc=wrc: e.tensor_add(
                    wrc[:, :, 1:2], wtc[:, :, 1:2], wtc[:, :, 0:1]), inc="V")
            src = wr_bf
        else:
            src = wt_bf
        p.w("sync", "V", last_v)
        tok = None
        for c in range(8):
            tok = p.emit("sync", lambda e, c=c, src=src: e.dma_start(
                dst_stash[c * 128:(c + 1) * 128, :], src[:, c * D:(c + 1) * D]), inc="DB")
        prev_stash = tok
        return tok

    prep_attn_weight(wqh, st_wq, True)
    prep_attn_weight(wkh, st_wk, True)
    st["attn_w_stash"] = prep_attn_weight(wvh, st_wv, False)
    A.release()

    A.mark()
    w1c = A.alloc(2 * D, F32)
    w1t_full = A.alloc(8 * HID, BF16)
    tr_tok1 = [0, 0]
    for c in range(32):
        off = (c % 2) * D
        wv_ = w1c[:, off:off + D]
        p.w("sync", "T", tr_tok1[c % 2])
        tok_ld = p.emit("sync", lambda e, c=c, wv_=wv_: e.dma_start(
            wv_, w1h[c * 128:(c + 1) * 128, :]), inc="DT%d" % (c % 2))
        tv = emit_transpose_f32(
            wv_, 8, lambda b, c=c: w1t_full[:, b * HID + c * 128: b * HID + c * 128 + 128],
            extra_t_waits=(("DT%d" % (c % 2), tok_ld),))
        tr_tok1[c % 2] = tr["lastT"]
    p.w("sync", "V", p.cnt["V"])
    p.w("sync", "DB", prev_stash)
    for r_ in range(8):
        prev_stash = p.emit("sync", lambda e, r_=r_: e.dma_start(
            st_w1[r_ * 128:(r_ + 1) * 128, :], w1t_full[:, r_ * HID:(r_ + 1) * HID]), inc="DB")
    A.release()

    q_tok2 = [("V", 0), ("V", 0)]
    A.mark()
    w2c = A.alloc(2 * HID, F32)
    w2q_f32 = A.alloc(HID, F32)
    w2t_full = A.alloc(32 * D, BF16)
    if QUANT == "ladder":
        scq = {k: A.alloc(HID // 2, F16) for k in
               ("ax", "sg", "l", "v", "dm", "e", "u", "v2", "r", "t1")}
        scq["a1"] = A.alloc(HID // 2, F32)
        scq["a"] = A.alloc(HID // 2, F32)
        scq["pp"] = A.alloc(HID // 2, F32)
        scq["pv"] = A.alloc(HID // 2, F32)
    else:
        w2q_f8 = A.alloc(HID // 2, F16).bitcast(F8)
    for c in range(8):
        off = (c % 2) * HID
        wv_ = w2c[:, off:off + HID]
        p.w("sync", q_tok2[c % 2][0], q_tok2[c % 2][1])
        tok_ld = p.emit("sync", lambda e, c=c, wv_=wv_: e.dma_start(
            wv_, w2h[c * 128:(c + 1) * 128, :]), inc="DT%d" % (c % 2))
        p.w("vector", "T", tr["lastT"])
        if QUANT == "ladder":
            for half in range(2):
                hw_ = HID // 2
                tq = emit_ladder(scq, wv_[:, half * hw_:(half + 1) * hw_],
                                 w2q_f32[:, half * hw_:(half + 1) * hw_], hw_,
                                 128, 3, -6.0, 3.99,
                                 (8, 4, 2, 1), (16, 8, 4, 2, 1), 240.0, True,
                                 ("DT%d" % (c % 2), tok_ld))
            dep = ("V", tq)
        else:
            p.w("scalar", "DT%d" % (c % 2), tok_ld)
            p.w("scalar", "T", tr["lastT"])
            p.emit("scalar", lambda e, wv_=wv_: e.copy(w2q_f8[:, :HID], wv_), inc="C")
            tq = p.emit("scalar", lambda e: e.copy(w2q_f32[:, :], w2q_f8[:, :HID]), inc="C")
            dep = ("C", tq)
        q_tok2[c % 2] = dep
        tv = emit_transpose_f32(
            w2q_f32[:, :], 32, lambda b, c=c: w2t_full[:, b * D + c * 128: b * D + c * 128 + 128],
            extra_t_waits=(dep,))
    p.w("sync", "V", p.cnt["V"])
    p.w("sync", "DB", prev_stash)
    for r_ in range(32):
        prev_stash = p.emit("sync", lambda e, r_=r_: e.dma_start(
            st_w2[r_ * 128:(r_ + 1) * 128, :], w2t_full[:, r_ * D:(r_ + 1) * D]), inc="DB")
    A.release()
    st["w_stash_done"] = prev_stash

    tok_bv = p.emit("sync", lambda e: e.dma_start(bscr[1:2, 0:D], bvh), inc="DS")

    def rope_fold_row(dep):
        p.w("vector", "DS", dep)
        sr = bscr[0:1, 0:D].rearrange("p (n two) -> p n two", two=2)
        dr = bscr[0:1, D:2 * D].rearrange("p (n two) -> p n two", two=2)
        p.emit("vector", lambda e: e.tensor_sub(
            dr[:, :, 0:1], sr[:, :, 0:1], sr[:, :, 1:2]), inc="V")
        return p.emit("vector", lambda e: e.tensor_add(
            dr[:, :, 1:2], sr[:, :, 1:2], sr[:, :, 0:1]), inc="V")

    tok_bq = p.emit("sync", lambda e: e.dma_start(bscr[0:1, 0:D], bqh), inc="DS")
    tqf = rope_fold_row(tok_bq)
    p.w("sync", "V", tqf)
    tbp0 = p.emit("sync", lambda e: e.dma_start(st_bp[0:1, :], bscr[0:1, D:2 * D]), inc="DS")
    p.w("sync", "DS", tbp0)
    tok_bk = p.emit("sync", lambda e: e.dma_start(bscr[0:1, 0:D], bkh), inc="DS")
    tqf = rope_fold_row(tok_bk)
    p.w("sync", "V", tqf)
    tbp1 = p.emit("sync", lambda e: e.dma_start(st_bp[1:2, :], bscr[0:1, D:2 * D]), inc="DS")
    p.w("sync", "DS", tbp1)
    p.emit("sync", lambda e: e.dma_start(
        bqr[:], st_bp[0:1, :].rearrange("one (c p) -> (one p) c", p=128),
        allow_slow_non_contiguous=True), inc="DS")
    st["bqr"] = p.emit("sync", lambda e: e.dma_start(
        bkr[:], st_bp[1:2, :].rearrange("one (c p) -> (one p) c", p=128),
        allow_slow_non_contiguous=True), inc="DS")
    st["b1L"] = p.emit("sync", lambda e: e.dma_start(
        b1L[:], b1h[0:1, :].rearrange("one (c p) -> (one p) c", p=128),
        allow_slow_non_contiguous=True), inc="DS")
    p.w("sync", "DS", tbp1)
    tok_b2 = p.emit("sync", lambda e: e.dma_start(bscr[0:1, 0:D], b2h), inc="DS")
    if QUANT == "ladder":
        A.mark()
        scb = {k: A.alloc(D, F16) for k in
               ("ax", "sg", "l", "v", "dm", "e", "u", "v2", "r", "t1")}
        scb["a1"] = A.alloc(D, F32)
        scb["a"] = A.alloc(D, F32)
        scb["pp"] = A.alloc(D, F32)
        scb["pv"] = A.alloc(D, F32)
        tok_b2q = emit_ladder(scb, bscr[0:1, 0:D], bscr[0:1, D:2 * D], D, 1, 10,
                              -14.0, 3.99, (16, 8, 4, 2, 1),
                              (2048, 1024, 512, 256, 128, 64, 32, 16, 8, 4, 2, 1),
                              65504.0, True, ("DS", tok_b2))
        A.release()
        sem_b2 = ("V", tok_b2q)
    else:
        p.w("scalar", "DS", tok_b2)
        b2f16 = b2qb[0:1, 0:512].bitcast(F16)
        p.emit("scalar", lambda e: e.copy(b2f16, bscr[0:1, 0:D]), inc="C")
        tok_b2q = p.emit("scalar", lambda e: e.copy(bscr[0:1, D:2 * D], b2f16), inc="C")
        sem_b2 = ("C", tok_b2q)

    def bcast_rows(dst, src_row, dep_sem, dep_tok):
        p.w("gpsimd", dep_sem, dep_tok)
        tok = p.emit("gpsimd", lambda e: e.dma_start(dst[0:1, :], src_row), inc="DG")
        n = 1
        while n < 128:
            p.w("gpsimd", "DG", tok)
            m = min(n, 128 - n)
            tok = p.emit("gpsimd", lambda e, n=n, m=m: e.dma_start(
                dst[n:n + m, :], dst[0:m, :]), inc="DG")
            n += m
        return tok

    st["b2qb"] = bcast_rows(b2qb, bscr[0:1, D:2 * D], sem_b2[0], sem_b2[1])
    st["bvb"] = bcast_rows(bvb, bscr[1:2, 0:D], "DS", tok_b2)

    def phase_barrier():
        for nm in ("T", "V", "C", "DG", "DB", "DW0", "DW1",
                   "DX0", "DX1", "DX2", "DX3", "DT0", "DT1"):
            p.w("sync", nm, p.cnt[nm])

    guard = {}
    last_out_store = 0
    x1_store = 0

    for seq in range(NSEQ):
        base = seq * S_LEN
        A.mark()
        wq_s = A.alloc(8 * D, BF16)
        wk_s = A.alloc(8 * D, BF16)
        q_fm = A.alloc(8 * S_LEN, BF16)
        k_fm = A.alloc(8 * S_LEN, BF16)
        v_tm = A.alloc(16 * D, BF16)
        xt = A.alloc(2 * D, F32)
        lnt = A.alloc(2 * D, F32)
        tpc = A.alloc(2 * 512, F32)
        etmp = A.alloc(2 * 512, F32)
        A.mark()
        ln1 = A.alloc(8 * S_LEN, BF16)

        tok_wq_ld = [None] * 8
        tok_wk_ld = [None] * 8
        phase_barrier()
        xt_busy = [0, 0]
        ln_ready = None
        ln_evac = []
        tp_busy = [0, 0]
        etmp_busy = [0, 0]
        qk_last_v = 0
        trstage = (psB[:, 1024:1536], psB[:, 1536:2048])

        def a2_strip(sc_):
            nonlocal qk_last_v
            for c in range(8):
                gi = sc_ * 8 + c
                tpp = gi % 2
                tview = tpc[:, tpp * 512:(tpp + 1) * 512]
                p.w("sync", "V", tp_busy[tpp])
                tok_t = p.emit("sync", lambda e, c=c, sc_=sc_, tview=tview: e.dma_start(
                    tview, tth[c * 128:(c + 1) * 128, sc_ * 512:(sc_ + 1) * 512]), inc="DT%d" % tpp)
                for ping, (wv_, bias_t, dst) in enumerate(
                        ((wq_s, bqr, q_fm), (wk_s, bkr, k_fm))):
                    dep2 = gi % 2
                    if ping == 0:
                        ps_sl = (psA0 if dep2 == 0 else psA1)[:, 0:512]
                    else:
                        ps_sl = psB[:, dep2 * 512:(dep2 + 1) * 512]
                    ev = etmp[:, ping * 512:(ping + 1) * 512]
                    p.w("tensor", "V", ln_evac[sc_ * 4 + 3])
                    p.w("tensor", "DB", tok_wk_ld[7])
                    p.w("tensor", "C", guard.get(("qk", ping, dep2), 0))
                    tmm = None
                    for kk in range(8):
                        tmm = p.emit("tensor", lambda e, wv_=wv_, c=c, sc_=sc_, kk=kk, ps_sl=ps_sl: e.matmul(
                            ps_sl,
                            wv_[:, kk * D + c * 128: kk * D + c * 128 + 128],
                            ln1[:, kk * S_LEN + sc_ * 512: kk * S_LEN + (sc_ + 1) * 512],
                            start=(kk == 0), stop=(kk == 7)),
                            inc="T" if kk == 7 else None)
                    p.w("scalar", "T", tmm)
                    p.w("scalar", "V", etmp_busy[ping])
                    p.w("scalar", "DS", st["bqr"])
                    tev = p.emit("scalar", lambda e, ps_sl=ps_sl, ev=ev, bias_t=bias_t, c=c: e.activation(
                        ev, ps_sl, AF.Identity, bias=bias_t[:, c:c + 1]), inc="C")
                    guard[("qk", ping, dep2)] = tev
                    p.w("vector", "C", tev)
                    p.w("vector", "DT%d" % tpp, tok_t)
                    qk_last_v = p.emit("vector", lambda e, dst=dst, c=c, sc_=sc_, ev=ev, tview=tview: e.tensor_mul(
                        dst[:, c * S_LEN + sc_ * 512: c * S_LEN + (sc_ + 1) * 512],
                        ev, tview), inc="V")
                    etmp_busy[ping] = qk_last_v
                tp_busy[tpp] = qk_last_v

        for i in range(16):
            pi = i % 2
            xv = xt[:, pi * D:pi * D + D]
            p.w("sync", "V", xt_busy[pi])
            tok_x = p.emit("sync", lambda e, i=i, xv=xv, base=base: e.dma_start(
                xv, xh[base + i * 128: base + (i + 1) * 128, :]), inc="DX%d" % pi)
            if i == 1:
                p.w("sync", "DB", st["w_stash_done"])
                for c in range(8):
                    tok_wq_ld[c] = p.emit("sync", lambda e, c=c: e.dma_start(
                        wq_s[:, c * D:(c + 1) * D], st_wq[c * 128:(c + 1) * 128, :]), inc="DB")
                for c in range(8):
                    tok_wk_ld[c] = p.emit("sync", lambda e, c=c: e.dma_start(
                        wk_s[:, c * D:(c + 1) * D], st_wk[c * 128:(c + 1) * 128, :]), inc="DB")
                if seq == 0:
                    dbg_tap("wq_s", [128, 8 * D], wq_s[:, :], "DB", tok_wq_ld[7], BF16)
            p.w("vector", "DX%d" % pi, tok_x)
            p.w("vector", "T", guard.get(("lnt", i % 2), 0))
            sm = (i % 4) * 8
            lv_ = lnt[:, (i % 2) * D:(i % 2) * D + D]
            p.emit("vector", lambda e, xv=xv, lv_=lv_: e.tensor_mul(
                lv_, xv, xv), inc="V")
            p.emit("vector", lambda e, xv=xv, sm=sm: e.reduce_sum(
                stats[:, sm:sm + 1], xv, AX.X), inc="V")
            tvv = p.emit("vector", lambda e, sm=sm, lv_=lv_: e.reduce_sum(
                stats[:, sm + 1:sm + 2], lv_, AX.X), inc="V")
            p.w("scalar", "V", tvv)
            p.emit("scalar", lambda e, sm=sm: e.activation(
                stats[:, sm + 6:sm + 7], stats[:, sm + 1:sm + 2], AF.Identity,
                scale=f32(1.0 / D), bias=f32(1e-5)), inc="C")
            tcs = p.emit("scalar", lambda e, sm=sm: e.mul(
                stats[:, sm + 2:sm + 3], stats[:, sm:sm + 1], f32(1.0 / D)), inc="C")
            p.w("vector", "C", tcs)
            tvq = p.emit("vector", lambda e, sm=sm: e.tensor_mul(
                stats[:, sm + 3:sm + 4], stats[:, sm + 2:sm + 3], stats[:, sm + 2:sm + 3]), inc="V")
            p.w("scalar", "V", tvq)
            tcs = p.emit("scalar", lambda e, sm=sm: e.activation(
                stats[:, sm + 5:sm + 6], stats[:, sm + 3:sm + 4], AF.Sqrt,
                scale=f32(-1.0), bias=stats[:, sm + 6:sm + 7]), inc="C")
            p.w("vector", "C", tcs)
            p.emit("vector", lambda e, sm=sm: e.reciprocal(
                stats[:, sm + 4:sm + 5], stats[:, sm + 5:sm + 6]), inc="V")
            p.emit("vector", lambda e, xv=xv, sm=sm, lv_=lv_: e.tensor_single_scalar(
                lv_, xv, stats[:, sm + 2:sm + 3], OP.subtract), inc="V")
            tvn = p.emit("vector", lambda e, sm=sm, lv_=lv_: e.tensor_single_scalar(
                lv_, lv_, stats[:, sm + 4:sm + 5], OP.mult), inc="V")
            xt_busy[pi] = tvn
            if seq == 0 and i == 0:
                dbg_tap("xv0", [128, D], xv, "V", tvn)
                dbg_tap("lnt0", [128, D], lv_, "V", tvn)
                dbg_tap("stats0", [128, 32], stats[:], "V", tvn)
            ln_ready = emit_transpose_f32(
                lv_, 8,
                lambda b, i=i: ln1[:, b * S_LEN + i * 128: b * S_LEN + i * 128 + 128],
                stage=trstage)
            guard[("lnt", i % 2)] = tr["lastT"]
            ln_evac.append(ln_ready)
            if i % 4 == 3:
                a2_strip(i // 4)

        if seq == 0:
            dbg_tap("ln1c0", [128, S_LEN], ln1[:, 0:S_LEN], "V", ln_ready, BF16)
        if seq == 0:
            dbg_tap("qc0", [128, S_LEN], q_fm[:, 0:S_LEN], "V", qk_last_v, BF16)
            dbg_tap("kc0", [128, S_LEN], k_fm[:, 0:S_LEN], "V", qk_last_v, BF16)
        wv_s = wq_s
        tok_wv_ld = [None] * 8
        p.w("sync", "V", qk_last_v)
        for c in range(8):
            tok_wv_ld[c] = p.emit("sync", lambda e, c=c: e.dma_start(
                wv_s[:, c * D:(c + 1) * D], st_wv[c * 128:(c + 1) * 128, :]), inc="DB")
        if seq == 0:
            dbg_tap("wv_s", [128, 8 * D], wv_s[:, :], "DB", tok_wv_ld[7], BF16)
        v_last = 0
        for i in range(16):
            for n in range(2):
                ping = (i * 2 + n) % 2
                ps_sl = psB[:, ping * 512:(ping + 1) * 512]
                if i == 0:
                    p.w("tensor", "C", p.cnt["C"])
                p.w("tensor", "V", guard.get(("v", ping), 0))
                p.w("tensor", "DB", tok_wv_ld[7])
                tmm = None
                for kk in range(8):
                    tmm = p.emit("tensor", lambda e, i=i, n=n, kk=kk, ps_sl=ps_sl: e.matmul(
                        ps_sl,
                        ln1[:, kk * S_LEN + i * 128: kk * S_LEN + i * 128 + 128],
                        wv_s[:, kk * D + n * 512: kk * D + (n + 1) * 512],
                        start=(kk == 0), stop=(kk == 7)),
                        inc="T" if kk == 7 else None)
                p.w("vector", "T", tmm)
                v_last = p.emit("vector", lambda e, i=i, n=n, ps_sl=ps_sl: e.tensor_copy(
                    v_tm[:, i * D + n * 512: i * D + (n + 1) * 512], ps_sl), inc="V")
                guard[("v", ping)] = v_last

        if seq == 0:
            dbg_tap("vt0", [128, D], v_tm[:, 0:D], "V", v_last, BF16)
        A.release()
        probs = A.alloc(S_LEN, F32)
        probsT = A.alloc(S_LEN, BF16)
        x1t = A.alloc(2 * D, F32)
        xbv = A.alloc(D, F32)

        x1t_busy = [0, 0]
        prev_evac = 0
        prev_exp = 0
        for i in range(16):
            sm = 16 + (i % 2) * 4
            p.w("tensor", "V", max(v_last, prev_evac))
            p.w("tensor", "C", prev_exp)
            tmm = None
            for kk in range(8):
                for tc_ in range(4):
                    tmm = p.emit("tensor", lambda e, i=i, kk=kk, tc_=tc_: e.matmul(
                        psB[:, tc_ * 512:(tc_ + 1) * 512],
                        q_fm[:, kk * S_LEN + i * 128: kk * S_LEN + i * 128 + 128],
                        k_fm[:, kk * S_LEN + tc_ * 512: kk * S_LEN + (tc_ + 1) * 512],
                        start=(kk == 0), stop=(kk == 7)),
                        inc="T" if kk == 7 and tc_ == 3 else None)
            p.w("vector", "T", tmm)
            tnm2 = p.emit("vector", lambda e, sm=sm: e.reduce_max(
                stats[:, sm:sm + 1], psB[:, 0:2048], AX.X, negate=True), inc="V")
            p.w("scalar", "V", tnm2)
            prev_exp = p.emit("scalar", lambda e, sm=sm: e.activation(
                probs[:, :], psB[:, 0:2048], AF.Exp, bias=stats[:, sm:sm + 1],
                scale=f32(1.0)), inc="C")
            p.w("vector", "C", prev_exp)
            trs = p.emit("vector", lambda e, sm=sm: e.reduce_sum(
                stats[:, sm + 2:sm + 3], probs[:, :], AX.X), inc="V")
            for g in range(4):
                p.w("tensor", "C", prev_exp)
                p.w("tensor", "V", prev_evac)
                p.w("tensor", "G", st["ident"])
                tt = None
                for b in range(4):
                    blk = g * 4 + b
                    tt = p.emit("tensor", lambda e, blk=blk, b=b: e.transpose(
                        psB[:, b * 128:(b + 1) * 128],
                        probs[:, blk * 128:(blk + 1) * 128], ident[:]),
                        inc="T" if b == 3 else None)
                tr["lastT"] = tt
                p.w("vector", "T", tt)
                if g == 2:
                    trz = p.emit("vector", lambda e, sm=sm: e.reciprocal(
                        stats[:, sm + 3:sm + 4], stats[:, sm + 2:sm + 3]), inc="V")
                prev_evac = p.emit("vector", lambda e, g=g: e.tensor_copy(
                    probsT[:, g * 512:(g + 1) * 512], psB[:, 0:512]), inc="V")
            if seq == 0 and i == 0:
                dbg_tap("probs0", [128, S_LEN], probs[:, :], "V", prev_evac)
            p.w("tensor", "V", max(prev_evac, guard.get("pv", 0)))
            tpv = None
            for tk in range(16):
                for n in range(2):
                    tpv = p.emit("tensor", lambda e, tk=tk, n=n: e.matmul(
                        psPV[:, n * 512:(n + 1) * 512],
                        probsT[:, tk * 128:(tk + 1) * 128],
                        v_tm[:, tk * D + n * 512: tk * D + (n + 1) * 512],
                        start=(tk == 0), stop=(tk == 15)),
                        inc="T" if tk == 15 and n == 1 else None)
            pi = i % 2
            xv = xt[:, pi * D:pi * D + D]
            p.w("sync", "V", xt_busy[pi])
            tok_x = p.emit("sync", lambda e, i=i, xv=xv, base=base: e.dma_start(
                xv, xh[base + i * 128: base + (i + 1) * 128, :]), inc="DX%d" % pi)
            p.w("vector", "DX%d" % pi, tok_x)
            p.w("vector", "DG", st["bvb"])
            tb = p.emit("vector", lambda e, xv=xv: e.tensor_add(xbv[:, :], xv, bvb[:]), inc="V")
            xt_busy[pi] = tb
            p.w("vector", "DG", x1t_busy[pi])
            p.w("vector", "T", tpv)
            p.w("vector", "V", trz)
            x1v = x1t[:, pi * D:pi * D + D]
            tx1 = p.emit("vector", lambda e, x1v=x1v, sm=sm: e.scalar_tensor_tensor(
                x1v, psPV[:, 0:1024], stats[:, sm + 3:sm + 4], xbv[:, :],
                OP.mult, OP.add), inc="V")
            guard["pv"] = tx1
            p.w("gpsimd", "V", tx1)
            x1_store = p.emit("gpsimd", lambda e, i=i, x1v=x1v, base=base: e.dma_start(
                x1h[base + i * 128: base + (i + 1) * 128, :], x1v), inc="DG")
            x1t_busy[pi] = x1_store
            if seq == 0 and i == 0:
                dbg_tap("x1t0", [128, D], x1v, "V", guard["pv"])

        A.release()

        A.mark()
        w1t_s = A.alloc(8 * HID, BF16)
        hq = A.alloc(32 * 512, BF16)
        ln2s = [A.alloc(8 * 512, BF16), A.alloc(8 * 512, BF16)]
        w2t_win = A.alloc(4 * D, BF16)
        xt2 = A.alloc(2 * D, F32)
        xt3 = A.alloc(2 * D, F32)
        lnt2 = A.alloc(4 * D, F32)
        y0 = A.alloc(2 * D, F32)
        outt = A.alloc(2 * D, F32)
        if QUANT == "ladder":
            scl = {k: A.alloc(D, F16) for k in
                   ("ax", "sg", "l", "v", "dm", "e", "u", "v2", "r", "t1")}
            scl["a1"] = A.alloc(D, F32)
            scl["a"] = A.alloc(D, F32)
            scl["pp"] = A.alloc(D, F32)
            scl["pv"] = A.alloc(D, F32)
            hsrc = A.alloc(2 * 512, F32)
            yq = A.alloc(D, F32)
        else:
            h_f8 = A.alloc(512 // 2, F16).bitcast(F8)
            y16 = A.alloc(D, F16)

        tok_w1_ld = [None] * 8
        phase_barrier()

        xt2_busy = [0, 0]
        xt3_busy = [0, 0]
        out_busy = [0, 0]
        ln2_ready_ch = {}

        def b1_chain(ch):
            crow = base + ch * 512
            for ii in range(4):
                pi = ii % 2
                xv = xt2[:, pi * D:pi * D + D]
                p.w("sync", "V", xt2_busy[pi])
                p.w("sync", "DG", x1_store)
                tok_x = p.emit("sync", lambda e, ii=ii, xv=xv, crow=crow: e.dma_start(
                    xv, x1h[crow + ii * 128: crow + (ii + 1) * 128, :]), inc="DX%d" % (2 + pi))
                p.w("vector", "DX%d" % (2 + pi), tok_x)
                p.w("vector", "T", guard.get(("lnt2", ii), 0))
                sm = ii * 8
                lv_ = lnt2[:, ii * D:ii * D + D]
                p.emit("vector", lambda e, xv=xv, lv_=lv_: e.tensor_mul(
                    lv_, xv, xv), inc="V")
                p.emit("vector", lambda e, xv=xv, sm=sm: e.reduce_sum(
                    stats[:, sm:sm + 1], xv, AX.X), inc="V")
                tvv = p.emit("vector", lambda e, sm=sm, lv_=lv_: e.reduce_sum(
                    stats[:, sm + 1:sm + 2], lv_, AX.X), inc="V")
                p.w("scalar", "V", tvv)
                p.emit("scalar", lambda e, sm=sm: e.activation(
                    stats[:, sm + 6:sm + 7], stats[:, sm + 1:sm + 2], AF.Identity,
                    scale=f32(1.0 / D), bias=f32(1e-5)), inc="C")
                tcs = p.emit("scalar", lambda e, sm=sm: e.mul(
                    stats[:, sm + 2:sm + 3], stats[:, sm:sm + 1], f32(1.0 / D)), inc="C")
                p.w("vector", "C", tcs)
                tvq = p.emit("vector", lambda e, sm=sm: e.tensor_mul(
                    stats[:, sm + 3:sm + 4], stats[:, sm + 2:sm + 3], stats[:, sm + 2:sm + 3]), inc="V")
                p.w("scalar", "V", tvq)
                tcs = p.emit("scalar", lambda e, sm=sm: e.activation(
                    stats[:, sm + 5:sm + 6], stats[:, sm + 3:sm + 4], AF.Sqrt,
                    scale=f32(-1.0), bias=stats[:, sm + 6:sm + 7]), inc="C")
                p.w("vector", "C", tcs)
                p.emit("vector", lambda e, sm=sm: e.reciprocal(
                    stats[:, sm + 4:sm + 5], stats[:, sm + 5:sm + 6]), inc="V")
                p.emit("vector", lambda e, xv=xv, sm=sm, lv_=lv_: e.tensor_single_scalar(
                    lv_, xv, stats[:, sm + 2:sm + 3], OP.subtract), inc="V")
                tvn = p.emit("vector", lambda e, sm=sm, lv_=lv_: e.tensor_single_scalar(
                    lv_, lv_, stats[:, sm + 4:sm + 5], OP.mult), inc="V")
                xt2_busy[pi] = tvn

        def b1_tr(ch):
            lnb = ln2s[ch % 2]
            last = None
            for ii in range(4):
                lv_ = lnt2[:, ii * D:ii * D + D]
                last = emit_transpose_f32(
                    lv_, 8,
                    lambda b, ii=ii, lnb=lnb: lnb[:, b * 512 + ii * 128: b * 512 + ii * 128 + 128])
                guard[("lnt2", ii)] = tr["lastT"]
            ln2_ready_ch[ch] = last

        b1_chain(0)
        p.w("sync", "DB", st["w_stash_done"])
        for c in range(8):
            tok_w1_ld[c] = p.emit("sync", lambda e, c=c: e.dma_start(
                w1t_s[:, c * HID:(c + 1) * HID], st_w1[c * 128:(c + 1) * 128, :]), inc="DB")
        b1_tr(0)
        for ch in range(4):
            crow = base + ch * 512
            lnb = ln2s[ch % 2]
            ln2_ready = ln2_ready_ch[ch]
            if seq == 0 and ch == 0:
                dbg_tap("ln2c0", [128, 512], lnb[:, 0:512], "V", ln2_ready, BF16)
            hq_ready_v = 0
            last_relu_c = 0
            hq_busy = [0, 0]
            for jt in range(32):
                ping = jt % 2
                ps = psA0 if ping == 0 else psA1
                p.w("tensor", "C", guard.get(("f1", ping), 0))
                p.w("tensor", "V", ln2_ready)
                p.w("tensor", "DB", tok_w1_ld[7])
                tmm = None
                for kk in range(8):
                    tmm = p.emit("tensor", lambda e, jt=jt, kk=kk, ps=ps, lnb=lnb: e.matmul(
                        ps[:, 0:512],
                        w1t_s[:, kk * HID + jt * 128: kk * HID + jt * 128 + 128],
                        lnb[:, kk * 512:(kk + 1) * 512],
                        start=(kk == 0), stop=(kk == 7)),
                        inc="T" if kk == 7 else None)
                p.w("scalar", "T", tmm)
                p.w("scalar", "DS", st["b1L"])
                if QUANT == "ladder":
                    hv = hsrc[:, ping * 512:(ping + 1) * 512]
                    p.w("scalar", "V", hq_busy[ping])
                    trl = p.emit("scalar", lambda e, ps=ps, hv=hv, jt=jt: e.activation(
                        hv, ps[:, 0:512], AF.Relu, bias=b1L[:, jt:jt + 1]), inc="C")
                    guard[("f1", ping)] = trl
                    last_relu_c = trl
                    hq_ready_v = emit_ladder(
                        scl, hv, hq[:, jt * 512:(jt + 1) * 512], 512, 128, 3,
                        -6.0, 7.99, (8, 4, 2, 1), (16, 8, 4, 2, 1), 240.0,
                        False, ("C", trl))
                    hq_busy[ping] = hq_ready_v
                else:
                    p.emit("scalar", lambda e, ps=ps, jt=jt: e.activation(
                        h_f8[:, 0:512], ps[:, 0:512], AF.Relu, bias=b1L[:, jt:jt + 1]), inc="C")
                    trl = p.emit("scalar", lambda e, jt=jt: e.copy(
                        hq[:, jt * 512:(jt + 1) * 512], h_f8[:, 0:512]), inc="C")
                    guard[("f1", ping)] = trl
                    last_relu_c = trl

            if seq == 0 and ch == 0:
                if QUANT == "ladder":
                    dbg_tap("hq0", [128, 512], hq[:, 0:512], "V", hq_ready_v, BF16)
                else:
                    dbg_tap("hq0", [128, 512], hq[:, 0:512], "C", last_relu_c, BF16)
            if ch < 3:
                b1_chain(ch + 1)
            yps = [psB[:, 0:1024], psB[:, 1024:2048], psPV[:, 0:1024]]
            if seq == 0 and ch == 0:
                dbg_tap("hqall", [128, 32 * 512], hq[:, :], "C", last_relu_c, BF16)
                dbg_tap("b2qbt", [128, D], b2qb[:], "DG", st["b2qb"])
            w2_busy = [0, guard.get("b3_prev_t", 0)]
            w2_tok = [0, 0]
            tff = None
            for jkp in range(17):
                if jkp < 16:
                    wp = jkp % 2
                    p.w("sync", "T", w2_busy[wp] if jkp > 0 else guard.get("b3_prev_t", 0))
                    for tl in range(2):
                        w2_tok[wp] = p.emit("sync", lambda e, jkp=jkp, wp=wp, tl=tl: e.dma_start(
                            w2t_win[:, (wp * 2 + tl) * D:(wp * 2 + tl + 1) * D],
                            st_w2[jkp * 256 + tl * 128: jkp * 256 + (tl + 1) * 128, :]),
                            inc="DW%d" % wp)
                if jkp < 1:
                    continue
                for jk in range((jkp - 1) * 2, jkp * 2):
                    p.w("tensor", "DW%d" % ((jk // 2) % 2), w2_tok[(jk // 2) % 2])
                    wwin = w2t_win[:, (jk % 4) * D:((jk % 4) + 1) * D]
                    if jk == 0:
                        p.w("tensor", "V", max(hq_ready_v, guard.get("ffn2", 0)))
                        p.w("tensor", "C", last_relu_c)
                    for ii in range(4):
                        for n in range(2):
                            if ii < 3:
                                dst = yps[ii][:, n * 512:(n + 1) * 512]
                            else:
                                dst = (psA0 if n == 0 else psA1)[:, 0:512]
                            tff = p.emit("tensor", lambda e, jk=jk, dst=dst, ii=ii, n=n, wwin=wwin: e.matmul(
                                dst,
                                hq[:, jk * 512 + ii * 128: jk * 512 + ii * 128 + 128],
                                wwin[:, n * 512:(n + 1) * 512],
                                start=(jk == 0), stop=(jk == 31)),
                                inc="T" if ii == 3 and n == 1 else None)
                    w2_busy[(jk // 2) % 2] = tff
            guard["b3_prev_t"] = tff
            guard[("f1", 0)] = 0
            guard[("f1", 1)] = 0
            yv3 = y0[:, D:2 * D]
            p.w("vector", "T", tff)
            p.w("vector", "DG", st["b2qb"])
            p.w("vector", "V", guard.get(("y0", 1), 0))
            p.emit("vector", lambda e, yv3=yv3: e.tensor_add(
                yv3[:, 0:512], psA0[:, 0:512], b2qb[:, 0:512]), inc="V")
            ty0_3 = p.emit("vector", lambda e, yv3=yv3: e.tensor_add(
                yv3[:, 512:1024], psA1[:, 0:512], b2qb[:, 512:1024]), inc="V")
            if ch < 3:
                b1_tr(ch + 1)
            y_ep = 0
            for ii in (3, 0, 1, 2):
                pi = ii % 2
                xv = xt3[:, pi * D:pi * D + D]
                p.w("sync", "V", xt3_busy[pi])
                tok_x = p.emit("sync", lambda e, ii=ii, xv=xv, crow=crow: e.dma_start(
                    xv, x1h[crow + ii * 128: crow + (ii + 1) * 128, :]), inc="DX%d" % pi)
                yv = y0[:, pi * D:pi * D + D]
                if ii == 3:
                    ty0 = ty0_3
                else:
                    p.w("vector", "T", tff)
                    p.w("vector", "DG", st["b2qb"])
                    p.w("vector", "V", guard.get(("y0", pi), 0))
                    ty0 = p.emit("vector", lambda e, ii=ii, yv=yv: e.tensor_add(
                        yv, yps[ii], b2qb[:]), inc="V")
                ov = outt[:, pi * D:pi * D + D]
                if seq == 0 and ch == 0 and ii == 0:
                    dbg_tap("y00", [128, D], yv, "V", ty0)
                p.w("vector", "DG", out_busy[pi])
                if QUANT == "ladder":
                    emit_ladder(scl, yv, yq[:, 0:D], D, 128, 10, -14.0, 3.99,
                                (16, 8, 4, 2, 1),
                                (2048, 1024, 512, 256, 128, 64, 32, 16, 8, 4, 2, 1),
                                65504.0, True, ("V", ty0))
                    p.w("vector", "DX%d" % pi, tok_x)
                    tout = p.emit("vector", lambda e, ov=ov, xv=xv: e.tensor_add(
                        ov, yq[:, 0:D], xv), inc="V")
                else:
                    p.w("scalar", "V", ty0)
                    tqc = p.emit("scalar", lambda e, yv=yv: e.copy(y16[:, 0:D], yv), inc="C")
                    p.w("vector", "C", tqc)
                    p.w("vector", "DX%d" % pi, tok_x)
                    tout = p.emit("vector", lambda e, ov=ov, xv=xv: e.tensor_add(
                        ov, y16[:, 0:D], xv), inc="V")
                guard[("y0", pi)] = tout
                xt3_busy[pi] = tout
                y_ep = tout
                p.w("gpsimd", "V", tout)
                last_out_store = p.emit("gpsimd", lambda e, ii=ii, ov=ov, crow=crow: e.dma_start(
                    outh[crow + ii * 128: crow + (ii + 1) * 128, :], ov), inc="DG")
                out_busy[pi] = last_out_store
            guard["ffn2"] = y_ep

        A.release()

    p.w("gpsimd", "DG", last_out_store)

    with nc.Block() as block:
        @block.sync
        def _(e):
            for fn in p.stream["sync"]:
                fn(e)

        @block.vector
        def _(e):
            for fn in p.stream["vector"]:
                fn(e)

        @block.scalar
        def _(e):
            for fn in p.stream["scalar"]:
                fn(e)

        @block.gpsimd
        def _(e):
            for fn in p.stream["gpsimd"]:
                fn(e)

        @block.tensor
        def _(e):
            for fn in p.stream["tensor"]:
                fn(e)

    return nc


def kernel(L0_attn_hidden_states, L0_attn_Wq, L0_attn_bq, L0_attn_Wk, L0_attn_bk,
           L0_attn_Wv, L0_attn_bv, L0_ffn_w1, L0_ffn_b1, L0_ffn_w2, L0_ffn_b2):
    x = np.ascontiguousarray(np.asarray(L0_attn_hidden_states, dtype=np.float32))
    tt = rope_table_T()
    common = {
        "wq": np.ascontiguousarray(np.asarray(L0_attn_Wq, np.float32)),
        "wk": np.ascontiguousarray(np.asarray(L0_attn_Wk, np.float32)),
        "wv": np.ascontiguousarray(np.asarray(L0_attn_Wv, np.float32)),
        "bq": np.asarray(L0_attn_bq, np.float32).reshape(1, D),
        "bk": np.asarray(L0_attn_bk, np.float32).reshape(1, D),
        "bv": np.asarray(L0_attn_bv, np.float32).reshape(1, D),
        "w1": np.ascontiguousarray(np.asarray(L0_ffn_w1, np.float32)),
        "b1": np.asarray(L0_ffn_b1, np.float32).reshape(1, HID),
        "w2": np.ascontiguousarray(np.asarray(L0_ffn_w2, np.float32)),
        "b2": np.asarray(L0_ffn_b2, np.float32).reshape(1, D),
        "tt": tt,
    }
    in_maps = []
    for c in range(8):
        m = dict(common)
        m["x"] = np.ascontiguousarray(
            x[c * NSEQ:(c + 1) * NSEQ].reshape(NSEQ * S_LEN, D))
        in_maps.append(m)
    nc = build_core_program()
    res = run_bass_kernel_spmd(nc, in_maps, core_ids=list(range(8)))
    if DEBUG:
        global _dbg
        _dbg = {k: np.asarray(res.results[0]["dbg_" + k]) for k in _dbg_names}
        global _dbg2
        _dbg2 = {k: np.asarray(res.results[1]["dbg_" + k]) for k in _dbg_names}
    out = np.concatenate(
        [np.asarray(r["out"]).reshape(NSEQ, S_LEN, D) for r in res.results], axis=0)
    return np.ascontiguousarray(out.astype(np.float32))


# version 18
# speedup vs baseline: 1.1477x; 1.0759x over previous
import numpy as np
import concourse.bass as bass
import concourse.mybir as mybir
from concourse.alu_op_type import AluOpType as OP
from concourse.bass_utils import run_bass_kernel_spmd

F32 = mybir.dt.float32
BF16 = mybir.dt.bfloat16
F16 = mybir.dt.float16
F8 = mybir.dt.float8e4
AF = mybir.ActivationFunctionType
AX = mybir.AxisListType
f32 = np.float32

QUANT = "cast"
DEBUG = False
_dbg_names = []

S_LEN = 2048
D = 1024
HID = 4096
NSEQ = 2
ROWS = NSEQ * S_LEN
INV_SQRT_HS = f32(1.0 / float(np.sqrt(np.float64(512.0))))

LN2_ = np.float64(0.6931471805599453)
INVLN2 = f32(1.0 / LN2_)
LN2HI = f32(LN2_)
LN2LO = f32(LN2_ - np.float64(f32(LN2_)))
B3HI = f32(3.0 * LN2_)
B3LO = f32(3.0 * LN2_ - np.float64(f32(3.0 * LN2_)))
NB3LO = f32(-np.float64(f32(3.0 * LN2_ - np.float64(f32(3.0 * LN2_)))))
NB10 = f32(-(10.0 * LN2_))

ENGS = ("sync", "vector", "scalar", "gpsimd", "tensor")


class Prog:
    def __init__(self, nc):
        self.nc = nc
        self.stream = {e: [] for e in ENGS}
        self.sems = {}
        self.cnt = {}
        self.last_wait = {}
        for name in ("T", "V", "C", "G", "DS", "DG", "DB", "DW0", "DW1",
                     "DX0", "DX1", "DX2", "DX3", "DT0", "DT1"):
            self.sems[name] = nc.alloc_semaphore("sem_" + name)
            self.cnt[name] = 0

    def w(self, eng, sem, tok):
        if tok is None or tok <= 0:
            return
        if self.last_wait.get((eng, sem), 0) >= tok:
            return
        self.last_wait[(eng, sem)] = tok
        s = self.sems[sem]
        self.stream[eng].append(lambda e, s=s, t=tok: e.wait_ge(s, t))

    def emit(self, eng, fn, inc=None):
        if inc is None:
            self.stream[eng].append(fn)
            return None
        s = self.sems[inc]
        step = 16 if inc.startswith("D") else 1
        self.cnt[inc] += step
        tok = self.cnt[inc]
        self.stream[eng].append(lambda e, fn=fn, s=s, st=step: fn(e).then_inc(s, st))
        return tok


class Arena:
    def __init__(self, t, nels_bf16):
        self.t = t
        self.nb = nels_bf16 * 2
        self.off = 0
        self.marks = []
        self.peak = 0

    def alloc(self, els, dtype):
        unit = {F32: 4, BF16: 2, F16: 2}[dtype]
        off = (self.off + 3) & ~3
        sz = els * unit
        assert off + sz <= self.nb, f"arena overflow: {off}+{sz} > {self.nb}"
        self.off = off + sz
        self.peak = max(self.peak, self.off)
        v = self.t[:, off // 2:(off + sz) // 2]
        if dtype != BF16:
            v = v.bitcast(dtype)
        return v

    def mark(self):
        self.marks.append(self.off)

    def release(self):
        self.off = self.marks.pop()


def rope_table_T():
    pos = np.arange(S_LEN, dtype=np.float64)
    inv = 10000.0 ** (np.arange(0, 512, 2, dtype=np.float64) / 512.0)
    radians = pos[:, None] / inv[None, :]
    sincos = np.stack((np.sin(radians), np.cos(radians)), axis=-1).reshape(S_LEN, -1)
    t = np.repeat(sincos, 2, axis=-1) * (512.0 ** -0.25)
    return np.ascontiguousarray(t.T.astype(np.float32))


def build_core_program():
    nc = bass.Bass("TRN2", target_bir_lowering=False)
    xh = nc.dram_tensor("x", [ROWS, D], F32, kind="ExternalInput").ap()
    wqh = nc.dram_tensor("wq", [D, D], F32, kind="ExternalInput").ap()
    wkh = nc.dram_tensor("wk", [D, D], F32, kind="ExternalInput").ap()
    wvh = nc.dram_tensor("wv", [D, D], F32, kind="ExternalInput").ap()
    bqh = nc.dram_tensor("bq", [1, D], F32, kind="ExternalInput").ap()
    bkh = nc.dram_tensor("bk", [1, D], F32, kind="ExternalInput").ap()
    bvh = nc.dram_tensor("bv", [1, D], F32, kind="ExternalInput").ap()
    w1h = nc.dram_tensor("w1", [HID, D], F32, kind="ExternalInput").ap()
    b1h = nc.dram_tensor("b1", [1, HID], F32, kind="ExternalInput").ap()
    w2h = nc.dram_tensor("w2", [D, HID], F32, kind="ExternalInput").ap()
    b2h = nc.dram_tensor("b2", [1, D], F32, kind="ExternalInput").ap()
    tth = nc.dram_tensor("tt", [D, S_LEN], F32, kind="ExternalInput").ap()
    outh = nc.dram_tensor("out", [ROWS, D], F32, kind="ExternalOutput").ap()
    x1h = nc.dram_tensor("x1s", [ROWS, D], F32, kind="Internal").ap()
    st_wq = nc.dram_tensor("st_wq", [D, D], BF16, kind="Internal").ap()
    st_wk = nc.dram_tensor("st_wk", [D, D], BF16, kind="Internal").ap()
    st_wv = nc.dram_tensor("st_wv", [D, D], BF16, kind="Internal").ap()
    st_w1 = nc.dram_tensor("st_w1", [D, HID], BF16, kind="Internal").ap()
    st_w2 = nc.dram_tensor("st_w2", [HID, D], BF16, kind="Internal").ap()
    st_bp = nc.dram_tensor("st_bp", [2, D], F32, kind="Internal").ap()
    dbg = {}
    def dbg_out(name, shape, dtype=F32):
        global _dbg_names
        if name not in _dbg_names:
            _dbg_names.append(name)
        dbg[name] = nc.dram_tensor("dbg_" + name, shape, dtype, kind="ExternalOutput").ap()
        return dbg[name]
    def dbg_tap(name, shape, src_ap, sem, tok, dtype=F32):
        if not DEBUG:
            return
        d = dbg_out(name, shape, dtype)
        p.w("gpsimd", sem, tok)
        tk = p.emit("gpsimd", lambda e, d=d, src_ap=src_ap: e.dma_start(d, src_ap), inc="DG")
        for eng in ("sync", "vector", "scalar", "tensor"):
            p.w(eng, "DG", tk)

    ident = nc.alloc_sbuf_tensor("ident", [128, 128], F32)
    ident_bf = nc.alloc_sbuf_tensor("ident_bf", [128, 128], BF16)
    bqr = nc.alloc_sbuf_tensor("bqr", [128, 8], F32)
    bkr = nc.alloc_sbuf_tensor("bkr", [128, 8], F32)
    b1L = nc.alloc_sbuf_tensor("b1L", [128, 32], F32)
    b2qb = nc.alloc_sbuf_tensor("b2qb", [128, D], F32)
    bvb = nc.alloc_sbuf_tensor("bvb", [128, D], F32)
    stats = nc.alloc_sbuf_tensor("stats", [128, 32], F32)
    bscr = nc.alloc_sbuf_tensor("bscr", [128, 2 * D], F32)

    NELS = 94208
    AB = nc.alloc_sbuf_tensor("AB", [128, NELS], BF16)

    psA0 = nc.alloc_psum_tensor("psA0", [128, 512], F32)
    psA1 = nc.alloc_psum_tensor("psA1", [128, 512], F32)
    psB = nc.alloc_psum_tensor("psB", [128, 2048], F32)
    psPV = nc.alloc_psum_tensor("psPV", [128, 1024], F32)

    p = Prog(nc)
    A = Arena(AB, NELS)
    st = {}

    def emit_ladder(sc, src_ap, out_ap, W, npart, man, e_lo, e_hi, e_levels,
                    r_levels, maxval, signed, dep):
        ax, sg, l, v, dm, e, u, v2, r, pp, t1 = (
            sc["ax"][:npart, :W], sc["sg"][:npart, :W], sc["l"][:npart, :W],
            sc["v"][:npart, :W], sc["dm"][:npart, :W], sc["e"][:npart, :W],
            sc["u"][:npart, :W], sc["v2"][:npart, :W], sc["r"][:npart, :W],
            sc["pp"][:npart, :W], sc["t1"][:npart, :W])
        a1, a = sc["a1"][:npart, :W], sc["a"][:npart, :W]
        pv = sc["pv"][:npart, :W]
        if dep is not None:
            p.w("vector", dep[0], dep[1])
        if signed:
            p.emit("vector", lambda ee: ee.scalar_tensor_tensor(
                ax, src_ap, float(-1.0), src_ap, OP.mult, OP.max), inc="V")
            p.emit("vector", lambda ee: ee.tensor_single_scalar(
                sg, src_ap, float(0.0), OP.is_lt), inc="V")
            p.emit("vector", lambda ee: ee.tensor_scalar(
                sg, sg, float(-2.0), float(1.0), OP.mult, OP.add), inc="V")
        else:
            p.emit("vector", lambda ee: ee.tensor_copy(ax, src_ap), inc="V")
        tv = p.emit("vector", lambda ee: ee.tensor_single_scalar(
            l, ax, float(1e-7), OP.max), inc="V")
        p.w("scalar", "V", tv)
        tc = p.emit("scalar", lambda ee: ee.activation(l, l, AF.Ln), inc="C")
        p.w("vector", "C", tc)
        p.emit("vector", lambda ee: ee.tensor_scalar(
            e, l, float(INVLN2), float(e_lo), OP.mult, OP.max), inc="V")
        p.emit("vector", lambda ee: ee.tensor_single_scalar(
            e, e, float(e_hi), OP.min), inc="V")
        p.emit("vector", lambda ee: ee.tensor_single_scalar(
            t1, e, float(-e_lo), OP.add), inc="V")
        first_e = True
        for stp in e_levels:
            sv_ = t1 if first_e else v
            p.emit("vector", lambda ee, s=stp, sv_=sv_: ee.tensor_scalar(
                dm, sv_, float(s), float(-s), OP.is_ge, OP.mult), inc="V")
            p.emit("vector", lambda ee, sv_=sv_: ee.tensor_add(v, sv_, dm), inc="V")
            first_e = False
        p.emit("vector", lambda ee: ee.tensor_sub(e, t1, v), inc="V")
        p.emit("vector", lambda ee: ee.tensor_single_scalar(
            e, e, float(e_lo), OP.add), inc="V")
        if man == 3:
            p.emit("vector", lambda ee: ee.tensor_scalar(
                a1, e, float(-LN2HI), float(B3HI), OP.mult, OP.add), inc="V")
            tv = p.emit("vector", lambda ee: ee.scalar_tensor_tensor(
                a, e, float(-LN2LO), a1, OP.mult, OP.add), inc="V")
            p.w("scalar", "V", tv)
            p.emit("scalar", lambda ee: ee.activation(
                pv, a, AF.Exp, bias=B3LO), inc="C")
            tc = p.emit("scalar", lambda ee: ee.activation(
                pp, a, AF.Exp, bias=NB3LO, scale=f32(-1.0)), inc="C")
            p.w("vector", "C", tc)
            p.emit("vector", lambda ee: ee.tensor_mul(u, ax, pv), inc="V")
        else:
            p.emit("vector", lambda ee: ee.tensor_single_scalar(
                a1, e, float(-LN2HI), OP.mult), inc="V")
            tv = p.emit("vector", lambda ee: ee.scalar_tensor_tensor(
                a, e, float(-LN2LO), a1, OP.mult, OP.add), inc="V")
            p.w("scalar", "V", tv)
            p.emit("scalar", lambda ee: ee.activation(pv, a, AF.Exp), inc="C")
            tc = p.emit("scalar", lambda ee: ee.activation(
                pp, a, AF.Exp, bias=NB10, scale=f32(-1.0)), inc="C")
            p.w("vector", "C", tc)
            p.emit("vector", lambda ee: ee.tensor_single_scalar(
                dm, ax, float(1024.0), OP.mult), inc="V")
            p.emit("vector", lambda ee: ee.tensor_mul(u, dm, pv), inc="V")
        first = True
        for stp in r_levels:
            sv = u if first else v2
            p.emit("vector", lambda ee, s=stp, sv=sv: ee.tensor_scalar(
                dm, sv, float(s - 0.5), float(-s), OP.is_ge, OP.mult), inc="V")
            p.emit("vector", lambda ee, sv=sv: ee.tensor_add(v2, sv, dm), inc="V")
            first = False
        p.emit("vector", lambda ee: ee.tensor_sub(r, u, v2), inc="V")
        p.emit("vector", lambda ee: ee.tensor_mul(r, r, pp), inc="V")
        p.emit("vector", lambda ee: ee.tensor_scalar(
            t1, r, float(-maxval), float(0.0), OP.add, OP.max), inc="V")
        p.emit("vector", lambda ee: ee.tensor_scalar(
            t1, t1, float(60000.0), float(60000.0), OP.mult, OP.mult), inc="V")
        if signed:
            p.emit("vector", lambda ee: ee.tensor_add(r, r, t1), inc="V")
            return p.emit("vector", lambda ee: ee.tensor_mul(out_ap, r, sg), inc="V")
        return p.emit("vector", lambda ee: ee.tensor_add(out_ap, r, t1), inc="V")

    tr = {"ping": 0, "lastT": 0}

    def emit_transpose_f32(src_view, nblk, dst_fn, extra_t_waits=(), stage=None,
                           idt=None):
        idt = ident if idt is None else idt
        """Transpose nblk 128x128 f32 blocks of src_view; write bf16 via dst_fn(b).
        Each group of 4 blocks stages through a psum ping pair (default
        psA0/psA1). Gates: transposes wait V >= current count (all prior vector
        writes incl. src) + extra."""
        pair = stage if stage is not None else (psA0, psA1)
        last_v = None
        for g0 in range(0, nblk, 4):
            gn = min(4, nblk - g0)
            ps = pair[tr["ping"]]
            tr["ping"] = 1 - tr["ping"]
            p.w("tensor", "V", p.cnt["V"])
            p.w("tensor", "C", p.cnt["C"])
            for s_, tk in extra_t_waits:
                p.w("tensor", s_, tk)
            p.w("tensor", "G", st["ident"])
            tt = None
            for b in range(gn):
                blk = g0 + b
                tt = p.emit("tensor", lambda e, blk=blk, b=b, ps=ps, idt=idt: e.transpose(
                    ps[:, b * 128:(b + 1) * 128],
                    src_view[:, blk * 128:(blk + 1) * 128], idt[:]),
                    inc="T" if b == gn - 1 else None)
            tr["lastT"] = tt
            p.w("vector", "T", tt)
            for b in range(gn):
                dst = dst_fn(g0 + b)
                last_v = p.emit("vector", lambda e, dst=dst, b=b, ps=ps: e.tensor_copy(
                    dst, ps[:, b * 128:(b + 1) * 128]), inc="V")
        return last_v

    p.emit("gpsimd", lambda e: e.memset(ident[:], 0.0))
    st["ident"] = p.emit("gpsimd", lambda e: e.affine_select(
        out=ident[:], in_=ident[:], compare_op=OP.not_equal, fill=1.0,
        base=0, pattern=[[-1, 128]], channel_multiplier=1), inc="G")
    p.w("scalar", "G", st["ident"])
    p.emit("scalar", lambda e: e.copy(ident_bf[:], ident[:]), inc="C")

    A.mark()
    wchunk = A.alloc(2 * D, F32)
    wt_bf = A.alloc(8 * D, BF16)
    wr_bf = A.alloc(8 * D, BF16)
    prev_stash = 0

    def prep_attn_weight(wsrc, dst_stash, fold):
        nonlocal prev_stash
        p.w("vector", "DB", prev_stash)
        last_v = None
        tr_tok = [0, 0]
        for c in range(8):
            off = (c % 2) * D
            wv_ = wchunk[:, off:off + D]
            p.w("sync", "T", tr_tok[c % 2])
            tok_ld = p.emit("sync", lambda e, c=c, wv_=wv_: e.dma_start(
                wv_, wsrc[c * 128:(c + 1) * 128, :]), inc="DT%d" % (c % 2))
            last_v = emit_transpose_f32(
                wv_, 8, lambda b, c=c: wt_bf[:, b * D + c * 128: b * D + c * 128 + 128],
                extra_t_waits=(("DT%d" % (c % 2), tok_ld), ("DB", prev_stash)))
            tr_tok[c % 2] = tr["lastT"]
        if fold:
            p.w("vector", "V", last_v)
            for c in range(8):
                wtc = wt_bf[:, c * D:(c + 1) * D].rearrange("p (n two) -> p n two", two=2)
                wrc = wr_bf[:, c * D:(c + 1) * D].rearrange("p (n two) -> p n two", two=2)
                p.emit("vector", lambda e, wtc=wtc, wrc=wrc: e.tensor_sub(
                    wrc[:, :, 0:1], wtc[:, :, 0:1], wtc[:, :, 1:2]), inc="V")
                last_v = p.emit("vector", lambda e, wtc=wtc, wrc=wrc: e.tensor_add(
                    wrc[:, :, 1:2], wtc[:, :, 1:2], wtc[:, :, 0:1]), inc="V")
            src = wr_bf
        else:
            src = wt_bf
        p.w("sync", "V", last_v)
        tok = None
        for c in range(8):
            tok = p.emit("sync", lambda e, c=c, src=src: e.dma_start(
                dst_stash[c * 128:(c + 1) * 128, :], src[:, c * D:(c + 1) * D]), inc="DB")
        prev_stash = tok
        return tok

    prep_attn_weight(wqh, st_wq, True)
    prep_attn_weight(wkh, st_wk, True)
    st["attn_w_stash"] = prep_attn_weight(wvh, st_wv, False)
    A.release()

    A.mark()
    w1c = A.alloc(2 * D, F32)
    w1t_full = A.alloc(8 * HID, BF16)
    tr_tok1 = [0, 0]
    for c in range(32):
        off = (c % 2) * D
        wv_ = w1c[:, off:off + D]
        p.w("sync", "T", tr_tok1[c % 2])
        tok_ld = p.emit("sync", lambda e, c=c, wv_=wv_: e.dma_start(
            wv_, w1h[c * 128:(c + 1) * 128, :]), inc="DT%d" % (c % 2))
        tv = emit_transpose_f32(
            wv_, 8, lambda b, c=c: w1t_full[:, b * HID + c * 128: b * HID + c * 128 + 128],
            extra_t_waits=(("DT%d" % (c % 2), tok_ld),))
        tr_tok1[c % 2] = tr["lastT"]
    p.w("sync", "V", p.cnt["V"])
    p.w("sync", "DB", prev_stash)
    for r_ in range(8):
        prev_stash = p.emit("sync", lambda e, r_=r_: e.dma_start(
            st_w1[r_ * 128:(r_ + 1) * 128, :], w1t_full[:, r_ * HID:(r_ + 1) * HID]), inc="DB")
    A.release()

    q_tok2 = [("V", 0), ("V", 0)]
    A.mark()
    w2c = A.alloc(2 * HID, F32)
    w2q_f32 = A.alloc(HID, F32)
    w2t_full = A.alloc(32 * D, BF16)
    if QUANT == "ladder":
        scq = {k: A.alloc(HID // 2, F16) for k in
               ("ax", "sg", "l", "v", "dm", "e", "u", "v2", "r", "t1")}
        scq["a1"] = A.alloc(HID // 2, F32)
        scq["a"] = A.alloc(HID // 2, F32)
        scq["pp"] = A.alloc(HID // 2, F32)
        scq["pv"] = A.alloc(HID // 2, F32)
    else:
        w2q_f8 = A.alloc(HID // 2, F16).bitcast(F8)
    for c in range(8):
        off = (c % 2) * HID
        wv_ = w2c[:, off:off + HID]
        p.w("sync", q_tok2[c % 2][0], q_tok2[c % 2][1])
        tok_ld = p.emit("sync", lambda e, c=c, wv_=wv_: e.dma_start(
            wv_, w2h[c * 128:(c + 1) * 128, :]), inc="DT%d" % (c % 2))
        p.w("vector", "T", tr["lastT"])
        if QUANT == "ladder":
            for half in range(2):
                hw_ = HID // 2
                tq = emit_ladder(scq, wv_[:, half * hw_:(half + 1) * hw_],
                                 w2q_f32[:, half * hw_:(half + 1) * hw_], hw_,
                                 128, 3, -6.0, 3.99,
                                 (8, 4, 2, 1), (16, 8, 4, 2, 1), 240.0, True,
                                 ("DT%d" % (c % 2), tok_ld))
            dep = ("V", tq)
        else:
            p.w("scalar", "DT%d" % (c % 2), tok_ld)
            p.w("scalar", "T", tr["lastT"])
            p.emit("scalar", lambda e, wv_=wv_: e.copy(w2q_f8[:, :HID], wv_), inc="C")
            tq = p.emit("scalar", lambda e: e.copy(w2q_f32[:, :], w2q_f8[:, :HID]), inc="C")
            dep = ("C", tq)
        q_tok2[c % 2] = dep
        tv = emit_transpose_f32(
            w2q_f32[:, :], 32, lambda b, c=c: w2t_full[:, b * D + c * 128: b * D + c * 128 + 128],
            extra_t_waits=(dep,))
    p.w("sync", "V", p.cnt["V"])
    p.w("sync", "DB", prev_stash)
    for r_ in range(32):
        prev_stash = p.emit("sync", lambda e, r_=r_: e.dma_start(
            st_w2[r_ * 128:(r_ + 1) * 128, :], w2t_full[:, r_ * D:(r_ + 1) * D]), inc="DB")
    A.release()
    st["w_stash_done"] = prev_stash

    tok_bv = p.emit("sync", lambda e: e.dma_start(bscr[1:2, 0:D], bvh), inc="DS")

    def rope_fold_row(dep):
        p.w("vector", "DS", dep)
        sr = bscr[0:1, 0:D].rearrange("p (n two) -> p n two", two=2)
        dr = bscr[0:1, D:2 * D].rearrange("p (n two) -> p n two", two=2)
        p.emit("vector", lambda e: e.tensor_sub(
            dr[:, :, 0:1], sr[:, :, 0:1], sr[:, :, 1:2]), inc="V")
        return p.emit("vector", lambda e: e.tensor_add(
            dr[:, :, 1:2], sr[:, :, 1:2], sr[:, :, 0:1]), inc="V")

    tok_bq = p.emit("sync", lambda e: e.dma_start(bscr[0:1, 0:D], bqh), inc="DS")
    tqf = rope_fold_row(tok_bq)
    p.w("sync", "V", tqf)
    tbp0 = p.emit("sync", lambda e: e.dma_start(st_bp[0:1, :], bscr[0:1, D:2 * D]), inc="DS")
    p.w("sync", "DS", tbp0)
    tok_bk = p.emit("sync", lambda e: e.dma_start(bscr[0:1, 0:D], bkh), inc="DS")
    tqf = rope_fold_row(tok_bk)
    p.w("sync", "V", tqf)
    tbp1 = p.emit("sync", lambda e: e.dma_start(st_bp[1:2, :], bscr[0:1, D:2 * D]), inc="DS")
    p.w("sync", "DS", tbp1)
    p.emit("sync", lambda e: e.dma_start(
        bqr[:], st_bp[0:1, :].rearrange("one (c p) -> (one p) c", p=128),
        allow_slow_non_contiguous=True), inc="DS")
    st["bqr"] = p.emit("sync", lambda e: e.dma_start(
        bkr[:], st_bp[1:2, :].rearrange("one (c p) -> (one p) c", p=128),
        allow_slow_non_contiguous=True), inc="DS")
    st["b1L"] = p.emit("sync", lambda e: e.dma_start(
        b1L[:], b1h[0:1, :].rearrange("one (c p) -> (one p) c", p=128),
        allow_slow_non_contiguous=True), inc="DS")
    p.w("sync", "DS", tbp1)
    tok_b2 = p.emit("sync", lambda e: e.dma_start(bscr[0:1, 0:D], b2h), inc="DS")
    if QUANT == "ladder":
        A.mark()
        scb = {k: A.alloc(D, F16) for k in
               ("ax", "sg", "l", "v", "dm", "e", "u", "v2", "r", "t1")}
        scb["a1"] = A.alloc(D, F32)
        scb["a"] = A.alloc(D, F32)
        scb["pp"] = A.alloc(D, F32)
        scb["pv"] = A.alloc(D, F32)
        tok_b2q = emit_ladder(scb, bscr[0:1, 0:D], bscr[0:1, D:2 * D], D, 1, 10,
                              -14.0, 3.99, (16, 8, 4, 2, 1),
                              (2048, 1024, 512, 256, 128, 64, 32, 16, 8, 4, 2, 1),
                              65504.0, True, ("DS", tok_b2))
        A.release()
        sem_b2 = ("V", tok_b2q)
    else:
        p.w("scalar", "DS", tok_b2)
        b2f16 = b2qb[0:1, 0:512].bitcast(F16)
        p.emit("scalar", lambda e: e.copy(b2f16, bscr[0:1, 0:D]), inc="C")
        tok_b2q = p.emit("scalar", lambda e: e.copy(bscr[0:1, D:2 * D], b2f16), inc="C")
        sem_b2 = ("C", tok_b2q)

    def bcast_rows(dst, src_row, dep_sem, dep_tok):
        p.w("gpsimd", dep_sem, dep_tok)
        tok = p.emit("gpsimd", lambda e: e.dma_start(dst[0:1, :], src_row), inc="DG")
        n = 1
        while n < 128:
            p.w("gpsimd", "DG", tok)
            m = min(n, 128 - n)
            tok = p.emit("gpsimd", lambda e, n=n, m=m: e.dma_start(
                dst[n:n + m, :], dst[0:m, :]), inc="DG")
            n += m
        return tok

    st["b2qb"] = bcast_rows(b2qb, bscr[0:1, D:2 * D], sem_b2[0], sem_b2[1])
    st["bvb"] = bcast_rows(bvb, bscr[1:2, 0:D], "DS", tok_b2)

    def phase_barrier():
        for nm in ("T", "V", "C", "DG", "DB", "DW0", "DW1",
                   "DX0", "DX1", "DX2", "DX3", "DT0", "DT1"):
            p.w("sync", nm, p.cnt[nm])

    guard = {}
    last_out_store = 0
    x1_store = 0

    for seq in range(NSEQ):
        base = seq * S_LEN
        A.mark()
        wq_s = A.alloc(8 * D, BF16)
        wk_s = A.alloc(8 * D, BF16)
        q_fm = A.alloc(8 * S_LEN, BF16)
        k_fm = A.alloc(8 * S_LEN, BF16)
        v_tm = A.alloc(16 * D, BF16)
        xt = A.alloc(2 * D, F32)
        lnt = A.alloc(2 * D, F32)
        tpc = A.alloc(2 * 512, F32)
        etmp = A.alloc(2 * 512, F32)
        A.mark()
        ln1 = A.alloc(8 * S_LEN, BF16)

        tok_wq_ld = [None] * 8
        tok_wk_ld = [None] * 8
        phase_barrier()
        xt_busy = [0, 0]
        ln_ready = None
        ln_evac = []
        tp_busy = [0, 0]
        etmp_busy = [0, 0]
        qk_last_v = 0
        trstage = (psB[:, 1024:1280].bitcast(BF16), psB[:, 1536:1792].bitcast(BF16))
        lnb = bscr[:, 0:1024].bitcast(BF16)

        def a2_strip(sc_):
            nonlocal qk_last_v
            for c in range(8):
                gi = sc_ * 8 + c
                tpp = gi % 2
                tview = tpc[:, tpp * 512:(tpp + 1) * 512]
                p.w("sync", "V", tp_busy[tpp])
                tok_t = p.emit("sync", lambda e, c=c, sc_=sc_, tview=tview: e.dma_start(
                    tview, tth[c * 128:(c + 1) * 128, sc_ * 512:(sc_ + 1) * 512]), inc="DT%d" % tpp)
                for ping, (wv_, bias_t, dst) in enumerate(
                        ((wq_s, bqr, q_fm), (wk_s, bkr, k_fm))):
                    dep2 = gi % 2
                    if ping == 0:
                        ps_sl = (psA0 if dep2 == 0 else psA1)[:, 0:512]
                    else:
                        ps_sl = psB[:, dep2 * 512:(dep2 + 1) * 512]
                    ev = etmp[:, ping * 512:(ping + 1) * 512]
                    p.w("tensor", "V", ln_evac[sc_ * 4 + 3])
                    p.w("tensor", "DB", tok_wk_ld[7])
                    p.w("tensor", "C", guard.get(("qk", ping, dep2), 0))
                    tmm = None
                    for kk in range(8):
                        tmm = p.emit("tensor", lambda e, wv_=wv_, c=c, sc_=sc_, kk=kk, ps_sl=ps_sl: e.matmul(
                            ps_sl,
                            wv_[:, kk * D + c * 128: kk * D + c * 128 + 128],
                            ln1[:, kk * S_LEN + sc_ * 512: kk * S_LEN + (sc_ + 1) * 512],
                            start=(kk == 0), stop=(kk == 7)),
                            inc="T" if kk == 7 else None)
                    p.w("scalar", "T", tmm)
                    p.w("scalar", "V", etmp_busy[ping])
                    p.w("scalar", "DS", st["bqr"])
                    tev = p.emit("scalar", lambda e, ps_sl=ps_sl, ev=ev, bias_t=bias_t, c=c: e.activation(
                        ev, ps_sl, AF.Identity, bias=bias_t[:, c:c + 1]), inc="C")
                    guard[("qk", ping, dep2)] = tev
                    p.w("vector", "C", tev)
                    p.w("vector", "DT%d" % tpp, tok_t)
                    qk_last_v = p.emit("vector", lambda e, dst=dst, c=c, sc_=sc_, ev=ev, tview=tview: e.tensor_mul(
                        dst[:, c * S_LEN + sc_ * 512: c * S_LEN + (sc_ + 1) * 512],
                        ev, tview), inc="V")
                    etmp_busy[ping] = qk_last_v
                tp_busy[tpp] = qk_last_v

        for i in range(16):
            pi = i % 2
            xv = xt[:, pi * D:pi * D + D]
            p.w("sync", "V", xt_busy[pi])
            tok_x = p.emit("sync", lambda e, i=i, xv=xv, base=base: e.dma_start(
                xv, xh[base + i * 128: base + (i + 1) * 128, :]), inc="DX%d" % pi)
            if i == 1:
                p.w("sync", "DB", st["w_stash_done"])
                for c in range(8):
                    tok_wq_ld[c] = p.emit("sync", lambda e, c=c: e.dma_start(
                        wq_s[:, c * D:(c + 1) * D], st_wq[c * 128:(c + 1) * 128, :]), inc="DB")
                for c in range(8):
                    tok_wk_ld[c] = p.emit("sync", lambda e, c=c: e.dma_start(
                        wk_s[:, c * D:(c + 1) * D], st_wk[c * 128:(c + 1) * 128, :]), inc="DB")
                if seq == 0:
                    dbg_tap("wq_s", [128, 8 * D], wq_s[:, :], "DB", tok_wq_ld[7], BF16)
            p.w("vector", "DX%d" % pi, tok_x)
            p.w("vector", "T", guard.get(("lnt", i % 2), 0))
            sm = (i % 4) * 8
            lv_ = lnt[:, (i % 2) * D:(i % 2) * D + D]
            p.emit("vector", lambda e, xv=xv, lv_=lv_: e.tensor_mul(
                lv_, xv, xv), inc="V")
            p.emit("vector", lambda e, xv=xv, sm=sm: e.reduce_sum(
                stats[:, sm:sm + 1], xv, AX.X), inc="V")
            tvv = p.emit("vector", lambda e, sm=sm, lv_=lv_: e.reduce_sum(
                stats[:, sm + 1:sm + 2], lv_, AX.X), inc="V")
            p.w("scalar", "V", tvv)
            p.emit("scalar", lambda e, sm=sm: e.activation(
                stats[:, sm + 6:sm + 7], stats[:, sm + 1:sm + 2], AF.Identity,
                scale=f32(1.0 / D), bias=f32(1e-5)), inc="C")
            tcs = p.emit("scalar", lambda e, sm=sm: e.mul(
                stats[:, sm + 2:sm + 3], stats[:, sm:sm + 1], f32(1.0 / D)), inc="C")
            p.w("vector", "C", tcs)
            tvq = p.emit("vector", lambda e, sm=sm: e.tensor_mul(
                stats[:, sm + 3:sm + 4], stats[:, sm + 2:sm + 3], stats[:, sm + 2:sm + 3]), inc="V")
            p.w("scalar", "V", tvq)
            tcs = p.emit("scalar", lambda e, sm=sm: e.activation(
                stats[:, sm + 5:sm + 6], stats[:, sm + 3:sm + 4], AF.Sqrt,
                scale=f32(-1.0), bias=stats[:, sm + 6:sm + 7]), inc="C")
            p.w("vector", "C", tcs)
            p.emit("vector", lambda e, sm=sm: e.reciprocal(
                stats[:, sm + 4:sm + 5], stats[:, sm + 5:sm + 6]), inc="V")
            p.emit("vector", lambda e, xv=xv, sm=sm, lv_=lv_: e.tensor_single_scalar(
                lv_, xv, stats[:, sm + 2:sm + 3], OP.subtract), inc="V")
            if i == 0:
                p.w("vector", "DG", max(st["bvb"], st["b2qb"]))
            lb_ = lnb[:, (i % 2) * D:(i % 2) * D + D]
            tvn = p.emit("vector", lambda e, sm=sm, lv_=lv_, lb_=lb_: e.tensor_single_scalar(
                lb_, lv_, stats[:, sm + 4:sm + 5], OP.mult), inc="V")
            xt_busy[pi] = tvn
            if seq == 0 and i == 0:
                dbg_tap("xv0", [128, D], xv, "V", tvn)
                dbg_tap("lnt0", [128, D], lb_, "V", tvn, BF16)
                dbg_tap("stats0", [128, 32], stats[:], "V", tvn)
            ln_ready = emit_transpose_f32(
                lb_, 8,
                lambda b, i=i: ln1[:, b * S_LEN + i * 128: b * S_LEN + i * 128 + 128],
                stage=trstage, idt=ident_bf)
            guard[("lnt", i % 2)] = tr["lastT"]
            ln_evac.append(ln_ready)
            if i % 4 == 3:
                a2_strip(i // 4)

        if seq == 0:
            dbg_tap("ln1c0", [128, S_LEN], ln1[:, 0:S_LEN], "V", ln_ready, BF16)
        if seq == 0:
            dbg_tap("qc0", [128, S_LEN], q_fm[:, 0:S_LEN], "V", qk_last_v, BF16)
            dbg_tap("kc0", [128, S_LEN], k_fm[:, 0:S_LEN], "V", qk_last_v, BF16)
        wv_s = wq_s
        tok_wv_ld = [None] * 8
        p.w("sync", "V", qk_last_v)
        for c in range(8):
            tok_wv_ld[c] = p.emit("sync", lambda e, c=c: e.dma_start(
                wv_s[:, c * D:(c + 1) * D], st_wv[c * 128:(c + 1) * 128, :]), inc="DB")
        if seq == 0:
            dbg_tap("wv_s", [128, 8 * D], wv_s[:, :], "DB", tok_wv_ld[7], BF16)
        v_last = 0
        for i in range(16):
            for n in range(2):
                ping = (i * 2 + n) % 2
                ps_sl = psB[:, ping * 512:(ping + 1) * 512]
                if i == 0:
                    p.w("tensor", "C", p.cnt["C"])
                p.w("tensor", "V", guard.get(("v", ping), 0))
                p.w("tensor", "DB", tok_wv_ld[7])
                tmm = None
                for kk in range(8):
                    tmm = p.emit("tensor", lambda e, i=i, n=n, kk=kk, ps_sl=ps_sl: e.matmul(
                        ps_sl,
                        ln1[:, kk * S_LEN + i * 128: kk * S_LEN + i * 128 + 128],
                        wv_s[:, kk * D + n * 512: kk * D + (n + 1) * 512],
                        start=(kk == 0), stop=(kk == 7)),
                        inc="T" if kk == 7 else None)
                p.w("vector", "T", tmm)
                v_last = p.emit("vector", lambda e, i=i, n=n, ps_sl=ps_sl: e.tensor_copy(
                    v_tm[:, i * D + n * 512: i * D + (n + 1) * 512], ps_sl), inc="V")
                guard[("v", ping)] = v_last

        if seq == 0:
            dbg_tap("vt0", [128, D], v_tm[:, 0:D], "V", v_last, BF16)
        A.release()
        probs = A.alloc(S_LEN, BF16)
        probsT = A.alloc(S_LEN, BF16)
        x1t = A.alloc(2 * D, F32)
        xbv = A.alloc(D, F32)

        x1t_busy = [0, 0]
        prev_evac = 0
        prev_exp = 0
        for i in range(16):
            sm = 16 + (i % 2) * 4
            p.w("tensor", "V", max(v_last, prev_evac))
            p.w("tensor", "C", prev_exp)
            tmm = None
            for kk in range(8):
                for tc_ in range(4):
                    tmm = p.emit("tensor", lambda e, i=i, kk=kk, tc_=tc_: e.matmul(
                        psB[:, tc_ * 512:(tc_ + 1) * 512],
                        q_fm[:, kk * S_LEN + i * 128: kk * S_LEN + i * 128 + 128],
                        k_fm[:, kk * S_LEN + tc_ * 512: kk * S_LEN + (tc_ + 1) * 512],
                        start=(kk == 0), stop=(kk == 7)),
                        inc="T" if kk == 7 and tc_ == 3 else None)
            p.w("vector", "T", tmm)
            tnm2 = p.emit("vector", lambda e, sm=sm: e.reduce_max(
                stats[:, sm:sm + 1], psB[:, 0:2048], AX.X, negate=True), inc="V")
            p.w("scalar", "V", tnm2)
            prev_exp = p.emit("scalar", lambda e, sm=sm: e.activation(
                probs[:, :], psB[:, 0:2048], AF.Exp, bias=stats[:, sm:sm + 1],
                scale=f32(1.0)), inc="C")
            p.w("vector", "C", prev_exp)
            trs = p.emit("vector", lambda e, sm=sm: e.reduce_sum(
                stats[:, sm + 2:sm + 3], probs[:, :], AX.X), inc="V")
            for g in range(4):
                p.w("tensor", "C", prev_exp)
                p.w("tensor", "V", prev_evac)
                p.w("tensor", "G", st["ident"])
                psb_bf = psB[:, 0:256].bitcast(BF16)
                tt = None
                for b in range(4):
                    blk = g * 4 + b
                    tt = p.emit("tensor", lambda e, blk=blk, b=b, psb_bf=psb_bf: e.transpose(
                        psb_bf[:, b * 128:(b + 1) * 128],
                        probs[:, blk * 128:(blk + 1) * 128], ident_bf[:]),
                        inc="T" if b == 3 else None)
                tr["lastT"] = tt
                p.w("vector", "T", tt)
                if g == 2:
                    trz = p.emit("vector", lambda e, sm=sm: e.reciprocal(
                        stats[:, sm + 3:sm + 4], stats[:, sm + 2:sm + 3]), inc="V")
                prev_evac = p.emit("vector", lambda e, g=g, psb_bf=psb_bf: e.tensor_copy(
                    probsT[:, g * 512:(g + 1) * 512], psb_bf[:, 0:512]), inc="V")
            if seq == 0 and i == 0:
                dbg_tap("probs0", [128, S_LEN], probs[:, :], "V", prev_evac, BF16)
            p.w("tensor", "V", max(prev_evac, guard.get("pv", 0)))
            tpv = None
            for tk in range(16):
                for n in range(2):
                    tpv = p.emit("tensor", lambda e, tk=tk, n=n: e.matmul(
                        psPV[:, n * 512:(n + 1) * 512],
                        probsT[:, tk * 128:(tk + 1) * 128],
                        v_tm[:, tk * D + n * 512: tk * D + (n + 1) * 512],
                        start=(tk == 0), stop=(tk == 15)),
                        inc="T" if tk == 15 and n == 1 else None)
            pi = i % 2
            xv = xt[:, pi * D:pi * D + D]
            p.w("sync", "V", xt_busy[pi])
            tok_x = p.emit("sync", lambda e, i=i, xv=xv, base=base: e.dma_start(
                xv, xh[base + i * 128: base + (i + 1) * 128, :]), inc="DX%d" % pi)
            p.w("vector", "DX%d" % pi, tok_x)
            p.w("vector", "DG", st["bvb"])
            tb = p.emit("vector", lambda e, xv=xv: e.tensor_add(xbv[:, :], xv, bvb[:]), inc="V")
            xt_busy[pi] = tb
            p.w("vector", "DG", x1t_busy[pi])
            p.w("vector", "T", tpv)
            p.w("vector", "V", trz)
            x1v = x1t[:, pi * D:pi * D + D]
            tx1 = p.emit("vector", lambda e, x1v=x1v, sm=sm: e.scalar_tensor_tensor(
                x1v, psPV[:, 0:1024], stats[:, sm + 3:sm + 4], xbv[:, :],
                OP.mult, OP.add), inc="V")
            guard["pv"] = tx1
            p.w("gpsimd", "V", tx1)
            x1_store = p.emit("gpsimd", lambda e, i=i, x1v=x1v, base=base: e.dma_start(
                x1h[base + i * 128: base + (i + 1) * 128, :], x1v), inc="DG")
            x1t_busy[pi] = x1_store
            if seq == 0 and i == 0:
                dbg_tap("x1t0", [128, D], x1v, "V", guard["pv"])

        A.release()

        A.mark()
        w1t_s = A.alloc(8 * HID, BF16)
        hq = A.alloc(32 * 512, BF16)
        ln2s = [A.alloc(8 * 512, BF16), A.alloc(8 * 512, BF16)]
        w2t_win = A.alloc(4 * D, BF16)
        xt2 = A.alloc(2 * D, F32)
        xt3 = A.alloc(2 * D, F32)
        lnt2 = A.alloc(4 * D, F32)
        y0 = A.alloc(2 * D, F32)
        outt = A.alloc(2 * D, F32)
        if QUANT == "ladder":
            scl = {k: A.alloc(D, F16) for k in
                   ("ax", "sg", "l", "v", "dm", "e", "u", "v2", "r", "t1")}
            scl["a1"] = A.alloc(D, F32)
            scl["a"] = A.alloc(D, F32)
            scl["pp"] = A.alloc(D, F32)
            scl["pv"] = A.alloc(D, F32)
            hsrc = A.alloc(2 * 512, F32)
            yq = A.alloc(D, F32)
        else:
            h_f8 = A.alloc(512 // 2, F16).bitcast(F8)
            y16 = A.alloc(D, F16)

        tok_w1_ld = [None] * 8
        phase_barrier()

        xt2_busy = [0, 0]
        xt3_busy = [0, 0]
        out_busy = [0, 0]
        ln2_ready_ch = {}

        lnb2 = bscr[:, 0:2048].bitcast(BF16)

        def b1_chain(ch):
            crow = base + ch * 512
            for ii in range(4):
                pi = ii % 2
                xv = xt2[:, pi * D:pi * D + D]
                p.w("sync", "V", xt2_busy[pi])
                p.w("sync", "DG", x1_store)
                tok_x = p.emit("sync", lambda e, ii=ii, xv=xv, crow=crow: e.dma_start(
                    xv, x1h[crow + ii * 128: crow + (ii + 1) * 128, :]), inc="DX%d" % (2 + pi))
                p.w("vector", "DX%d" % (2 + pi), tok_x)
                p.w("vector", "T", guard.get(("lnt2", ii), 0))
                sm = ii * 8
                lv_ = lnt2[:, ii * D:ii * D + D]
                p.emit("vector", lambda e, xv=xv, lv_=lv_: e.tensor_mul(
                    lv_, xv, xv), inc="V")
                p.emit("vector", lambda e, xv=xv, sm=sm: e.reduce_sum(
                    stats[:, sm:sm + 1], xv, AX.X), inc="V")
                tvv = p.emit("vector", lambda e, sm=sm, lv_=lv_: e.reduce_sum(
                    stats[:, sm + 1:sm + 2], lv_, AX.X), inc="V")
                p.w("scalar", "V", tvv)
                p.emit("scalar", lambda e, sm=sm: e.activation(
                    stats[:, sm + 6:sm + 7], stats[:, sm + 1:sm + 2], AF.Identity,
                    scale=f32(1.0 / D), bias=f32(1e-5)), inc="C")
                tcs = p.emit("scalar", lambda e, sm=sm: e.mul(
                    stats[:, sm + 2:sm + 3], stats[:, sm:sm + 1], f32(1.0 / D)), inc="C")
                p.w("vector", "C", tcs)
                tvq = p.emit("vector", lambda e, sm=sm: e.tensor_mul(
                    stats[:, sm + 3:sm + 4], stats[:, sm + 2:sm + 3], stats[:, sm + 2:sm + 3]), inc="V")
                p.w("scalar", "V", tvq)
                tcs = p.emit("scalar", lambda e, sm=sm: e.activation(
                    stats[:, sm + 5:sm + 6], stats[:, sm + 3:sm + 4], AF.Sqrt,
                    scale=f32(-1.0), bias=stats[:, sm + 6:sm + 7]), inc="C")
                p.w("vector", "C", tcs)
                p.emit("vector", lambda e, sm=sm: e.reciprocal(
                    stats[:, sm + 4:sm + 5], stats[:, sm + 5:sm + 6]), inc="V")
                p.emit("vector", lambda e, xv=xv, sm=sm, lv_=lv_: e.tensor_single_scalar(
                    lv_, xv, stats[:, sm + 2:sm + 3], OP.subtract), inc="V")
                lb_ = lnb2[:, ii * D:ii * D + D]
                tvn = p.emit("vector", lambda e, sm=sm, lv_=lv_, lb_=lb_: e.tensor_single_scalar(
                    lb_, lv_, stats[:, sm + 4:sm + 5], OP.mult), inc="V")
                xt2_busy[pi] = tvn

        def b1_tr(ch):
            dstb = ln2s[ch % 2]
            last = None
            for ii in range(4):
                lb_ = lnb2[:, ii * D:ii * D + D]
                last = emit_transpose_f32(
                    lb_, 8,
                    lambda b, ii=ii, dstb=dstb: dstb[:, b * 512 + ii * 128: b * 512 + ii * 128 + 128],
                    stage=(psA0[:, 0:256].bitcast(BF16), psA1[:, 0:256].bitcast(BF16)),
                    idt=ident_bf)
                guard[("lnt2", ii)] = tr["lastT"]
            ln2_ready_ch[ch] = last

        b1_chain(0)
        p.w("sync", "DB", st["w_stash_done"])
        for c in range(8):
            tok_w1_ld[c] = p.emit("sync", lambda e, c=c: e.dma_start(
                w1t_s[:, c * HID:(c + 1) * HID], st_w1[c * 128:(c + 1) * 128, :]), inc="DB")
        b1_tr(0)
        for ch in range(4):
            crow = base + ch * 512
            lnb = ln2s[ch % 2]
            ln2_ready = ln2_ready_ch[ch]
            if seq == 0 and ch == 0:
                dbg_tap("ln2c0", [128, 512], lnb[:, 0:512], "V", ln2_ready, BF16)
            hq_ready_v = 0
            last_relu_c = 0
            hq_busy = [0, 0]
            for jt in range(32):
                ping = jt % 2
                ps = psA0 if ping == 0 else psA1
                p.w("tensor", "C", guard.get(("f1", ping), 0))
                p.w("tensor", "V", ln2_ready)
                p.w("tensor", "DB", tok_w1_ld[7])
                tmm = None
                for kk in range(8):
                    tmm = p.emit("tensor", lambda e, jt=jt, kk=kk, ps=ps, lnb=lnb: e.matmul(
                        ps[:, 0:512],
                        w1t_s[:, kk * HID + jt * 128: kk * HID + jt * 128 + 128],
                        lnb[:, kk * 512:(kk + 1) * 512],
                        start=(kk == 0), stop=(kk == 7)),
                        inc="T" if kk == 7 else None)
                p.w("scalar", "T", tmm)
                p.w("scalar", "DS", st["b1L"])
                if QUANT == "ladder":
                    hv = hsrc[:, ping * 512:(ping + 1) * 512]
                    p.w("scalar", "V", hq_busy[ping])
                    trl = p.emit("scalar", lambda e, ps=ps, hv=hv, jt=jt: e.activation(
                        hv, ps[:, 0:512], AF.Relu, bias=b1L[:, jt:jt + 1]), inc="C")
                    guard[("f1", ping)] = trl
                    last_relu_c = trl
                    hq_ready_v = emit_ladder(
                        scl, hv, hq[:, jt * 512:(jt + 1) * 512], 512, 128, 3,
                        -6.0, 7.99, (8, 4, 2, 1), (16, 8, 4, 2, 1), 240.0,
                        False, ("C", trl))
                    hq_busy[ping] = hq_ready_v
                else:
                    p.emit("scalar", lambda e, ps=ps, jt=jt: e.activation(
                        h_f8[:, 0:512], ps[:, 0:512], AF.Relu, bias=b1L[:, jt:jt + 1]), inc="C")
                    trl = p.emit("scalar", lambda e, jt=jt: e.copy(
                        hq[:, jt * 512:(jt + 1) * 512], h_f8[:, 0:512]), inc="C")
                    guard[("f1", ping)] = trl
                    last_relu_c = trl

            if seq == 0 and ch == 0:
                if QUANT == "ladder":
                    dbg_tap("hq0", [128, 512], hq[:, 0:512], "V", hq_ready_v, BF16)
                else:
                    dbg_tap("hq0", [128, 512], hq[:, 0:512], "C", last_relu_c, BF16)
            if ch < 3:
                b1_chain(ch + 1)
            yps = [psB[:, 0:1024], psB[:, 1024:2048], psPV[:, 0:1024]]
            if seq == 0 and ch == 0:
                dbg_tap("hqall", [128, 32 * 512], hq[:, :], "C", last_relu_c, BF16)
                dbg_tap("b2qbt", [128, D], b2qb[:], "DG", st["b2qb"])
            w2_busy = [0, guard.get("b3_prev_t", 0)]
            w2_tok = [0, 0]
            tff = None
            for jkp in range(17):
                if jkp < 16:
                    wp = jkp % 2
                    p.w("sync", "T", w2_busy[wp] if jkp > 0 else guard.get("b3_prev_t", 0))
                    for tl in range(2):
                        w2_tok[wp] = p.emit("sync", lambda e, jkp=jkp, wp=wp, tl=tl: e.dma_start(
                            w2t_win[:, (wp * 2 + tl) * D:(wp * 2 + tl + 1) * D],
                            st_w2[jkp * 256 + tl * 128: jkp * 256 + (tl + 1) * 128, :]),
                            inc="DW%d" % wp)
                if jkp < 1:
                    continue
                for jk in range((jkp - 1) * 2, jkp * 2):
                    p.w("tensor", "DW%d" % ((jk // 2) % 2), w2_tok[(jk // 2) % 2])
                    wwin = w2t_win[:, (jk % 4) * D:((jk % 4) + 1) * D]
                    if jk == 0:
                        p.w("tensor", "V", max(hq_ready_v, guard.get("ffn2", 0)))
                        p.w("tensor", "C", last_relu_c)
                    for ii in range(4):
                        for n in range(2):
                            if ii < 3:
                                dst = yps[ii][:, n * 512:(n + 1) * 512]
                            else:
                                dst = (psA0 if n == 0 else psA1)[:, 0:512]
                            tff = p.emit("tensor", lambda e, jk=jk, dst=dst, ii=ii, n=n, wwin=wwin: e.matmul(
                                dst,
                                hq[:, jk * 512 + ii * 128: jk * 512 + ii * 128 + 128],
                                wwin[:, n * 512:(n + 1) * 512],
                                start=(jk == 0), stop=(jk == 31)),
                                inc="T" if ii == 3 and n == 1 else None)
                    w2_busy[(jk // 2) % 2] = tff
            guard["b3_prev_t"] = tff
            guard[("f1", 0)] = 0
            guard[("f1", 1)] = 0
            yv3 = y0[:, D:2 * D]
            p.w("vector", "T", tff)
            p.w("vector", "DG", st["b2qb"])
            p.w("vector", "V", guard.get(("y0", 1), 0))
            p.emit("vector", lambda e, yv3=yv3: e.tensor_add(
                yv3[:, 0:512], psA0[:, 0:512], b2qb[:, 0:512]), inc="V")
            ty0_3 = p.emit("vector", lambda e, yv3=yv3: e.tensor_add(
                yv3[:, 512:1024], psA1[:, 0:512], b2qb[:, 512:1024]), inc="V")
            if ch < 3:
                b1_tr(ch + 1)
            y_ep = 0
            for ii in (3, 0, 1, 2):
                pi = ii % 2
                xv = xt3[:, pi * D:pi * D + D]
                p.w("sync", "V", xt3_busy[pi])
                tok_x = p.emit("sync", lambda e, ii=ii, xv=xv, crow=crow: e.dma_start(
                    xv, x1h[crow + ii * 128: crow + (ii + 1) * 128, :]), inc="DX%d" % pi)
                yv = y0[:, pi * D:pi * D + D]
                if ii == 3:
                    ty0 = ty0_3
                else:
                    p.w("vector", "T", tff)
                    p.w("vector", "DG", st["b2qb"])
                    p.w("vector", "V", guard.get(("y0", pi), 0))
                    ty0 = p.emit("vector", lambda e, ii=ii, yv=yv: e.tensor_add(
                        yv, yps[ii], b2qb[:]), inc="V")
                ov = outt[:, pi * D:pi * D + D]
                if seq == 0 and ch == 0 and ii == 0:
                    dbg_tap("y00", [128, D], yv, "V", ty0)
                p.w("vector", "DG", out_busy[pi])
                if QUANT == "ladder":
                    emit_ladder(scl, yv, yq[:, 0:D], D, 128, 10, -14.0, 3.99,
                                (16, 8, 4, 2, 1),
                                (2048, 1024, 512, 256, 128, 64, 32, 16, 8, 4, 2, 1),
                                65504.0, True, ("V", ty0))
                    p.w("vector", "DX%d" % pi, tok_x)
                    tout = p.emit("vector", lambda e, ov=ov, xv=xv: e.tensor_add(
                        ov, yq[:, 0:D], xv), inc="V")
                else:
                    p.w("scalar", "V", ty0)
                    tqc = p.emit("scalar", lambda e, yv=yv: e.copy(y16[:, 0:D], yv), inc="C")
                    p.w("vector", "C", tqc)
                    p.w("vector", "DX%d" % pi, tok_x)
                    tout = p.emit("vector", lambda e, ov=ov, xv=xv: e.tensor_add(
                        ov, y16[:, 0:D], xv), inc="V")
                guard[("y0", pi)] = tout
                xt3_busy[pi] = tout
                y_ep = tout
                p.w("gpsimd", "V", tout)
                last_out_store = p.emit("gpsimd", lambda e, ii=ii, ov=ov, crow=crow: e.dma_start(
                    outh[crow + ii * 128: crow + (ii + 1) * 128, :], ov), inc="DG")
                out_busy[pi] = last_out_store
            guard["ffn2"] = y_ep

        A.release()

    p.w("gpsimd", "DG", last_out_store)

    with nc.Block() as block:
        @block.sync
        def _(e):
            for fn in p.stream["sync"]:
                fn(e)

        @block.vector
        def _(e):
            for fn in p.stream["vector"]:
                fn(e)

        @block.scalar
        def _(e):
            for fn in p.stream["scalar"]:
                fn(e)

        @block.gpsimd
        def _(e):
            for fn in p.stream["gpsimd"]:
                fn(e)

        @block.tensor
        def _(e):
            for fn in p.stream["tensor"]:
                fn(e)

    return nc


def kernel(L0_attn_hidden_states, L0_attn_Wq, L0_attn_bq, L0_attn_Wk, L0_attn_bk,
           L0_attn_Wv, L0_attn_bv, L0_ffn_w1, L0_ffn_b1, L0_ffn_w2, L0_ffn_b2):
    x = np.ascontiguousarray(np.asarray(L0_attn_hidden_states, dtype=np.float32))
    tt = rope_table_T()
    common = {
        "wq": np.ascontiguousarray(np.asarray(L0_attn_Wq, np.float32)),
        "wk": np.ascontiguousarray(np.asarray(L0_attn_Wk, np.float32)),
        "wv": np.ascontiguousarray(np.asarray(L0_attn_Wv, np.float32)),
        "bq": np.asarray(L0_attn_bq, np.float32).reshape(1, D),
        "bk": np.asarray(L0_attn_bk, np.float32).reshape(1, D),
        "bv": np.asarray(L0_attn_bv, np.float32).reshape(1, D),
        "w1": np.ascontiguousarray(np.asarray(L0_ffn_w1, np.float32)),
        "b1": np.asarray(L0_ffn_b1, np.float32).reshape(1, HID),
        "w2": np.ascontiguousarray(np.asarray(L0_ffn_w2, np.float32)),
        "b2": np.asarray(L0_ffn_b2, np.float32).reshape(1, D),
        "tt": tt,
    }
    in_maps = []
    for c in range(8):
        m = dict(common)
        m["x"] = np.ascontiguousarray(
            x[c * NSEQ:(c + 1) * NSEQ].reshape(NSEQ * S_LEN, D))
        in_maps.append(m)
    nc = build_core_program()
    res = run_bass_kernel_spmd(nc, in_maps, core_ids=list(range(8)))
    if DEBUG:
        global _dbg
        _dbg = {k: np.asarray(res.results[0]["dbg_" + k]) for k in _dbg_names}
        global _dbg2
        _dbg2 = {k: np.asarray(res.results[1]["dbg_" + k]) for k in _dbg_names}
    out = np.concatenate(
        [np.asarray(r["out"]).reshape(NSEQ, S_LEN, D) for r in res.results], axis=0)
    return np.ascontiguousarray(out.astype(np.float32))


# version 19
# speedup vs baseline: 1.2409x; 1.0812x over previous
import numpy as np
import concourse.bass as bass
import concourse.mybir as mybir
from concourse.alu_op_type import AluOpType as OP
from concourse.bass_utils import run_bass_kernel_spmd

F32 = mybir.dt.float32
BF16 = mybir.dt.bfloat16
F16 = mybir.dt.float16
F8 = mybir.dt.float8e4
AF = mybir.ActivationFunctionType
AX = mybir.AxisListType
f32 = np.float32

QUANT = "cast"
DEBUG = False
_dbg_names = []

S_LEN = 2048
D = 1024
HID = 4096
NSEQ = 2
ROWS = NSEQ * S_LEN
INV_SQRT_HS = f32(1.0 / float(np.sqrt(np.float64(512.0))))

LN2_ = np.float64(0.6931471805599453)
INVLN2 = f32(1.0 / LN2_)
LN2HI = f32(LN2_)
LN2LO = f32(LN2_ - np.float64(f32(LN2_)))
B3HI = f32(3.0 * LN2_)
B3LO = f32(3.0 * LN2_ - np.float64(f32(3.0 * LN2_)))
NB3LO = f32(-np.float64(f32(3.0 * LN2_ - np.float64(f32(3.0 * LN2_)))))
NB10 = f32(-(10.0 * LN2_))

ENGS = ("sync", "vector", "scalar", "gpsimd", "tensor")


class Prog:
    def __init__(self, nc):
        self.nc = nc
        self.stream = {e: [] for e in ENGS}
        self.sems = {}
        self.cnt = {}
        self.last_wait = {}
        for name in ("T", "V", "C", "G", "DS", "DG", "DB", "DW0", "DW1",
                     "DX0", "DX1", "DX2", "DX3", "DT0", "DT1"):
            self.sems[name] = nc.alloc_semaphore("sem_" + name)
            self.cnt[name] = 0

    def w(self, eng, sem, tok):
        if tok is None or tok <= 0:
            return
        if self.last_wait.get((eng, sem), 0) >= tok:
            return
        self.last_wait[(eng, sem)] = tok
        s = self.sems[sem]
        self.stream[eng].append(lambda e, s=s, t=tok: e.wait_ge(s, t))

    def emit(self, eng, fn, inc=None):
        if inc is None:
            self.stream[eng].append(fn)
            return None
        s = self.sems[inc]
        step = 16 if inc.startswith("D") else 1
        self.cnt[inc] += step
        tok = self.cnt[inc]
        self.stream[eng].append(lambda e, fn=fn, s=s, st=step: fn(e).then_inc(s, st))
        return tok


class Arena:
    def __init__(self, t, nels_bf16):
        self.t = t
        self.nb = nels_bf16 * 2
        self.off = 0
        self.marks = []
        self.peak = 0

    def alloc(self, els, dtype):
        unit = {F32: 4, BF16: 2, F16: 2}[dtype]
        off = (self.off + 3) & ~3
        sz = els * unit
        assert off + sz <= self.nb, f"arena overflow: {off}+{sz} > {self.nb}"
        self.off = off + sz
        self.peak = max(self.peak, self.off)
        v = self.t[:, off // 2:(off + sz) // 2]
        if dtype != BF16:
            v = v.bitcast(dtype)
        return v

    def mark(self):
        self.marks.append(self.off)

    def release(self):
        self.off = self.marks.pop()


def rope_table_T():
    pos = np.arange(S_LEN, dtype=np.float64)
    inv = 10000.0 ** (np.arange(0, 512, 2, dtype=np.float64) / 512.0)
    radians = pos[:, None] / inv[None, :]
    sincos = np.stack((np.sin(radians), np.cos(radians)), axis=-1).reshape(S_LEN, -1)
    t = np.repeat(sincos, 2, axis=-1) * (512.0 ** -0.25)
    return np.ascontiguousarray(t.T.astype(np.float32))


def build_core_program():
    nc = bass.Bass("TRN2", target_bir_lowering=False)
    xh = nc.dram_tensor("x", [ROWS, D], F32, kind="ExternalInput").ap()
    wqh = nc.dram_tensor("wq", [D, D], F32, kind="ExternalInput").ap()
    wkh = nc.dram_tensor("wk", [D, D], F32, kind="ExternalInput").ap()
    wvh = nc.dram_tensor("wv", [D, D], F32, kind="ExternalInput").ap()
    bqh = nc.dram_tensor("bq", [1, D], F32, kind="ExternalInput").ap()
    bkh = nc.dram_tensor("bk", [1, D], F32, kind="ExternalInput").ap()
    bvh = nc.dram_tensor("bv", [1, D], F32, kind="ExternalInput").ap()
    w1h = nc.dram_tensor("w1", [HID, D], F32, kind="ExternalInput").ap()
    b1h = nc.dram_tensor("b1", [1, HID], F32, kind="ExternalInput").ap()
    w2h = nc.dram_tensor("w2", [D, HID], F32, kind="ExternalInput").ap()
    b2h = nc.dram_tensor("b2", [1, D], F32, kind="ExternalInput").ap()
    tth = nc.dram_tensor("tt", [D, S_LEN], F32, kind="ExternalInput").ap()
    outh = nc.dram_tensor("out", [ROWS, D], F32, kind="ExternalOutput").ap()
    x1h = nc.dram_tensor("x1s", [ROWS, D], F32, kind="Internal").ap()
    st_wq = nc.dram_tensor("st_wq", [D, D], BF16, kind="Internal").ap()
    st_wk = nc.dram_tensor("st_wk", [D, D], BF16, kind="Internal").ap()
    st_wv = nc.dram_tensor("st_wv", [D, D], BF16, kind="Internal").ap()
    st_w1 = nc.dram_tensor("st_w1", [D, HID], BF16, kind="Internal").ap()
    st_w2 = nc.dram_tensor("st_w2", [HID, D], BF16, kind="Internal").ap()
    st_bp = nc.dram_tensor("st_bp", [2, D], F32, kind="Internal").ap()
    dbg = {}
    def dbg_out(name, shape, dtype=F32):
        global _dbg_names
        if name not in _dbg_names:
            _dbg_names.append(name)
        dbg[name] = nc.dram_tensor("dbg_" + name, shape, dtype, kind="ExternalOutput").ap()
        return dbg[name]
    def dbg_tap(name, shape, src_ap, sem, tok, dtype=F32):
        if not DEBUG:
            return
        d = dbg_out(name, shape, dtype)
        p.w("gpsimd", sem, tok)
        tk = p.emit("gpsimd", lambda e, d=d, src_ap=src_ap: e.dma_start(d, src_ap), inc="DG")
        for eng in ("sync", "vector", "scalar", "tensor"):
            p.w(eng, "DG", tk)

    ident = nc.alloc_sbuf_tensor("ident", [128, 128], F32)
    ident_bf = nc.alloc_sbuf_tensor("ident_bf", [128, 128], BF16)
    bqr = nc.alloc_sbuf_tensor("bqr", [128, 8], F32)
    bkr = nc.alloc_sbuf_tensor("bkr", [128, 8], F32)
    b1L = nc.alloc_sbuf_tensor("b1L", [128, 32], F32)
    b2qb = nc.alloc_sbuf_tensor("b2qb", [128, D], F32)
    bvb = nc.alloc_sbuf_tensor("bvb", [128, D], F32)
    stats = nc.alloc_sbuf_tensor("stats", [128, 32], F32)
    bscr = nc.alloc_sbuf_tensor("bscr", [128, 2 * D], F32)

    NELS = 94208
    AB = nc.alloc_sbuf_tensor("AB", [128, NELS], BF16)

    psA0 = nc.alloc_psum_tensor("psA0", [128, 512], F32)
    psA1 = nc.alloc_psum_tensor("psA1", [128, 512], F32)
    psB = nc.alloc_psum_tensor("psB", [128, 2048], F32)
    psPV = nc.alloc_psum_tensor("psPV", [128, 1024], F32)

    p = Prog(nc)
    A = Arena(AB, NELS)
    st = {}

    def emit_ladder(sc, src_ap, out_ap, W, npart, man, e_lo, e_hi, e_levels,
                    r_levels, maxval, signed, dep):
        ax, sg, l, v, dm, e, u, v2, r, pp, t1 = (
            sc["ax"][:npart, :W], sc["sg"][:npart, :W], sc["l"][:npart, :W],
            sc["v"][:npart, :W], sc["dm"][:npart, :W], sc["e"][:npart, :W],
            sc["u"][:npart, :W], sc["v2"][:npart, :W], sc["r"][:npart, :W],
            sc["pp"][:npart, :W], sc["t1"][:npart, :W])
        a1, a = sc["a1"][:npart, :W], sc["a"][:npart, :W]
        pv = sc["pv"][:npart, :W]
        if dep is not None:
            p.w("vector", dep[0], dep[1])
        if signed:
            p.emit("vector", lambda ee: ee.scalar_tensor_tensor(
                ax, src_ap, float(-1.0), src_ap, OP.mult, OP.max), inc="V")
            p.emit("vector", lambda ee: ee.tensor_single_scalar(
                sg, src_ap, float(0.0), OP.is_lt), inc="V")
            p.emit("vector", lambda ee: ee.tensor_scalar(
                sg, sg, float(-2.0), float(1.0), OP.mult, OP.add), inc="V")
        else:
            p.emit("vector", lambda ee: ee.tensor_copy(ax, src_ap), inc="V")
        tv = p.emit("vector", lambda ee: ee.tensor_single_scalar(
            l, ax, float(1e-7), OP.max), inc="V")
        p.w("scalar", "V", tv)
        tc = p.emit("scalar", lambda ee: ee.activation(l, l, AF.Ln), inc="C")
        p.w("vector", "C", tc)
        p.emit("vector", lambda ee: ee.tensor_scalar(
            e, l, float(INVLN2), float(e_lo), OP.mult, OP.max), inc="V")
        p.emit("vector", lambda ee: ee.tensor_single_scalar(
            e, e, float(e_hi), OP.min), inc="V")
        p.emit("vector", lambda ee: ee.tensor_single_scalar(
            t1, e, float(-e_lo), OP.add), inc="V")
        first_e = True
        for stp in e_levels:
            sv_ = t1 if first_e else v
            p.emit("vector", lambda ee, s=stp, sv_=sv_: ee.tensor_scalar(
                dm, sv_, float(s), float(-s), OP.is_ge, OP.mult), inc="V")
            p.emit("vector", lambda ee, sv_=sv_: ee.tensor_add(v, sv_, dm), inc="V")
            first_e = False
        p.emit("vector", lambda ee: ee.tensor_sub(e, t1, v), inc="V")
        p.emit("vector", lambda ee: ee.tensor_single_scalar(
            e, e, float(e_lo), OP.add), inc="V")
        if man == 3:
            p.emit("vector", lambda ee: ee.tensor_scalar(
                a1, e, float(-LN2HI), float(B3HI), OP.mult, OP.add), inc="V")
            tv = p.emit("vector", lambda ee: ee.scalar_tensor_tensor(
                a, e, float(-LN2LO), a1, OP.mult, OP.add), inc="V")
            p.w("scalar", "V", tv)
            p.emit("scalar", lambda ee: ee.activation(
                pv, a, AF.Exp, bias=B3LO), inc="C")
            tc = p.emit("scalar", lambda ee: ee.activation(
                pp, a, AF.Exp, bias=NB3LO, scale=f32(-1.0)), inc="C")
            p.w("vector", "C", tc)
            p.emit("vector", lambda ee: ee.tensor_mul(u, ax, pv), inc="V")
        else:
            p.emit("vector", lambda ee: ee.tensor_single_scalar(
                a1, e, float(-LN2HI), OP.mult), inc="V")
            tv = p.emit("vector", lambda ee: ee.scalar_tensor_tensor(
                a, e, float(-LN2LO), a1, OP.mult, OP.add), inc="V")
            p.w("scalar", "V", tv)
            p.emit("scalar", lambda ee: ee.activation(pv, a, AF.Exp), inc="C")
            tc = p.emit("scalar", lambda ee: ee.activation(
                pp, a, AF.Exp, bias=NB10, scale=f32(-1.0)), inc="C")
            p.w("vector", "C", tc)
            p.emit("vector", lambda ee: ee.tensor_single_scalar(
                dm, ax, float(1024.0), OP.mult), inc="V")
            p.emit("vector", lambda ee: ee.tensor_mul(u, dm, pv), inc="V")
        first = True
        for stp in r_levels:
            sv = u if first else v2
            p.emit("vector", lambda ee, s=stp, sv=sv: ee.tensor_scalar(
                dm, sv, float(s - 0.5), float(-s), OP.is_ge, OP.mult), inc="V")
            p.emit("vector", lambda ee, sv=sv: ee.tensor_add(v2, sv, dm), inc="V")
            first = False
        p.emit("vector", lambda ee: ee.tensor_sub(r, u, v2), inc="V")
        p.emit("vector", lambda ee: ee.tensor_mul(r, r, pp), inc="V")
        p.emit("vector", lambda ee: ee.tensor_scalar(
            t1, r, float(-maxval), float(0.0), OP.add, OP.max), inc="V")
        p.emit("vector", lambda ee: ee.tensor_scalar(
            t1, t1, float(60000.0), float(60000.0), OP.mult, OP.mult), inc="V")
        if signed:
            p.emit("vector", lambda ee: ee.tensor_add(r, r, t1), inc="V")
            return p.emit("vector", lambda ee: ee.tensor_mul(out_ap, r, sg), inc="V")
        return p.emit("vector", lambda ee: ee.tensor_add(out_ap, r, t1), inc="V")

    tr = {"ping": 0, "lastT": 0}

    def emit_transpose_f32(src_view, nblk, dst_fn, extra_t_waits=(), stage=None,
                           idt=None):
        idt = ident if idt is None else idt
        """Transpose nblk 128x128 f32 blocks of src_view; write bf16 via dst_fn(b).
        Each group of 4 blocks stages through a psum ping pair (default
        psA0/psA1). Gates: transposes wait V >= current count (all prior vector
        writes incl. src) + extra."""
        pair = stage if stage is not None else (psA0, psA1)
        last_v = None
        for g0 in range(0, nblk, 4):
            gn = min(4, nblk - g0)
            ps = pair[tr["ping"]]
            tr["ping"] = 1 - tr["ping"]
            p.w("tensor", "V", p.cnt["V"])
            p.w("tensor", "C", p.cnt["C"])
            for s_, tk in extra_t_waits:
                p.w("tensor", s_, tk)
            p.w("tensor", "G", st["ident"])
            tt = None
            for b in range(gn):
                blk = g0 + b
                tt = p.emit("tensor", lambda e, blk=blk, b=b, ps=ps, idt=idt: e.transpose(
                    ps[:, b * 128:(b + 1) * 128],
                    src_view[:, blk * 128:(blk + 1) * 128], idt[:]),
                    inc="T" if b == gn - 1 else None)
            tr["lastT"] = tt
            p.w("vector", "T", tt)
            for b in range(gn):
                dst = dst_fn(g0 + b)
                last_v = p.emit("vector", lambda e, dst=dst, b=b, ps=ps: e.tensor_copy(
                    dst, ps[:, b * 128:(b + 1) * 128]), inc="V")
        return last_v

    p.emit("gpsimd", lambda e: e.memset(ident[:], 0.0))
    st["ident"] = p.emit("gpsimd", lambda e: e.affine_select(
        out=ident[:], in_=ident[:], compare_op=OP.not_equal, fill=1.0,
        base=0, pattern=[[-1, 128]], channel_multiplier=1), inc="G")
    p.w("scalar", "G", st["ident"])
    p.emit("scalar", lambda e: e.copy(ident_bf[:], ident[:]), inc="C")

    A.mark()
    wchunk = A.alloc(2 * D, F32)
    wt_bf = A.alloc(8 * D, BF16)
    wr_bf = A.alloc(8 * D, BF16)
    prev_stash = 0

    def prep_attn_weight(wsrc, dst_stash, fold):
        nonlocal prev_stash
        p.w("vector", "DB", prev_stash)
        last_v = None
        tr_tok = [0, 0]
        for c in range(8):
            off = (c % 2) * D
            wv_ = wchunk[:, off:off + D]
            p.w("sync", "T", tr_tok[c % 2])
            tok_ld = p.emit("sync", lambda e, c=c, wv_=wv_: e.dma_start(
                wv_, wsrc[c * 128:(c + 1) * 128, :]), inc="DT%d" % (c % 2))
            last_v = emit_transpose_f32(
                wv_, 8, lambda b, c=c: wt_bf[:, b * D + c * 128: b * D + c * 128 + 128],
                extra_t_waits=(("DT%d" % (c % 2), tok_ld), ("DB", prev_stash)))
            tr_tok[c % 2] = tr["lastT"]
        if fold:
            p.w("vector", "V", last_v)
            for c in range(8):
                wtc = wt_bf[:, c * D:(c + 1) * D].rearrange("p (n two) -> p n two", two=2)
                wrc = wr_bf[:, c * D:(c + 1) * D].rearrange("p (n two) -> p n two", two=2)
                p.emit("vector", lambda e, wtc=wtc, wrc=wrc: e.tensor_sub(
                    wrc[:, :, 0:1], wtc[:, :, 0:1], wtc[:, :, 1:2]), inc="V")
                last_v = p.emit("vector", lambda e, wtc=wtc, wrc=wrc: e.tensor_add(
                    wrc[:, :, 1:2], wtc[:, :, 1:2], wtc[:, :, 0:1]), inc="V")
            src = wr_bf
        else:
            src = wt_bf
        p.w("sync", "V", last_v)
        tok = None
        for c in range(8):
            tok = p.emit("sync", lambda e, c=c, src=src: e.dma_start(
                dst_stash[c * 128:(c + 1) * 128, :], src[:, c * D:(c + 1) * D]), inc="DB")
        prev_stash = tok
        return tok

    prep_attn_weight(wqh, st_wq, True)
    prep_attn_weight(wkh, st_wk, True)
    st["attn_w_stash"] = prep_attn_weight(wvh, st_wv, False)
    A.release()

    A.mark()
    w1c = A.alloc(2 * D, F32)
    w1t_full = A.alloc(8 * HID, BF16)
    tr_tok1 = [0, 0]
    for c in range(32):
        off = (c % 2) * D
        wv_ = w1c[:, off:off + D]
        p.w("sync", "T", tr_tok1[c % 2])
        tok_ld = p.emit("sync", lambda e, c=c, wv_=wv_: e.dma_start(
            wv_, w1h[c * 128:(c + 1) * 128, :]), inc="DT%d" % (c % 2))
        tv = emit_transpose_f32(
            wv_, 8, lambda b, c=c: w1t_full[:, b * HID + c * 128: b * HID + c * 128 + 128],
            extra_t_waits=(("DT%d" % (c % 2), tok_ld),))
        tr_tok1[c % 2] = tr["lastT"]
    p.w("sync", "V", p.cnt["V"])
    p.w("sync", "DB", prev_stash)
    for r_ in range(8):
        prev_stash = p.emit("sync", lambda e, r_=r_: e.dma_start(
            st_w1[r_ * 128:(r_ + 1) * 128, :], w1t_full[:, r_ * HID:(r_ + 1) * HID]), inc="DB")
    A.release()

    q_tok2 = [("V", 0), ("V", 0)]
    A.mark()
    w2c = A.alloc(2 * HID, F32)
    w2q_f32 = A.alloc(HID, F32)
    w2t_full = A.alloc(32 * D, BF16)
    if QUANT == "ladder":
        scq = {k: A.alloc(HID // 2, F16) for k in
               ("ax", "sg", "l", "v", "dm", "e", "u", "v2", "r", "t1")}
        scq["a1"] = A.alloc(HID // 2, F32)
        scq["a"] = A.alloc(HID // 2, F32)
        scq["pp"] = A.alloc(HID // 2, F32)
        scq["pv"] = A.alloc(HID // 2, F32)
    else:
        w2q_f8 = A.alloc(HID // 2, F16).bitcast(F8)
    for c in range(8):
        off = (c % 2) * HID
        wv_ = w2c[:, off:off + HID]
        p.w("sync", q_tok2[c % 2][0], q_tok2[c % 2][1])
        tok_ld = p.emit("sync", lambda e, c=c, wv_=wv_: e.dma_start(
            wv_, w2h[c * 128:(c + 1) * 128, :]), inc="DT%d" % (c % 2))
        p.w("vector", "T", tr["lastT"])
        if QUANT == "ladder":
            for half in range(2):
                hw_ = HID // 2
                tq = emit_ladder(scq, wv_[:, half * hw_:(half + 1) * hw_],
                                 w2q_f32[:, half * hw_:(half + 1) * hw_], hw_,
                                 128, 3, -6.0, 3.99,
                                 (8, 4, 2, 1), (16, 8, 4, 2, 1), 240.0, True,
                                 ("DT%d" % (c % 2), tok_ld))
            dep = ("V", tq)
        else:
            p.w("scalar", "DT%d" % (c % 2), tok_ld)
            p.w("scalar", "T", tr["lastT"])
            p.emit("scalar", lambda e, wv_=wv_: e.copy(w2q_f8[:, :HID], wv_), inc="C")
            tq = p.emit("scalar", lambda e: e.copy(w2q_f32[:, :], w2q_f8[:, :HID]), inc="C")
            dep = ("C", tq)
        q_tok2[c % 2] = dep
        tv = emit_transpose_f32(
            w2q_f32[:, :], 32, lambda b, c=c: w2t_full[:, b * D + c * 128: b * D + c * 128 + 128],
            extra_t_waits=(dep,))
    p.w("sync", "V", p.cnt["V"])
    p.w("sync", "DB", prev_stash)
    for r_ in range(32):
        prev_stash = p.emit("sync", lambda e, r_=r_: e.dma_start(
            st_w2[r_ * 128:(r_ + 1) * 128, :], w2t_full[:, r_ * D:(r_ + 1) * D]), inc="DB")
    A.release()
    st["w_stash_done"] = prev_stash

    tok_bv = p.emit("sync", lambda e: e.dma_start(bscr[1:2, 0:D], bvh), inc="DS")

    def rope_fold_row(dep):
        p.w("vector", "DS", dep)
        sr = bscr[0:1, 0:D].rearrange("p (n two) -> p n two", two=2)
        dr = bscr[0:1, D:2 * D].rearrange("p (n two) -> p n two", two=2)
        p.emit("vector", lambda e: e.tensor_sub(
            dr[:, :, 0:1], sr[:, :, 0:1], sr[:, :, 1:2]), inc="V")
        return p.emit("vector", lambda e: e.tensor_add(
            dr[:, :, 1:2], sr[:, :, 1:2], sr[:, :, 0:1]), inc="V")

    tok_bq = p.emit("sync", lambda e: e.dma_start(bscr[0:1, 0:D], bqh), inc="DS")
    tqf = rope_fold_row(tok_bq)
    p.w("sync", "V", tqf)
    tbp0 = p.emit("sync", lambda e: e.dma_start(st_bp[0:1, :], bscr[0:1, D:2 * D]), inc="DS")
    p.w("sync", "DS", tbp0)
    tok_bk = p.emit("sync", lambda e: e.dma_start(bscr[0:1, 0:D], bkh), inc="DS")
    tqf = rope_fold_row(tok_bk)
    p.w("sync", "V", tqf)
    tbp1 = p.emit("sync", lambda e: e.dma_start(st_bp[1:2, :], bscr[0:1, D:2 * D]), inc="DS")
    p.w("sync", "DS", tbp1)
    p.emit("sync", lambda e: e.dma_start(
        bqr[:], st_bp[0:1, :].rearrange("one (c p) -> (one p) c", p=128),
        allow_slow_non_contiguous=True), inc="DS")
    st["bqr"] = p.emit("sync", lambda e: e.dma_start(
        bkr[:], st_bp[1:2, :].rearrange("one (c p) -> (one p) c", p=128),
        allow_slow_non_contiguous=True), inc="DS")
    st["b1L"] = p.emit("sync", lambda e: e.dma_start(
        b1L[:], b1h[0:1, :].rearrange("one (c p) -> (one p) c", p=128),
        allow_slow_non_contiguous=True), inc="DS")
    p.w("sync", "DS", tbp1)
    tok_b2 = p.emit("sync", lambda e: e.dma_start(bscr[0:1, 0:D], b2h), inc="DS")
    if QUANT == "ladder":
        A.mark()
        scb = {k: A.alloc(D, F16) for k in
               ("ax", "sg", "l", "v", "dm", "e", "u", "v2", "r", "t1")}
        scb["a1"] = A.alloc(D, F32)
        scb["a"] = A.alloc(D, F32)
        scb["pp"] = A.alloc(D, F32)
        scb["pv"] = A.alloc(D, F32)
        tok_b2q = emit_ladder(scb, bscr[0:1, 0:D], bscr[0:1, D:2 * D], D, 1, 10,
                              -14.0, 3.99, (16, 8, 4, 2, 1),
                              (2048, 1024, 512, 256, 128, 64, 32, 16, 8, 4, 2, 1),
                              65504.0, True, ("DS", tok_b2))
        A.release()
        sem_b2 = ("V", tok_b2q)
    else:
        p.w("scalar", "DS", tok_b2)
        b2f16 = b2qb[0:1, 0:512].bitcast(F16)
        p.emit("scalar", lambda e: e.copy(b2f16, bscr[0:1, 0:D]), inc="C")
        tok_b2q = p.emit("scalar", lambda e: e.copy(bscr[0:1, D:2 * D], b2f16), inc="C")
        sem_b2 = ("C", tok_b2q)

    def bcast_rows(dst, src_row, dep_sem, dep_tok):
        p.w("gpsimd", dep_sem, dep_tok)
        tok = p.emit("gpsimd", lambda e: e.dma_start(dst[0:1, :], src_row), inc="DG")
        n = 1
        while n < 128:
            p.w("gpsimd", "DG", tok)
            m = min(n, 128 - n)
            tok = p.emit("gpsimd", lambda e, n=n, m=m: e.dma_start(
                dst[n:n + m, :], dst[0:m, :]), inc="DG")
            n += m
        return tok

    st["b2qb"] = bcast_rows(b2qb, bscr[0:1, D:2 * D], sem_b2[0], sem_b2[1])
    st["bvb"] = bcast_rows(bvb, bscr[1:2, 0:D], "DS", tok_b2)

    def phase_barrier():
        for nm in ("T", "V", "C", "DG", "DB", "DW0", "DW1",
                   "DX0", "DX1", "DX2", "DX3", "DT0", "DT1"):
            p.w("sync", nm, p.cnt[nm])

    guard = {}
    last_out_store = 0
    x1_store = 0

    for seq in range(NSEQ):
        base = seq * S_LEN
        A.mark()
        wq_s = A.alloc(8 * D, BF16)
        wk_s = A.alloc(8 * D, BF16)
        q_fm = A.alloc(8 * S_LEN, BF16)
        k_fm = A.alloc(8 * S_LEN, BF16)
        v_tm = A.alloc(16 * D, BF16)
        xt = A.alloc(2 * D, F32)
        lnt = A.alloc(2 * D, F32)
        tpc = A.alloc(2 * 512, F32)
        etmp = A.alloc(2 * 512, F32)
        A.mark()
        ln1 = A.alloc(8 * S_LEN, BF16)

        tok_wq_ld = [None] * 8
        tok_wk_ld = [None] * 8
        phase_barrier()
        xt_busy = [0, 0]
        ln_ready = None
        ln_evac = []
        tp_busy = [0, 0]
        etmp_busy = [0, 0]
        qk_last_v = 0
        trstage = (psB[:, 1024:1280].bitcast(BF16), psB[:, 1536:1792].bitcast(BF16))
        lnb = bscr[:, 0:1024].bitcast(BF16)

        def a2_strip(sc_):
            nonlocal qk_last_v
            for c in range(8):
                gi = sc_ * 8 + c
                tpp = gi % 2
                tview = tpc[:, tpp * 512:(tpp + 1) * 512]
                p.w("sync", "V", tp_busy[tpp])
                tok_t = p.emit("sync", lambda e, c=c, sc_=sc_, tview=tview: e.dma_start(
                    tview, tth[c * 128:(c + 1) * 128, sc_ * 512:(sc_ + 1) * 512]), inc="DT%d" % tpp)
                for ping, (wv_, bias_t, dst) in enumerate(
                        ((wq_s, bqr, q_fm), (wk_s, bkr, k_fm))):
                    dep2 = gi % 2
                    if ping == 0:
                        ps_sl = (psA0 if dep2 == 0 else psA1)[:, 0:512]
                    else:
                        ps_sl = psB[:, dep2 * 512:(dep2 + 1) * 512]
                    ev = etmp[:, ping * 512:(ping + 1) * 512]
                    p.w("tensor", "V", ln_evac[sc_ * 4 + 3])
                    p.w("tensor", "DB", tok_wk_ld[7])
                    p.w("tensor", "C", guard.get(("qk", ping, dep2), 0))
                    tmm = None
                    for kk in range(8):
                        tmm = p.emit("tensor", lambda e, wv_=wv_, c=c, sc_=sc_, kk=kk, ps_sl=ps_sl: e.matmul(
                            ps_sl,
                            wv_[:, kk * D + c * 128: kk * D + c * 128 + 128],
                            ln1[:, kk * S_LEN + sc_ * 512: kk * S_LEN + (sc_ + 1) * 512],
                            start=(kk == 0), stop=(kk == 7)),
                            inc="T" if kk == 7 else None)
                    p.w("scalar", "T", tmm)
                    p.w("scalar", "V", etmp_busy[ping])
                    p.w("scalar", "DS", st["bqr"])
                    tev = p.emit("scalar", lambda e, ps_sl=ps_sl, ev=ev, bias_t=bias_t, c=c: e.activation(
                        ev, ps_sl, AF.Identity, bias=bias_t[:, c:c + 1]), inc="C")
                    guard[("qk", ping, dep2)] = tev
                    p.w("vector", "C", tev)
                    p.w("vector", "DT%d" % tpp, tok_t)
                    qk_last_v = p.emit("vector", lambda e, dst=dst, c=c, sc_=sc_, ev=ev, tview=tview: e.tensor_mul(
                        dst[:, c * S_LEN + sc_ * 512: c * S_LEN + (sc_ + 1) * 512],
                        ev, tview), inc="V")
                    etmp_busy[ping] = qk_last_v
                tp_busy[tpp] = qk_last_v

        for i in range(16):
            pi = i % 2
            xv = xt[:, pi * D:pi * D + D]
            p.w("sync", "V", xt_busy[pi])
            tok_x = p.emit("sync", lambda e, i=i, xv=xv, base=base: e.dma_start(
                xv, xh[base + i * 128: base + (i + 1) * 128, :]), inc="DX%d" % pi)
            if i == 1:
                p.w("sync", "DB", st["w_stash_done"])
                for c in range(8):
                    tok_wq_ld[c] = p.emit("sync", lambda e, c=c: e.dma_start(
                        wq_s[:, c * D:(c + 1) * D], st_wq[c * 128:(c + 1) * 128, :]), inc="DB")
                for c in range(8):
                    tok_wk_ld[c] = p.emit("sync", lambda e, c=c: e.dma_start(
                        wk_s[:, c * D:(c + 1) * D], st_wk[c * 128:(c + 1) * 128, :]), inc="DB")
                if seq == 0:
                    dbg_tap("wq_s", [128, 8 * D], wq_s[:, :], "DB", tok_wq_ld[7], BF16)
            p.w("vector", "DX%d" % pi, tok_x)
            p.w("vector", "T", guard.get(("lnt", i % 2), 0))
            sm = (i % 4) * 8
            lv_ = lnt[:, (i % 2) * D:(i % 2) * D + D]
            p.emit("vector", lambda e, xv=xv, lv_=lv_: e.tensor_mul(
                lv_, xv, xv), inc="V")
            p.emit("vector", lambda e, xv=xv, sm=sm: e.reduce_sum(
                stats[:, sm:sm + 1], xv, AX.X), inc="V")
            tvv = p.emit("vector", lambda e, sm=sm, lv_=lv_: e.reduce_sum(
                stats[:, sm + 1:sm + 2], lv_, AX.X), inc="V")
            p.w("scalar", "V", tvv)
            p.emit("scalar", lambda e, sm=sm: e.activation(
                stats[:, sm + 6:sm + 7], stats[:, sm + 1:sm + 2], AF.Identity,
                scale=f32(1.0 / D), bias=f32(1e-5)), inc="C")
            tcs = p.emit("scalar", lambda e, sm=sm: e.mul(
                stats[:, sm + 2:sm + 3], stats[:, sm:sm + 1], f32(1.0 / D)), inc="C")
            p.w("vector", "C", tcs)
            tvq = p.emit("vector", lambda e, sm=sm: e.tensor_mul(
                stats[:, sm + 3:sm + 4], stats[:, sm + 2:sm + 3], stats[:, sm + 2:sm + 3]), inc="V")
            p.w("scalar", "V", tvq)
            tcs = p.emit("scalar", lambda e, sm=sm: e.activation(
                stats[:, sm + 5:sm + 6], stats[:, sm + 3:sm + 4], AF.Sqrt,
                scale=f32(-1.0), bias=stats[:, sm + 6:sm + 7]), inc="C")
            p.w("vector", "C", tcs)
            p.emit("vector", lambda e, sm=sm: e.reciprocal(
                stats[:, sm + 4:sm + 5], stats[:, sm + 5:sm + 6]), inc="V")
            p.emit("vector", lambda e, xv=xv, sm=sm, lv_=lv_: e.tensor_single_scalar(
                lv_, xv, stats[:, sm + 2:sm + 3], OP.subtract), inc="V")
            if i == 0:
                p.w("vector", "DG", max(st["bvb"], st["b2qb"]))
            lb_ = lnb[:, (i % 2) * D:(i % 2) * D + D]
            tvn = p.emit("vector", lambda e, sm=sm, lv_=lv_, lb_=lb_: e.tensor_single_scalar(
                lb_, lv_, stats[:, sm + 4:sm + 5], OP.mult), inc="V")
            xt_busy[pi] = tvn
            if seq == 0 and i == 0:
                dbg_tap("xv0", [128, D], xv, "V", tvn)
                dbg_tap("lnt0", [128, D], lb_, "V", tvn, BF16)
                dbg_tap("stats0", [128, 32], stats[:], "V", tvn)
            ln_ready = emit_transpose_f32(
                lb_, 8,
                lambda b, i=i: ln1[:, b * S_LEN + i * 128: b * S_LEN + i * 128 + 128],
                stage=trstage, idt=ident_bf)
            guard[("lnt", i % 2)] = tr["lastT"]
            ln_evac.append(ln_ready)
            if i % 4 == 3:
                a2_strip(i // 4)

        if seq == 0:
            dbg_tap("ln1c0", [128, S_LEN], ln1[:, 0:S_LEN], "V", ln_ready, BF16)
        if seq == 0:
            dbg_tap("qc0", [128, S_LEN], q_fm[:, 0:S_LEN], "V", qk_last_v, BF16)
            dbg_tap("kc0", [128, S_LEN], k_fm[:, 0:S_LEN], "V", qk_last_v, BF16)
        wv_s = wq_s
        tok_wv_ld = [None] * 8
        p.w("sync", "V", qk_last_v)
        for c in range(8):
            tok_wv_ld[c] = p.emit("sync", lambda e, c=c: e.dma_start(
                wv_s[:, c * D:(c + 1) * D], st_wv[c * 128:(c + 1) * 128, :]), inc="DB")
        if seq == 0:
            dbg_tap("wv_s", [128, 8 * D], wv_s[:, :], "DB", tok_wv_ld[7], BF16)
        v_last = 0
        for i in range(16):
            for n in range(2):
                ping = (i * 2 + n) % 2
                ps_sl = psB[:, ping * 512:(ping + 1) * 512]
                if i == 0:
                    p.w("tensor", "C", p.cnt["C"])
                p.w("tensor", "V", guard.get(("v", ping), 0))
                p.w("tensor", "DB", tok_wv_ld[7])
                tmm = None
                for kk in range(8):
                    tmm = p.emit("tensor", lambda e, i=i, n=n, kk=kk, ps_sl=ps_sl: e.matmul(
                        ps_sl,
                        ln1[:, kk * S_LEN + i * 128: kk * S_LEN + i * 128 + 128],
                        wv_s[:, kk * D + n * 512: kk * D + (n + 1) * 512],
                        start=(kk == 0), stop=(kk == 7)),
                        inc="T" if kk == 7 else None)
                p.w("vector", "T", tmm)
                v_last = p.emit("vector", lambda e, i=i, n=n, ps_sl=ps_sl: e.tensor_copy(
                    v_tm[:, i * D + n * 512: i * D + (n + 1) * 512], ps_sl), inc="V")
                guard[("v", ping)] = v_last

        if seq == 0:
            dbg_tap("vt0", [128, D], v_tm[:, 0:D], "V", v_last, BF16)
        A.release()
        probs = A.alloc(S_LEN, BF16)
        probsT = A.alloc(2 * S_LEN, BF16)
        x1t = A.alloc(2 * D, F32)
        xbv = A.alloc(D, F32)

        x1t_busy = [0, 0]
        evac_last = {}
        exp_tok = {}
        score_tok = {}
        pv_tok = {}
        tr_ping_guard = [0, 0]
        for ii in range(18):
            if ii >= 2:
                i2 = ii - 2
                par2 = i2 % 2
                p.w("tensor", "V", max(evac_last[i2], guard.get("pv", 0)))
                tpv = None
                for tk in range(16):
                    for n in range(2):
                        tpv = p.emit("tensor", lambda e, tk=tk, n=n, par2=par2: e.matmul(
                            psPV[:, n * 512:(n + 1) * 512],
                            probsT[:, par2 * S_LEN + tk * 128: par2 * S_LEN + (tk + 1) * 128],
                            v_tm[:, tk * D + n * 512: tk * D + (n + 1) * 512],
                            start=(tk == 0), stop=(tk == 15)),
                            inc="T" if tk == 15 and n == 1 else None)
                pv_tok[i2] = tpv
            if 1 <= ii <= 16:
                i1 = ii - 1
                par1 = i1 % 2
                sm1 = 16 + (i1 % 4) * 4
                ev = None
                for g in range(4):
                    pg = tr["ping"]
                    ps_bf = (psA0 if pg == 0 else psA1)[:, 0:256].bitcast(BF16)
                    tr["ping"] = 1 - pg
                    p.w("tensor", "C", exp_tok[i1])
                    p.w("tensor", "V", tr_ping_guard[pg])
                    p.w("tensor", "G", st["ident"])
                    tt = None
                    for b in range(4):
                        blk = g * 4 + b
                        tt = p.emit("tensor", lambda e, blk=blk, b=b, ps_bf=ps_bf: e.transpose(
                            ps_bf[:, b * 128:(b + 1) * 128],
                            probs[:, blk * 128:(blk + 1) * 128], ident_bf[:]),
                            inc="T" if b == 3 else None)
                    tr["lastT"] = tt
                    p.w("vector", "T", tt)
                    if g == 1:
                        p.w("vector", "C", exp_tok[i1])
                        p.emit("vector", lambda e, sm1=sm1: e.reduce_sum(
                            stats[:, sm1 + 2:sm1 + 3], probs[:, :], AX.X), inc="V")
                    if g == 3:
                        p.emit("vector", lambda e, sm1=sm1: e.reciprocal(
                            stats[:, sm1 + 3:sm1 + 4], stats[:, sm1 + 2:sm1 + 3]), inc="V")
                    ev = p.emit("vector", lambda e, g=g, par1=par1, ps_bf=ps_bf: e.tensor_copy(
                        probsT[:, par1 * S_LEN + g * 512: par1 * S_LEN + (g + 1) * 512],
                        ps_bf[:, 0:512]), inc="V")
                    tr_ping_guard[pg] = ev
                evac_last[i1] = ev
                if seq == 0 and i1 == 0:
                    dbg_tap("probs0", [128, S_LEN], probs[:, :], "V", ev, BF16)
            if ii < 16:
                p.w("tensor", "V", v_last)
                p.w("tensor", "C", exp_tok.get(ii - 1, 0))
                tmm = None
                for kk in range(8):
                    for tc_ in range(4):
                        tmm = p.emit("tensor", lambda e, ii=ii, kk=kk, tc_=tc_: e.matmul(
                            psB[:, tc_ * 512:(tc_ + 1) * 512],
                            q_fm[:, kk * S_LEN + ii * 128: kk * S_LEN + ii * 128 + 128],
                            k_fm[:, kk * S_LEN + tc_ * 512: kk * S_LEN + (tc_ + 1) * 512],
                            start=(kk == 0), stop=(kk == 7)),
                            inc="T" if kk == 7 and tc_ == 3 else None)
                score_tok[ii] = tmm
            if ii >= 2:
                i2 = ii - 2
                pi = i2 % 2
                sm2 = 16 + (i2 % 4) * 4
                xv = xt[:, pi * D:pi * D + D]
                p.w("sync", "V", xt_busy[pi])
                tok_x = p.emit("sync", lambda e, i2=i2, xv=xv, base=base: e.dma_start(
                    xv, xh[base + i2 * 128: base + (i2 + 1) * 128, :]), inc="DX%d" % pi)
                p.w("vector", "DX%d" % pi, tok_x)
                p.w("vector", "DG", st["bvb"])
                tb = p.emit("vector", lambda e, xv=xv: e.tensor_add(
                    xbv[:, :], xv, bvb[:]), inc="V")
                xt_busy[pi] = tb
                p.w("vector", "DG", x1t_busy[pi])
                p.w("vector", "T", pv_tok[i2])
                x1v = x1t[:, pi * D:pi * D + D]
                tx1 = p.emit("vector", lambda e, x1v=x1v, sm2=sm2: e.scalar_tensor_tensor(
                    x1v, psPV[:, 0:1024], stats[:, sm2 + 3:sm2 + 4], xbv[:, :],
                    OP.mult, OP.add), inc="V")
                guard["pv"] = tx1
                p.w("gpsimd", "V", tx1)
                x1_store = p.emit("gpsimd", lambda e, i2=i2, x1v=x1v, base=base: e.dma_start(
                    x1h[base + i2 * 128: base + (i2 + 1) * 128, :], x1v), inc="DG")
                x1t_busy[pi] = x1_store
                if seq == 0 and i2 == 0:
                    dbg_tap("x1t0", [128, D], x1v, "V", guard["pv"])
            if ii < 16:
                sm = 16 + (ii % 4) * 4
                p.w("vector", "T", score_tok[ii])
                tnm = p.emit("vector", lambda e, sm=sm: e.reduce_max(
                    stats[:, sm:sm + 1], psB[:, 0:2048], AX.X, negate=True), inc="V")
                p.w("scalar", "V", tnm)
                exp_tok[ii] = p.emit("scalar", lambda e, sm=sm: e.activation(
                    probs[:, :], psB[:, 0:2048], AF.Exp, bias=stats[:, sm:sm + 1],
                    scale=f32(1.0)), inc="C")

        A.release()

        A.mark()
        w1t_s = A.alloc(8 * HID, BF16)
        hq = A.alloc(32 * 512, BF16)
        ln2s = [A.alloc(8 * 512, BF16), A.alloc(8 * 512, BF16)]
        w2t_win = A.alloc(4 * D, BF16)
        xt2 = A.alloc(2 * D, F32)
        xt3 = A.alloc(2 * D, F32)
        lnt2 = A.alloc(4 * D, F32)
        y0 = A.alloc(2 * D, F32)
        outt = A.alloc(2 * D, F32)
        if QUANT == "ladder":
            scl = {k: A.alloc(D, F16) for k in
                   ("ax", "sg", "l", "v", "dm", "e", "u", "v2", "r", "t1")}
            scl["a1"] = A.alloc(D, F32)
            scl["a"] = A.alloc(D, F32)
            scl["pp"] = A.alloc(D, F32)
            scl["pv"] = A.alloc(D, F32)
            hsrc = A.alloc(2 * 512, F32)
            yq = A.alloc(D, F32)
        else:
            h_f8 = A.alloc(512 // 2, F16).bitcast(F8)
            y16 = A.alloc(D, F16)

        tok_w1_ld = [None] * 8
        phase_barrier()

        xt2_busy = [0, 0]
        xt3_busy = [0, 0]
        out_busy = [0, 0]
        ln2_ready_ch = {}

        lnb2 = bscr[:, 0:2048].bitcast(BF16)

        def b1_chain(ch):
            crow = base + ch * 512
            for ii in range(4):
                pi = ii % 2
                xv = xt2[:, pi * D:pi * D + D]
                p.w("sync", "V", xt2_busy[pi])
                p.w("sync", "DG", x1_store)
                tok_x = p.emit("sync", lambda e, ii=ii, xv=xv, crow=crow: e.dma_start(
                    xv, x1h[crow + ii * 128: crow + (ii + 1) * 128, :]), inc="DX%d" % (2 + pi))
                p.w("vector", "DX%d" % (2 + pi), tok_x)
                p.w("vector", "T", guard.get(("lnt2", ii), 0))
                sm = ii * 8
                lv_ = lnt2[:, ii * D:ii * D + D]
                p.emit("vector", lambda e, xv=xv, lv_=lv_: e.tensor_mul(
                    lv_, xv, xv), inc="V")
                p.emit("vector", lambda e, xv=xv, sm=sm: e.reduce_sum(
                    stats[:, sm:sm + 1], xv, AX.X), inc="V")
                tvv = p.emit("vector", lambda e, sm=sm, lv_=lv_: e.reduce_sum(
                    stats[:, sm + 1:sm + 2], lv_, AX.X), inc="V")
                p.w("scalar", "V", tvv)
                p.emit("scalar", lambda e, sm=sm: e.activation(
                    stats[:, sm + 6:sm + 7], stats[:, sm + 1:sm + 2], AF.Identity,
                    scale=f32(1.0 / D), bias=f32(1e-5)), inc="C")
                tcs = p.emit("scalar", lambda e, sm=sm: e.mul(
                    stats[:, sm + 2:sm + 3], stats[:, sm:sm + 1], f32(1.0 / D)), inc="C")
                p.w("vector", "C", tcs)
                tvq = p.emit("vector", lambda e, sm=sm: e.tensor_mul(
                    stats[:, sm + 3:sm + 4], stats[:, sm + 2:sm + 3], stats[:, sm + 2:sm + 3]), inc="V")
                p.w("scalar", "V", tvq)
                tcs = p.emit("scalar", lambda e, sm=sm: e.activation(
                    stats[:, sm + 5:sm + 6], stats[:, sm + 3:sm + 4], AF.Sqrt,
                    scale=f32(-1.0), bias=stats[:, sm + 6:sm + 7]), inc="C")
                p.w("vector", "C", tcs)
                p.emit("vector", lambda e, sm=sm: e.reciprocal(
                    stats[:, sm + 4:sm + 5], stats[:, sm + 5:sm + 6]), inc="V")
                p.emit("vector", lambda e, xv=xv, sm=sm, lv_=lv_: e.tensor_single_scalar(
                    lv_, xv, stats[:, sm + 2:sm + 3], OP.subtract), inc="V")
                lb_ = lnb2[:, ii * D:ii * D + D]
                tvn = p.emit("vector", lambda e, sm=sm, lv_=lv_, lb_=lb_: e.tensor_single_scalar(
                    lb_, lv_, stats[:, sm + 4:sm + 5], OP.mult), inc="V")
                xt2_busy[pi] = tvn

        def b1_tr(ch):
            dstb = ln2s[ch % 2]
            last = None
            for ii in range(4):
                lb_ = lnb2[:, ii * D:ii * D + D]
                last = emit_transpose_f32(
                    lb_, 8,
                    lambda b, ii=ii, dstb=dstb: dstb[:, b * 512 + ii * 128: b * 512 + ii * 128 + 128],
                    stage=(psA0[:, 0:256].bitcast(BF16), psA1[:, 0:256].bitcast(BF16)),
                    idt=ident_bf)
                guard[("lnt2", ii)] = tr["lastT"]
            ln2_ready_ch[ch] = last

        b1_chain(0)
        p.w("sync", "DB", st["w_stash_done"])
        for c in range(8):
            tok_w1_ld[c] = p.emit("sync", lambda e, c=c: e.dma_start(
                w1t_s[:, c * HID:(c + 1) * HID], st_w1[c * 128:(c + 1) * 128, :]), inc="DB")
        b1_tr(0)
        for ch in range(4):
            crow = base + ch * 512
            lnb = ln2s[ch % 2]
            ln2_ready = ln2_ready_ch[ch]
            if seq == 0 and ch == 0:
                dbg_tap("ln2c0", [128, 512], lnb[:, 0:512], "V", ln2_ready, BF16)
            hq_ready_v = 0
            last_relu_c = 0
            hq_busy = [0, 0]
            for jt in range(32):
                ping = jt % 2
                ps = psA0 if ping == 0 else psA1
                p.w("tensor", "C", guard.get(("f1", ping), 0))
                p.w("tensor", "V", ln2_ready)
                p.w("tensor", "DB", tok_w1_ld[7])
                tmm = None
                for kk in range(8):
                    tmm = p.emit("tensor", lambda e, jt=jt, kk=kk, ps=ps, lnb=lnb: e.matmul(
                        ps[:, 0:512],
                        w1t_s[:, kk * HID + jt * 128: kk * HID + jt * 128 + 128],
                        lnb[:, kk * 512:(kk + 1) * 512],
                        start=(kk == 0), stop=(kk == 7)),
                        inc="T" if kk == 7 else None)
                p.w("scalar", "T", tmm)
                p.w("scalar", "DS", st["b1L"])
                if QUANT == "ladder":
                    hv = hsrc[:, ping * 512:(ping + 1) * 512]
                    p.w("scalar", "V", hq_busy[ping])
                    trl = p.emit("scalar", lambda e, ps=ps, hv=hv, jt=jt: e.activation(
                        hv, ps[:, 0:512], AF.Relu, bias=b1L[:, jt:jt + 1]), inc="C")
                    guard[("f1", ping)] = trl
                    last_relu_c = trl
                    hq_ready_v = emit_ladder(
                        scl, hv, hq[:, jt * 512:(jt + 1) * 512], 512, 128, 3,
                        -6.0, 7.99, (8, 4, 2, 1), (16, 8, 4, 2, 1), 240.0,
                        False, ("C", trl))
                    hq_busy[ping] = hq_ready_v
                else:
                    p.emit("scalar", lambda e, ps=ps, jt=jt: e.activation(
                        h_f8[:, 0:512], ps[:, 0:512], AF.Relu, bias=b1L[:, jt:jt + 1]), inc="C")
                    trl = p.emit("scalar", lambda e, jt=jt: e.copy(
                        hq[:, jt * 512:(jt + 1) * 512], h_f8[:, 0:512]), inc="C")
                    guard[("f1", ping)] = trl
                    last_relu_c = trl

            if seq == 0 and ch == 0:
                if QUANT == "ladder":
                    dbg_tap("hq0", [128, 512], hq[:, 0:512], "V", hq_ready_v, BF16)
                else:
                    dbg_tap("hq0", [128, 512], hq[:, 0:512], "C", last_relu_c, BF16)
            if ch < 3:
                b1_chain(ch + 1)
            yps = [psB[:, 0:1024], psB[:, 1024:2048], psPV[:, 0:1024]]
            if seq == 0 and ch == 0:
                dbg_tap("hqall", [128, 32 * 512], hq[:, :], "C", last_relu_c, BF16)
                dbg_tap("b2qbt", [128, D], b2qb[:], "DG", st["b2qb"])
            w2_busy = [0, guard.get("b3_prev_t", 0)]
            w2_tok = [0, 0]
            tff = None
            for jkp in range(17):
                if jkp < 16:
                    wp = jkp % 2
                    p.w("sync", "T", w2_busy[wp] if jkp > 0 else guard.get("b3_prev_t", 0))
                    for tl in range(2):
                        w2_tok[wp] = p.emit("sync", lambda e, jkp=jkp, wp=wp, tl=tl: e.dma_start(
                            w2t_win[:, (wp * 2 + tl) * D:(wp * 2 + tl + 1) * D],
                            st_w2[jkp * 256 + tl * 128: jkp * 256 + (tl + 1) * 128, :]),
                            inc="DW%d" % wp)
                if jkp < 1:
                    continue
                for jk in range((jkp - 1) * 2, jkp * 2):
                    p.w("tensor", "DW%d" % ((jk // 2) % 2), w2_tok[(jk // 2) % 2])
                    wwin = w2t_win[:, (jk % 4) * D:((jk % 4) + 1) * D]
                    if jk == 0:
                        p.w("tensor", "V", max(hq_ready_v, guard.get("ffn2", 0)))
                        p.w("tensor", "C", last_relu_c)
                    for ii in range(4):
                        for n in range(2):
                            if ii < 3:
                                dst = yps[ii][:, n * 512:(n + 1) * 512]
                            else:
                                dst = (psA0 if n == 0 else psA1)[:, 0:512]
                            tff = p.emit("tensor", lambda e, jk=jk, dst=dst, ii=ii, n=n, wwin=wwin: e.matmul(
                                dst,
                                hq[:, jk * 512 + ii * 128: jk * 512 + ii * 128 + 128],
                                wwin[:, n * 512:(n + 1) * 512],
                                start=(jk == 0), stop=(jk == 31)),
                                inc="T" if ii == 3 and n == 1 else None)
                    w2_busy[(jk // 2) % 2] = tff
            guard["b3_prev_t"] = tff
            guard[("f1", 0)] = 0
            guard[("f1", 1)] = 0
            yv3 = y0[:, D:2 * D]
            p.w("vector", "T", tff)
            p.w("vector", "DG", st["b2qb"])
            p.w("vector", "V", guard.get(("y0", 1), 0))
            p.emit("vector", lambda e, yv3=yv3: e.tensor_add(
                yv3[:, 0:512], psA0[:, 0:512], b2qb[:, 0:512]), inc="V")
            ty0_3 = p.emit("vector", lambda e, yv3=yv3: e.tensor_add(
                yv3[:, 512:1024], psA1[:, 0:512], b2qb[:, 512:1024]), inc="V")
            if ch < 3:
                b1_tr(ch + 1)
            y_ep = 0
            for ii in (3, 0, 1, 2):
                pi = ii % 2
                xv = xt3[:, pi * D:pi * D + D]
                p.w("sync", "V", xt3_busy[pi])
                tok_x = p.emit("sync", lambda e, ii=ii, xv=xv, crow=crow: e.dma_start(
                    xv, x1h[crow + ii * 128: crow + (ii + 1) * 128, :]), inc="DX%d" % pi)
                yv = y0[:, pi * D:pi * D + D]
                if ii == 3:
                    ty0 = ty0_3
                else:
                    p.w("vector", "T", tff)
                    p.w("vector", "DG", st["b2qb"])
                    p.w("vector", "V", guard.get(("y0", pi), 0))
                    ty0 = p.emit("vector", lambda e, ii=ii, yv=yv: e.tensor_add(
                        yv, yps[ii], b2qb[:]), inc="V")
                ov = outt[:, pi * D:pi * D + D]
                if seq == 0 and ch == 0 and ii == 0:
                    dbg_tap("y00", [128, D], yv, "V", ty0)
                p.w("vector", "DG", out_busy[pi])
                if QUANT == "ladder":
                    emit_ladder(scl, yv, yq[:, 0:D], D, 128, 10, -14.0, 3.99,
                                (16, 8, 4, 2, 1),
                                (2048, 1024, 512, 256, 128, 64, 32, 16, 8, 4, 2, 1),
                                65504.0, True, ("V", ty0))
                    p.w("vector", "DX%d" % pi, tok_x)
                    tout = p.emit("vector", lambda e, ov=ov, xv=xv: e.tensor_add(
                        ov, yq[:, 0:D], xv), inc="V")
                else:
                    p.w("scalar", "V", ty0)
                    tqc = p.emit("scalar", lambda e, yv=yv: e.copy(y16[:, 0:D], yv), inc="C")
                    p.w("vector", "C", tqc)
                    p.w("vector", "DX%d" % pi, tok_x)
                    tout = p.emit("vector", lambda e, ov=ov, xv=xv: e.tensor_add(
                        ov, y16[:, 0:D], xv), inc="V")
                guard[("y0", pi)] = tout
                xt3_busy[pi] = tout
                y_ep = tout
                p.w("gpsimd", "V", tout)
                last_out_store = p.emit("gpsimd", lambda e, ii=ii, ov=ov, crow=crow: e.dma_start(
                    outh[crow + ii * 128: crow + (ii + 1) * 128, :], ov), inc="DG")
                out_busy[pi] = last_out_store
            guard["ffn2"] = y_ep

        A.release()

    p.w("gpsimd", "DG", last_out_store)

    with nc.Block() as block:
        @block.sync
        def _(e):
            for fn in p.stream["sync"]:
                fn(e)

        @block.vector
        def _(e):
            for fn in p.stream["vector"]:
                fn(e)

        @block.scalar
        def _(e):
            for fn in p.stream["scalar"]:
                fn(e)

        @block.gpsimd
        def _(e):
            for fn in p.stream["gpsimd"]:
                fn(e)

        @block.tensor
        def _(e):
            for fn in p.stream["tensor"]:
                fn(e)

    return nc


def kernel(L0_attn_hidden_states, L0_attn_Wq, L0_attn_bq, L0_attn_Wk, L0_attn_bk,
           L0_attn_Wv, L0_attn_bv, L0_ffn_w1, L0_ffn_b1, L0_ffn_w2, L0_ffn_b2):
    x = np.ascontiguousarray(np.asarray(L0_attn_hidden_states, dtype=np.float32))
    tt = rope_table_T()
    common = {
        "wq": np.ascontiguousarray(np.asarray(L0_attn_Wq, np.float32)),
        "wk": np.ascontiguousarray(np.asarray(L0_attn_Wk, np.float32)),
        "wv": np.ascontiguousarray(np.asarray(L0_attn_Wv, np.float32)),
        "bq": np.asarray(L0_attn_bq, np.float32).reshape(1, D),
        "bk": np.asarray(L0_attn_bk, np.float32).reshape(1, D),
        "bv": np.asarray(L0_attn_bv, np.float32).reshape(1, D),
        "w1": np.ascontiguousarray(np.asarray(L0_ffn_w1, np.float32)),
        "b1": np.asarray(L0_ffn_b1, np.float32).reshape(1, HID),
        "w2": np.ascontiguousarray(np.asarray(L0_ffn_w2, np.float32)),
        "b2": np.asarray(L0_ffn_b2, np.float32).reshape(1, D),
        "tt": tt,
    }
    in_maps = []
    for c in range(8):
        m = dict(common)
        m["x"] = np.ascontiguousarray(
            x[c * NSEQ:(c + 1) * NSEQ].reshape(NSEQ * S_LEN, D))
        in_maps.append(m)
    nc = build_core_program()
    res = run_bass_kernel_spmd(nc, in_maps, core_ids=list(range(8)))
    if DEBUG:
        global _dbg
        _dbg = {k: np.asarray(res.results[0]["dbg_" + k]) for k in _dbg_names}
        global _dbg2
        _dbg2 = {k: np.asarray(res.results[1]["dbg_" + k]) for k in _dbg_names}
    out = np.concatenate(
        [np.asarray(r["out"]).reshape(NSEQ, S_LEN, D) for r in res.results], axis=0)
    return np.ascontiguousarray(out.astype(np.float32))
